# version 02
# baseline (speedup 1.0000x reference)
import numpy as np
from contextlib import ExitStack

import concourse.bass as bass
import concourse.mybir as mybir

F32 = mybir.dt.float32
BF16 = mybir.dt.bfloat16
I32 = mybir.dt.int32
AF = mybir.ActivationFunctionType
ALU = mybir.AluOpType
AX = mybir.AxisListType

_QUEUES = {
    "pe": ("tensor", 1),
    "act": ("scalar", 1),
    "dve": ("vector", 1),
    "pool": ("gpsimd", 1),
    "sp": ("sync", 1),
    "qsp": ("sync", 16),
    "qact": ("scalar", 16),
    "qpool": ("gpsimd", 16),
}
_ENGINE_OF = {
    "pe": "tensor", "act": "scalar", "dve": "vector", "pool": "gpsimd",
    "sp": "sync", "qsp": "sync", "qact": "scalar", "qpool": "gpsimd",
}
_ENGINES = ("tensor", "scalar", "vector", "gpsimd", "sync")


def _dt_size(dt):
    try:
        return dt.size_bytes()
    except Exception:
        pass
    return {F32: 4, BF16: 2, I32: 4}.get(dt, 4)


def _ap_box(ap):
    t = ap.tensor
    shape = list(t.shape)
    esz = _dt_size(ap.dtype)
    rowsz = 1
    for s in shape[1:]:
        rowsz *= s
    off = ap.offset
    span = 1
    for step, cnt in ap.ap:
        span += (cnt - 1) * abs(step)
    lo, hi = off, off + span
    if rowsz > 0 and len(shape) > 1:
        p_lo = lo // rowsz
        p_hi = (hi - 1) // rowsz + 1
        if p_hi - p_lo <= 1:
            b_lo = (lo % rowsz) * esz
            b_hi = ((hi - 1) % rowsz + 1) * esz
        else:
            b_lo, b_hi = 0, rowsz * esz
    else:
        p_lo, p_hi, b_lo, b_hi = 0, 1, lo * esz, hi * esz
    base = getattr(t, "base_partition", 0) or 0
    return (t.name, base + p_lo, base + p_hi, b_lo, b_hi)


def _overlap(a, b):
    return (a[0] == b[0] and a[1] < b[2] and b[1] < a[2]
            and a[3] < b[4] and b[3] < a[4])


class _Access:
    __slots__ = ("box", "q", "val")
    def __init__(self, box, q, val):
        self.box = box; self.q = q; self.val = val


class Prog:
    NDMA = 8

    def __init__(self, nc: bass.Bass, stack: ExitStack):
        self.nc = nc
        self.stack = stack
        self.sems = {}
        self.counts = {}
        self.ethunks = {e: [] for e in _ENGINES}
        self.writes = {}
        self.reads = {}
        self.pruned_w = {}
        self.pruned_r = {}
        self._flushed = False
        self.skeys = []
        for q in _QUEUES:
            if q.startswith("q"):
                for i in range(self.NDMA):
                    k = f"{q}{i}"
                    self.sems[k] = stack.enter_context(nc.semaphore(f"S_{k}"))
                    self.counts[k] = 0
                    self.skeys.append(k)
            else:
                self.sems[q] = stack.enter_context(nc.semaphore(f"S_{q}"))
                self.counts[q] = 0
                self.skeys.append(q)
        self.clocks = {k: {} for k in self.skeys}
        self._rr = {q: 0 for q in _QUEUES if q.startswith("q")}

    def sbuf(self, name, shape, dtype):
        return self.stack.enter_context(self.nc.sbuf_tensor(name, shape, dtype))

    def psum(self, name, shape, dtype=F32):
        return self.stack.enter_context(self.nc.psum_tensor(name, shape, dtype))

    def _need_waits(self, skey, rboxes, wboxes):
        need = {}
        def add(acc):
            if skey == "pe" and acc.q == "pe":
                return
            v = acc.val
            if acc.q[0] == "q":
                v = self.counts[acc.q]
            if v > need.get(acc.q, 0):
                need[acc.q] = v
        def add_pruned(d, name):
            for oq, v in d.get(name, {}).items():
                if not (skey == "pe" and oq == "pe"):
                    vv = self.counts[oq] if oq[0] == "q" else v
                    if vv > need.get(oq, 0):
                        need[oq] = vv
        for box in rboxes:
            for acc in self.writes.get(box[0], ()):
                if _overlap(box, acc.box):
                    add(acc)
            add_pruned(self.pruned_w, box[0])
        for box in wboxes:
            for acc in self.writes.get(box[0], ()):
                if _overlap(box, acc.box):
                    add(acc)
            for acc in self.reads.get(box[0], ()):
                if _overlap(box, acc.box):
                    add(acc)
            add_pruned(self.pruned_w, box[0])
            add_pruned(self.pruned_r, box[0])
        clk = self.clocks[skey]
        return {oq: v for oq, v in need.items() if v > clk.get(oq, 0)}

    def _update_clock(self, q, waited):
        clk = self.clocks[q]
        for oq, val in waited.items():
            if val > clk.get(oq, 0):
                clk[oq] = val
            if val == self.counts[oq]:
                for tq, tv in self.clocks[oq].items():
                    if tq != q and tv > clk.get(tq, 0):
                        clk[tq] = tv

    def _prune(self, lst, summary, name):
        if len(lst) > 24:
            drop = lst[0:len(lst) - 16]
            del lst[0:len(lst) - 16]
            s = summary.setdefault(name, {})
            for acc in drop:
                if acc.val > s.get(acc.q, 0):
                    s[acc.q] = acc.val

    def _record(self, q, val, rboxes, wboxes):
        for box in rboxes:
            lst = self.reads.setdefault(box[0], [])
            lst.append(_Access(box, q, val))
            self._prune(lst, self.pruned_r, box[0])
        for box in wboxes:
            lst = self.writes.setdefault(box[0], [])
            lst.append(_Access(box, q, val))
            self._prune(lst, self.pruned_w, box[0])

    def emit(self, q, fn, reads=(), writes=(), inc=True, sem_hint=None):
        assert not self._flushed
        if q.startswith("q"):
            if sem_hint is None:
                i = self._rr[q]; self._rr[q] = (i + 1) % self.NDMA
            else:
                i = hash(sem_hint) % self.NDMA
            skey = f"{q}{i}"
        else:
            skey = q
        rboxes = [_ap_box(a) for a in reads]
        wboxes = [_ap_box(a) for a in writes]
        waits = self._need_waits(skey, rboxes, wboxes)
        for oq, v in waits.items():
            if v > self.counts[oq]:
                raise RuntimeError(
                    f"future-wait: {skey} op would wait {oq}>={v} but only "
                    f"{self.counts[oq]} emitted (open accumulation group read/"
                    f"written by later consumer?) reads={[b[0] for b in rboxes]} "
                    f"writes={[b[0] for b in wboxes]}")
        if q.startswith("q") and self.counts[skey] > 0:
            if waits.get(skey, 0) < self.counts[skey] and                self.clocks[skey].get(skey, 0) < self.counts[skey]:
                waits[skey] = self.counts[skey]
        self._update_clock(skey, waits)
        sem_inc = _QUEUES[q][1]
        val = self.counts[skey] + sem_inc
        if inc:
            self.counts[skey] = val
        sems = self.sems
        def thunk(eng, waits=waits, fn=fn, skey=skey, inc=inc, sem_inc=sem_inc):
            for oq, v in waits.items():
                eng.wait_ge(sems[oq], v)
            ins = fn(eng)
            if inc:
                if ins is None:
                    raise RuntimeError("emit(inc=True) got no instruction")
                ins.then_inc(sems[skey], sem_inc)
        self.ethunks[_ENGINE_OF[q]].append(thunk)
        self._record(skey, val, rboxes, wboxes)
        return (skey, val)

    def barrier(self):
        snapshot = {k: v for k, v in self.counts.items() if v > 0}
        sems = self.sems
        for eng in _ENGINES:
            def thunk(e, snapshot=snapshot):
                for oq, v in snapshot.items():
                    e.wait_ge(sems[oq], v)
            self.ethunks[eng].append(thunk)
        for k in self.skeys:
            clk = self.clocks[k]
            for oq, v in snapshot.items():
                if v > clk.get(oq, 0):
                    clk[oq] = v

    def dma(self, q, out, in_, sem_hint=None, **kw):
        assert q in ("qsp", "qact", "qpool")
        return self.emit(q, lambda e: e.dma_start(out=out, in_=in_, **kw),
                         reads=[in_], writes=[out], sem_hint=sem_hint)

    def mm(self, out, lhsT, rhs, start=True, stop=True, inc=None, **kw):
        if inc is None:
            inc = True
        return self.emit("pe", lambda e: e.matmul(out, lhsT, rhs, start=start, stop=stop, **kw),
                         reads=[lhsT, rhs], writes=[out], inc=inc)

    def mm_acc(self, out, pairs, **kw):
        n = len(pairs)
        for i, (l, r) in enumerate(pairs):
            self.mm(out, l, r, start=(i == 0), stop=(i == n - 1), inc=(i == n - 1), **kw)

    def transpose(self, out, in_, identity):
        return self.emit("pe", lambda e: e.transpose(out, in_, identity),
                         reads=[in_, identity], writes=[out])

    def act(self, out, in_, func, bias=0.0, scale=1.0, accum_out=None):
        reads = [in_] + [a for a in (bias, scale) if hasattr(a, "tensor")]
        writes = [out] + ([accum_out] if accum_out is not None else [])
        return self.emit("act", lambda e: e.activation(out, in_, func, bias=bias, scale=scale,
                                                       accum_out=accum_out),
                         reads=reads, writes=writes)

    def _vecop(self, q, name, out, ins, kwargs=None):
        kwargs = kwargs or {}
        def fn(e, name=name, out=out, ins=ins, kwargs=kwargs):
            return getattr(e, name)(out, *ins, **kwargs)
        reads = [a for a in ins if hasattr(a, "tensor")]
        return self.emit(q, fn, reads=reads, writes=[out])

    def copy(self, out, in_, q="dve"):
        return self._vecop(q, "tensor_copy", out, [in_])

    def tt(self, out, a, b, op, q="dve"):
        return self._vecop(q, "tensor_tensor", out, [a, b], kwargs={"op": op})

    def ts(self, out, in_, s1, s2=None, op0=ALU.mult, op1=ALU.bypass, q="dve"):
        def fn(e):
            return e.tensor_scalar(out, in_, s1, s2, op0, op1)
        reads = [in_] + [s for s in (s1, s2) if hasattr(s, "tensor")]
        return self.emit(q, fn, reads=reads, writes=[out])

    def stt(self, out, in0, scalar, in1, op0, op1, q="dve"):
        def fn(e):
            return e.scalar_tensor_tensor(out, in0, scalar, in1, op0, op1)
        reads = [in0, in1] + ([scalar] if hasattr(scalar, "tensor") else [])
        return self.emit(q, fn, reads=reads, writes=[out])

    def reduce(self, out, in_, op, q="dve", negate=None):
        kw = {"axis": AX.X, "op": op}
        if negate is not None:
            kw["negate"] = negate
        return self._vecop(q, "tensor_reduce", out, [in_], kwargs=kw)

    def memset(self, ap, val, q="dve"):
        return self.emit(q, lambda e: e.memset(ap, val), reads=[], writes=[ap])

    def select(self, out, mask, on_true, on_false):
        def fn(e):
            return e.select(out, mask, on_true, on_false)
        reads = [a for a in (mask, on_true, on_false) if hasattr(a, "tensor")]
        return self.emit("dve", fn, reads=reads, writes=[out])

    def reciprocal(self, out, in_):
        return self._vecop("dve", "reciprocal", out, [in_])

    def scan(self, out, d0, d1, initial, op0, op1, q="dve"):
        def fn(e):
            return e.tensor_tensor_scan(out, d0, d1, initial, op0, op1)
        reads = [a for a in (d0, d1, initial) if hasattr(a, "tensor")]
        return self.emit(q, fn, reads=reads, writes=[out])

    def iota(self, out, pattern, base=None, channel_multiplier=None):
        kw = {}
        if base is not None:
            kw["base"] = base
        if channel_multiplier is not None:
            kw["channel_multiplier"] = channel_multiplier
        return self.emit("pool", lambda e: e.iota(out, pattern, **kw),
                         reads=[], writes=[out])

    def flush(self):
        assert not self._flushed
        self._flushed = True
        nc = self.nc
        counts = dict(self.counts)
        sems = self.sems
        with nc.Block() as block:
            for eng_name in _ENGINES:
                thunks = self.ethunks[eng_name]
                if eng_name == "sync":
                    def final_sync(eng, thunks=thunks, counts=counts):
                        for th in thunks:
                            th(eng)
                        for q, v in counts.items():
                            if v > 0:
                                eng.wait_ge(sems[q], v)
                    block.sync(final_sync)
                else:
                    if not thunks:
                        continue
                    def body(eng, thunks=thunks):
                        for th in thunks:
                            th(eng)
                    getattr(block, eng_name)(body)
        return self


"""Sublayer launch builders for the macaron megakernel (per-core, feature-major).

Layout conventions (per core):
  - Residual stream in HBM: xT [1024, T] fp32 (feature-major; host transposes).
  - SBUF activation tiles: [128 partitions (feature chunk), cols].
  - Weights fed from host already transposed to [contraction, out] where needed.
  - bf16 matmuls: lhsT [128, M<=128] base 0, rhs [128, N<=512], PSUM fp32.
  - fp32 matmuls: K<=64; halves accumulated in separate base-0 / base-64 PSUM
    groups, then combined on DVE.
"""
import numpy as np
import ml_dtypes
from contextlib import ExitStack

import concourse.bass as bass
import concourse.mybir as mybir

D = 1024
NCH = 8
EPS = 1e-5


def bf(x):
    return np.asarray(x, dtype=np.float32).astype(ml_dtypes.bfloat16)


class Common:
    def __init__(self, p: Prog, T, TT=512):
        self.p = p
        self.T = T
        self.TT = TT
        self.nt = T // TT
        nc = p.nc
        self.ones_col_f = p.sbuf("ones_col_f", [128, 1], F32)
        self.ones_row_f = p.sbuf("ones_row_f", [1, 128], F32)
        ones_col_d = nc.inline_tensor(np.ones((128, 1), np.float32), "ones_col_d")
        ones_row_d = nc.inline_tensor(np.ones((1, 128), np.float32), "ones_row_d")
        p.dma("qsp", self.ones_col_f[:], ones_col_d.ap())
        p.dma("qsp", self.ones_row_f[:], ones_row_d.ap())
        self.ps = [p.psum(f"psb{i}", [128, 512], F32) for i in range(7)]

    def mm_f32(self, out, pos_pairs, neg_pairs=(), n=512, accumulate=False):
        p = self.p
        psA, psB, psC, psD = self.ps[0], self.ps[1], self.ps[2], self.ps[3]
        for i, (l, r) in enumerate(pos_pairs):
            lp = _half(l, 0); rp = _half(r, 0)
            p.mm(psA[:, :n], lp, rp, start=(i == 0), stop=(i == len(pos_pairs) - 1))
        for i, (l, r) in enumerate(pos_pairs):
            p.mm(psB[:, :n], _half(l, 1), _half(r, 1), start=(i == 0),
                 stop=(i == len(pos_pairs) - 1))
        for i, (l, r) in enumerate(neg_pairs):
            p.mm(psC[:, :n], _half(l, 0), _half(r, 0), start=(i == 0),
                 stop=(i == len(neg_pairs) - 1))
        for i, (l, r) in enumerate(neg_pairs):
            p.mm(psD[:, :n], _half(l, 1), _half(r, 1), start=(i == 0),
                 stop=(i == len(neg_pairs) - 1))
        if accumulate:
            p.tt(out, out, psA[:out.ap[0][1], :n], ALU.add)
        else:
            p.copy(out, psA[:out.ap[0][1], :n])
        p.tt(out, out, psB[:out.ap[0][1], :n], ALU.add)
        if neg_pairs:
            p.tt(out, out, psC[:out.ap[0][1], :n], ALU.subtract)
            p.tt(out, out, psD[:out.ap[0][1], :n], ALU.subtract)

    def layernorm_tile(self, xs, t0, cols, xn_out, sq_ws, rows_ws, out_off=None,
                       f32_out=None):
        p = self.p
        o0 = t0 if out_off is None else out_off
        psA, psB, sqA, sqB, bc_r, bc_m = (self.ps[i] for i in range(6))
        for c in range(NCH):
            x_t = xs[c][:, t0:t0 + cols]
            p.mm(psA[:1, :cols], self.ones_col_f[0:64, :], xs[c][0:64, t0:t0 + cols],
                 start=(c == 0), stop=(c == NCH - 1))
            p.mm(psB[:1, :cols], self.ones_col_f[64:128, :], xs[c][64:128, t0:t0 + cols],
                 start=(c == 0), stop=(c == NCH - 1))
            p.act(sq_ws[c][:, :cols], x_t, AF.Square)
            p.mm(sqA[:1, :cols], self.ones_col_f[0:64, :], sq_ws[c][0:64, :cols],
                 start=(c == 0), stop=(c == NCH - 1))
            p.mm(sqB[:1, :cols], self.ones_col_f[64:128, :], sq_ws[c][64:128, :cols],
                 start=(c == 0), stop=(c == NCH - 1))
        mean = rows_ws[:, 0:cols]
        s2 = rows_ws[:, cols:2 * cols]
        var = rows_ws[:, 2 * cols:3 * cols]
        rstd = rows_ws[:, 3 * cols:4 * cols]
        mr = rows_ws[:, 4 * cols:5 * cols]
        tmpr = rows_ws[:, 5 * cols:6 * cols]
        p.copy(mean, psA[:1, :cols])
        p.tt(mean, mean, psB[:1, :cols], ALU.add)
        p.ts(mean, mean, 1.0 / D, None, ALU.mult)
        p.copy(s2, sqA[:1, :cols])
        p.tt(s2, s2, sqB[:1, :cols], ALU.add)
        p.tt(tmpr, mean, mean, ALU.mult)
        p.ts(s2, s2, 1.0 / D, None, ALU.mult)
        p.tt(var, s2, tmpr, ALU.subtract)
        p.ts(var, var, EPS, None, ALU.add)
        p.act(tmpr, var, AF.Sqrt)
        p.reciprocal(rstd, tmpr)
        p.tt(mr, mean, rstd, ALU.mult)
        p.mm(bc_r[:, :cols], self.ones_row_f[:1, :], rstd)
        p.mm(bc_m[:, :cols], self.ones_row_f[:1, :], mr)
        for c in range(NCH):
            x_t = xs[c][:, t0:t0 + cols]
            p.tt(sq_ws[c][:, :cols], x_t, bc_r[:, :cols], ALU.mult)
            if f32_out is None:
                p.tt(xn_out[c][:, o0:o0 + cols], sq_ws[c][:, :cols], bc_m[:, :cols],
                     ALU.subtract)
            else:
                p.tt(f32_out[c][:, :cols], sq_ws[c][:, :cols], bc_m[:, :cols],
                     ALU.subtract)
                p.copy(xn_out[c][:, o0:o0 + cols], f32_out[c][:, :cols])

    def load_w_bf(self, name, w_hbm, rows, cols, q="qpool"):
        p = self.p
        tiles = []
        for c in range(rows // 128):
            t = p.sbuf(f"{name}_{c}", [128, cols], BF16)
            p.dma(q, t[:], w_hbm[c * 128:(c + 1) * 128, :], sem_hint=name)
            tiles.append(t)
        return tiles

    def load_x(self, xT):
        p = self.p
        xs = [p.sbuf(f"xs{c}", [128, self.T], F32) for c in range(NCH)]
        for c in range(NCH):
            p.dma("qsp", xs[c][:], xT[c * 128:(c + 1) * 128, :], sem_hint="x")
        return xs

    def store_x(self, xs, yT):
        p = self.p
        for c in range(NCH):
            p.dma("qsp", yT[c * 128:(c + 1) * 128, :], xs[c][:], sem_hint="y")


def _half(ap, h):
    return ap[h * 64:(h + 1) * 64, :]



def build_l0ffn(T=4096, TT=512):
    nc = bass.Bass("TRN2", target_bir_lowering=False, debug=False)
    xT = nc.dram_tensor("xT", [D, T], F32, kind="ExternalInput").ap()
    wr = nc.dram_tensor("wr", [D, D], F32, kind="ExternalInput").ap()
    wi = nc.dram_tensor("wi", [D, D], F32, kind="ExternalInput").ap()
    biasT = nc.dram_tensor("biasT", [128, NCH], F32, kind="ExternalInput").ap()
    dftC = nc.dram_tensor("dftC", [D, D], F32, kind="ExternalInput").ap()
    dftS = nc.dram_tensor("dftS", [D, D], F32, kind="ExternalInput").ap()
    yT = nc.dram_tensor("yT", [D, T], F32, kind="ExternalOutput").ap()

    with ExitStack() as stack:
        p = Prog(nc, stack)
        cm = Common(p, T, TT)
        rem = [p.sbuf(f"rem{c}", [128, D], BF16) for c in range(NCH)]
        bias_s = p.sbuf("bias_s", [128, NCH], F32)
        p.dma("qsp", bias_s[:], biasT)
        p.ts(bias_s[:], bias_s[:], 0.5, None, ALU.mult)

        with ExitStack() as ph0:
            def sb(name, shape, dt=F32):
                return ph0.enter_context(nc.sbuf_tensor(name, shape, dt))
            wr_s = [sb(f"wr{c}", [128, D]) for c in range(NCH)]
            wi_s = [sb(f"wi{c}", [128, D]) for c in range(NCH)]
            C_s = [sb(f"C{c}", [128, D]) for c in range(NCH)]
            S_s = [sb(f"S{c}", [128, D]) for c in range(NCH)]
            Pr_s = [sb(f"Pr{c}", [128, 512]) for c in range(NCH)]
            Pi_s = [sb(f"Pi{c}", [128, 512]) for c in range(NCH)]
            ws_comb = sb("ws_comb", [128, 512])
            for c in range(NCH):
                p.dma("qsp", wr_s[c][:], wr[c * 128:(c + 1) * 128, :], sem_hint="wr")
                p.dma("qsp", wi_s[c][:], wi[c * 128:(c + 1) * 128, :], sem_hint="wi")
                p.dma("qact", C_s[c][:], dftC[c * 128:(c + 1) * 128, :], sem_hint="C")
                p.dma("qact", S_s[c][:], dftS[c * 128:(c + 1) * 128, :], sem_hint="S")
            for bt in range(D // 512):
                b0 = bt * 512
                for m in range(NCH):
                    mc = slice(m * 128, (m + 1) * 128)
                    cm.mm_f32(Pr_s[m][:, :],
                              [(wr_s[f][:, mc], C_s[f][:, b0:b0 + 512]) for f in range(NCH)]
                              + [(wi_s[f][:, mc], S_s[f][:, b0:b0 + 512]) for f in range(NCH)])
                    cm.mm_f32(Pi_s[m][:, :],
                              [(wi_s[f][:, mc], C_s[f][:, b0:b0 + 512]) for f in range(NCH)],
                              [(wr_s[f][:, mc], S_s[f][:, b0:b0 + 512]) for f in range(NCH)])
                for bc4 in range(4):
                    bcs = slice(bc4 * 128, (bc4 + 1) * 128)
                    bchunk = bt * 4 + bc4
                    for at in range(2):
                        a0 = at * 512
                        psA, psB, psC, psD = cm.ps[0], cm.ps[1], cm.ps[2], cm.ps[3]
                        for i in range(NCH):
                            p.mm(psA[:, :], _half(Pr_s[i][:, bcs], 0),
                                 _half(C_s[i][:, a0:a0 + 512], 0),
                                 start=(i == 0), stop=(i == NCH - 1))
                        for i in range(NCH):
                            p.mm(psB[:, :], _half(Pr_s[i][:, bcs], 1),
                                 _half(C_s[i][:, a0:a0 + 512], 1),
                                 start=(i == 0), stop=(i == NCH - 1))
                        for i in range(NCH):
                            p.mm(psC[:, :], _half(Pi_s[i][:, bcs], 0),
                                 _half(S_s[i][:, a0:a0 + 512], 0),
                                 start=(i == 0), stop=(i == NCH - 1))
                        for i in range(NCH):
                            p.mm(psD[:, :], _half(Pi_s[i][:, bcs], 1),
                                 _half(S_s[i][:, a0:a0 + 512], 1),
                                 start=(i == 0), stop=(i == NCH - 1))
                        ws = ws_comb
                        p.copy(ws[:, :], psA[:, :])
                        p.tt(ws[:, :], ws[:, :], psB[:, :], ALU.add)
                        p.tt(ws[:, :], ws[:, :], psC[:, :], ALU.subtract)
                        p.tt(ws[:, :], ws[:, :], psD[:, :], ALU.subtract)
                        p.ts(rem[bchunk][:, a0:a0 + 512], ws[:, :], 1.0 / D, None, ALU.mult)
            p.barrier()

        xs = cm.load_x(xT)
        xn = [p.sbuf(f"xn{c}", [128, TT], BF16) for c in range(NCH)]
        sq_ws = [p.sbuf(f"sq_ws{c}", [128, TT], F32) for c in range(NCH)]
        rows_ws = p.sbuf("rows_ws", [1, 6 * TT], F32)
        yw = p.sbuf("yw", [128, TT], F32)
        for t in range(T // TT):
            t0 = t * TT
            cm.layernorm_tile(xs, t0, TT, xn, sq_ws, rows_ws, out_off=0)
            for m in range(NCH):
                mc = slice(m * 128, (m + 1) * 128)
                ps = cm.ps[6]
                for c in range(NCH):
                    p.mm(ps[:, :], rem[c][:, mc], xn[c][:, :], start=(c == 0),
                         stop=(c == NCH - 1))
                p.ts(yw[:, :], ps[:, :], 0.5, bias_s[:, m:m + 1], ALU.mult, ALU.add)
                p.tt(xs[m][:, t0:t0 + TT], xs[m][:, t0:t0 + TT], yw[:, :], ALU.add)
        cm.store_x(xs, yT)
        p.flush()
    return nc



H = 16
DH = 64


def _ln_tile_v2(p, cm, xstage, xb, sq, TT, xn, ones_col_bf, rows_ws, from_bf=False,
                f32_out=None):
    psA, sqA, bc_r, bc_m = cm.ps[0], cm.ps[2], cm.ps[4], cm.ps[5]
    for c in range(NCH):
        if not from_bf:
            p.copy(xb[c][:, :TT], xstage[c][:, :TT])
        p.act(sq[c][:, :TT], xb[c][:, :TT] if from_bf else xstage[c][:, :TT],
              AF.Square)
        p.mm(psA[:1, :TT], ones_col_bf[:, 0:1], xb[c][:, :TT], start=(c == 0),
             stop=(c == NCH - 1))
        p.mm(sqA[:1, :TT], ones_col_bf[:, 0:1], sq[c][:, :TT], start=(c == 0),
             stop=(c == NCH - 1))
    mean = rows_ws[:, 0:TT]
    var = rows_ws[:, 2 * TT:3 * TT]
    rstd = rows_ws[:, 3 * TT:4 * TT]
    mr = rows_ws[:, 4 * TT:5 * TT]
    tmpr = rows_ws[:, 5 * TT:6 * TT]
    p.ts(mean, psA[:1, :TT], 1.0 / D, None, ALU.mult)
    p.ts(var, sqA[:1, :TT], 1.0 / D, None, ALU.mult)
    p.tt(tmpr, mean, mean, ALU.mult)
    p.tt(var, var, tmpr, ALU.subtract)
    p.ts(var, var, EPS, None, ALU.add)
    p.act(tmpr, var, AF.Sqrt)
    p.reciprocal(rstd, tmpr)
    p.tt(mr, mean, rstd, ALU.mult)
    p.mm(bc_r[:, :TT], cm.ones_row_f[:1, :], rstd)
    p.mm(bc_m[:, :TT], cm.ones_row_f[:1, :], mr)
    for c in range(NCH):
        srcc = xb[c] if from_bf else xstage[c]
        p.tt(sq[c][:, :TT], srcc[:, :TT], bc_r[:, :TT], ALU.mult)
        if f32_out is None:
            p.tt(xn[c][:, :TT], sq[c][:, :TT], bc_m[:, :TT], ALU.subtract)
        else:
            p.tt(f32_out[c][:, :TT], sq[c][:, :TT], bc_m[:, :TT], ALU.subtract)
            p.copy(xn[c][:, :TT], f32_out[c][:, :TT])


def build_attn(kind, T=4096, TT=256):
    assert kind in ("l0", "l1", "l3")
    nc = bass.Bass("TRN2", target_bir_lowering=False, debug=False)
    xT = nc.dram_tensor("xT", [D, T], F32, kind="ExternalInput").ap()
    yT = nc.dram_tensor("yT", [D, T], F32, kind="ExternalOutput").ap()
    ins = {}
    if kind == "l1":
        for nm in ("WqT", "WkT", "WvT"):
            ins[nm] = nc.dram_tensor(nm, [D, D], F32, kind="ExternalInput").ap()
        ins["WpH"] = nc.dram_tensor("WpH", [H, DH, D], F32, kind="ExternalInput").ap()
        for nm in ("bqT", "bkT", "bvT", "bpT"):
            ins[nm] = nc.dram_tensor(nm, [128, NCH], F32, kind="ExternalInput").ap()
    elif kind == "l3":
        for nm in ("Wq", "Wk", "Wv"):
            ins[nm] = nc.dram_tensor(nm, [D, D], F32, kind="ExternalInput").ap()
        ins["WoH"] = nc.dram_tensor("WoH", [H, DH, D], F32, kind="ExternalInput").ap()
        ins["boT"] = nc.dram_tensor("boT", [128, NCH], F32, kind="ExternalInput").ap()
        ins["ropeC"] = nc.dram_tensor("ropeC", [128, T], BF16, kind="ExternalInput").ap()
        ins["ropeS"] = nc.dram_tensor("ropeS", [128, T], BF16, kind="ExternalInput").ap()
        ins["rperm"] = nc.dram_tensor("rperm", [128, 128], F32, kind="ExternalInput").ap()
        ins["cmask"] = nc.dram_tensor("cmask", [128, 4 * 512], F32, kind="ExternalInput").ap()
    else:
        for nm in ("qW", "kW", "vW", "oW"):
            ins[nm] = nc.dram_tensor(nm, [4, 256, 256], F32, kind="ExternalInput").ap()
        for nm in ("qbE", "kbE", "vbE", "obE"):
            ins[nm] = nc.dram_tensor(nm, [128, 8], F32, kind="ExternalInput").ap()
        ins["J"] = nc.dram_tensor("J", [1, D], F32, kind="ExternalInput").ap()

    qsp = nc.dram_tensor("qsp", [H, DH, T], BF16, kind="Internal").ap()
    ksp = nc.dram_tensor("ksp", [H, DH, T], BF16, kind="Internal").ap()

    nt = T // TT
    scale = DH ** -0.5
    with ExitStack() as stack:
        p = Prog(nc, stack)
        cm = Common(p, T, TT)
        ones_col_bf = p.sbuf("ones_col_bf", [128, 1], BF16)
        ones_sq_bf = p.sbuf("ones_sq_bf", [1, 128], BF16)
        ident_bf = p.sbuf("ident_bf", [128, 128], BF16)
        idf_d = nc.inline_tensor(np.eye(128, dtype=np.float32).astype(ml_dtypes.bfloat16), "idf_d")
        onesb_d = nc.inline_tensor(np.ones((1, 128), np.float32).astype(ml_dtypes.bfloat16), "onesb_d")
        onesc_d = nc.inline_tensor(np.ones((128, 1), np.float32).astype(ml_dtypes.bfloat16), "onesc_d")
        p.dma("qsp", ident_bf[:], idf_d.ap())
        p.dma("qsp", ones_sq_bf[:], onesb_d.ap())
        p.dma("qsp", ones_col_bf[:], onesc_d.ap())

        ve = [p.sbuf(f"ve{h}", [128, T // 128, 65], BF16) for h in range(H)]
        for h in range(H):
            p.memset(ve[h][:, :, 64:65], 1.0)

        bias_cols = {}
        alpha_cols = {}
        if kind == "l1":
            for nm in ("Wq", "Wk", "Wv", "Wp"):
                alpha_cols[nm] = p.sbuf(f"{nm}_acol", [128, NCH], F32)
            for nm in ("bq", "bk", "bv", "bp"):
                bias_cols[nm] = p.sbuf(f"{nm}_col", [128, NCH], F32)
        elif kind == "l3":
            bias_cols["bo"] = p.sbuf("bo_col", [128, NCH], F32)
        else:
            for nm in ("bq", "bk", "bv", "bo"):
                bias_cols[nm] = p.sbuf(f"{nm}_eff", [128, NCH], F32)
        if kind == "l0":
            cj = p.sbuf("cj", [1, 2], F32)


        with ExitStack() as phW:
            def sbW(name, shape, dt=F32):
                return phW.enter_context(nc.sbuf_tensor(name, shape, dt))

            if kind == "l1":
                stage = sbW("stageW", [128, D])
                absb = sbW("absW", [128, D])
                arow = sbW("arow", [1, D])
                stage64 = sbW("stage64", [64, D])
                Ws = {}
                psA, psB, pst = cm.ps[0], cm.ps[1], cm.ps[2]
                for nm, hb in (("Wq", "WqT"), ("Wk", "WkT"), ("Wv", "WvT")):
                    tiles = [sbW(f"{nm}s{c}", [128, D], BF16) for c in range(NCH)]
                    Ws[nm] = tiles
                    for c in range(NCH):
                        p.dma("qsp", stage[:], ins[hb][c * 128:(c + 1) * 128, :],
                              sem_hint="wstage")
                        p.act(tiles[c][:], stage[:], AF.Sign)
                        p.act(absb[:], stage[:], AF.Abs)
                        for hv in range(2):
                            sl = slice(hv * 512, hv * 512 + 512)
                            p.mm(psA[:1, sl] if hv == 0 else psB[:1, :512],
                                 cm.ones_col_f[0:64, :], absb[0:64, sl],
                                 start=(c == 0), stop=(c == NCH - 1))
                            p.mm(pst[:1, :512] if hv == 0 else cm.ps[3][:1, :512],
                                 cm.ones_col_f[64:128, :], absb[64:128, sl],
                                 start=(c == 0), stop=(c == NCH - 1))
                    p.copy(arow[:, 0:512], psA[:1, 0:512])
                    p.tt(arow[:, 0:512], arow[:, 0:512], pst[:1, :512], ALU.add)
                    p.copy(arow[:, 512:1024], psB[:1, :512])
                    p.tt(arow[:, 512:1024], arow[:, 512:1024], cm.ps[3][:1, :512], ALU.add)
                    p.ts(arow[:], arow[:], 1.0 / D, None, ALU.mult)
                    acol = alpha_cols[nm]
                    for c in range(NCH):
                        p.transpose(pst[:, 0:1], arow[:1, c * 128:(c + 1) * 128],
                                    cm.ones_row_f[:1, 0:1])
                        p.copy(acol[:, c:c + 1], pst[:, 0:1])
                for nm, hb in (("bq", "bqT"), ("bk", "bkT"), ("bv", "bvT"), ("bp", "bpT")):
                    p.dma("qsp", bias_cols[nm][:], ins[hb])
            elif kind == "l3":
                def ld(nm):
                    tiles = [sbW(f"{nm}s{c}", [128, D], BF16) for c in range(NCH)]
                    for c in range(NCH):
                        p.dma("qpool", tiles[c][:], ins[nm][c * 128:(c + 1) * 128, :],
                              sem_hint=nm)
                    return tiles
                Ws = {"Wq": ld("Wq"), "Wk": ld("Wk"), "Wv": ld("Wv")}
                p.dma("qsp", bias_cols["bo"][:], ins["boT"])
                ropeC = sbW("ropeC_s", [128, T], BF16)
                ropeS = sbW("ropeS_s", [128, T], BF16)
                p.dma("qact", ropeC[:], ins["ropeC"], sem_hint="rc")
                p.dma("qact", ropeS[:], ins["ropeS"], sem_hint="rs")
                rperm = sbW("rperm_s", [128, 128], BF16)
                rp_st = sbW("rp_st", [128, 128])
                p.dma("qsp", rp_st[:], ins["rperm"])
                p.copy(rperm[:], rp_st[:])
            else:
                QSIGN = {
                    0: [(0, 1), (1, -1), (2, -1), (3, -1)],
                    1: [(1, 1), (0, 1), (3, -1), (2, 1)],
                    2: [(2, 1), (3, 1), (0, 1), (1, -1)],
                    3: [(3, 1), (2, -1), (1, 1), (0, 1)],
                }
                BSIGN = {0: [1, -1, -1, -1], 1: [1, 1, 1, -1],
                         2: [1, -1, 1, 1], 3: [1, 1, -1, 1]}
                stage = sbW("stageQ", [128, 256])
                stage64 = sbW("stageQ64", [64, 256])
                Ws = {}
                for nm, hb in (("Wq", "qW"), ("Wk", "kW"), ("Wv", "vW")):
                    tiles = [sbW(f"{nm}s{ch}", [128, D], BF16) for ch in range(NCH)]
                    Ws[nm] = tiles
                    for cin in range(4):
                        for hf in range(2):
                            ch = 2 * cin + hf
                            for comp in range(4):
                                w, sg = QSIGN[comp][cin]
                                p.dma("qsp", stage[:],
                                      ins[hb][w, 128 * hf:128 * hf + 128, :],
                                      sem_hint="qw")
                                p.ts(tiles[ch][:, 256 * comp:256 * comp + 256],
                                     stage[:], float(sg), None, ALU.mult)
                for nm, hb in (("bq", "qbE"), ("bk", "kbE"), ("bv", "vbE"), ("bo", "obE")):
                    raw = sbW(f"{nm}_raw", [128, 8])
                    p.dma("qsp", raw[:], ins[hb])
                    eff = bias_cols[nm]
                    for comp in range(4):
                        for hf in range(2):
                            ch = comp * 2 + hf
                            s = BSIGN[comp]
                            p.ts(eff[:, ch:ch + 1], raw[:, hf:hf + 1],
                                 float(s[0]), None, ALU.mult)
                            for w in range(1, 4):
                                p.stt(eff[:, ch:ch + 1], raw[:, w * 2 + hf:w * 2 + hf + 1],
                                      float(s[w]), eff[:, ch:ch + 1], ALU.mult, ALU.add)
                    bias_cols[nm] = eff
                jrow = sbW("jrow", [1, D])
                jsin = sbW("jsin", [1, D])
                p.dma("qsp", jrow[:], ins["J"])
                p.act(jsin[:], jrow[:], AF.Sin)
                p.reduce(cj[:, 0:1], jsin[:], ALU.add)
                p.ts(cj[:, 0:1], cj[:, 0:1], 0.1 / D, None, ALU.mult)

            from_bf = (kind == "l0")
            if not from_bf:
                xstage = [sbW(f"xa{c}", [128, TT]) for c in range(NCH)]
            else:
                xstage = [sbW("xa_sh", [128, TT])]* NCH
            xbA = [sbW(f"xab{c}", [128, TT], BF16) for c in range(NCH)]
            sqw = [sbW(f"sqa{c}", [128, TT], BF16) for c in range(NCH)]
            xn = [sbW(f"xna{c}", [128, TT], BF16) for c in range(NCH)]
            rows_ws = sbW("rows_a", [1, 6 * TT])
            if kind == "l0":
                qf = [sbW(f"qf{c}", [128, TT]) for c in range(NCH)]
                kf = [sbW(f"kf{c}", [128, TT]) for c in range(NCH)]
                n2 = [sbW(f"n2_{i}", [128, TT]) for i in range(2)]
                rn = [sbW(f"rn_{i}", [128, TT]) for i in range(2)]
                rnq = [sbW(f"rnq_{i}", [128, TT]) for i in range(2)]
            else:
                qf = kf = None
            if kind == "l3":
                tr1 = sbW("tr1", [128, TT])
                tr2 = sbW("tr2", [128, TT])
            qb_t = [sbW(f"qbt{c}", [128, TT], BF16) for c in range(NCH)]
            kb_t = [sbW(f"kbt{c}", [128, TT], BF16) for c in range(NCH)]
            vb_t = [sbW(f"vbt{c}", [128, TT], BF16) for c in range(NCH)]
            ps_tr = p.psum("pstr_a", [128, 512], BF16)

            for t in range(nt):
                t0 = t * TT
                if from_bf:
                    for c in range(NCH):
                        p.dma("qpool", xbA[c][:], xT[c * 128:(c + 1) * 128, t0:t0 + TT],
                              sem_hint=f"xa{c % 4}")
                else:
                    for c in range(NCH):
                        p.dma("qsp", xstage[c][:], xT[c * 128:(c + 1) * 128, t0:t0 + TT],
                              sem_hint=f"xa{c % 4}")
                _ln_tile_v2(p, cm, xstage, xbA, sqw, TT, xn, ones_col_bf, rows_ws,
                            from_bf=from_bf)
                for (dst_f, dst_b, wname, bname) in (
                        (qf, qb_t, "Wq", "bq"), (kf, kb_t, "Wk", "bk"),
                        (None, vb_t, "Wv", "bv")):
                    W = Ws[wname]
                    for m in range(NCH):
                        ps = cm.ps[6]
                        for c in range(NCH):
                            p.mm(ps[:, :TT], W[c][:, m * 128:(m + 1) * 128],
                                 xn[c][:, :TT], start=(c == 0), stop=(c == NCH - 1))
                        if kind == "l1":
                            p.ts(dst_b[m][:, :TT], ps[:, :TT],
                                 alpha_cols[wname][:, m:m + 1],
                                 bias_cols[bname][:, m:m + 1], ALU.mult, ALU.add)
                        elif kind == "l3":
                            sc0 = scale if wname == "Wq" else 1.0
                            p.ts(dst_b[m][:, :TT], ps[:, :TT], sc0, None, ALU.mult)
                        else:
                            tgt = dst_b[m] if dst_f is None else dst_f[m]
                            p.ts(tgt[:, :TT], ps[:, :TT], 1.0,
                                 bias_cols[bname][:, m:m + 1], ALU.mult, ALU.add)
                if kind == "l0":
                    for (src, dst, do_scale) in ((qf, qb_t, True), (kf, kb_t, False)):
                        for i in range(2):
                            p.act(n2[i][:, :TT], src[i][:, :TT], AF.Square)
                            for cc in (i + 2, i + 4, i + 6):
                                p.act(rn[i][:, :TT], src[cc][:, :TT], AF.Square)
                                p.tt(n2[i][:, :TT], n2[i][:, :TT], rn[i][:, :TT], ALU.add)
                            p.ts(n2[i][:, :TT], n2[i][:, :TT], 1e-6, None, ALU.add)
                            p.act(rn[i][:, :TT], n2[i][:, :TT], AF.Sqrt)
                            p.reciprocal(rn[i][:, :TT], rn[i][:, :TT])
                            if do_scale:
                                p.ts(rnq[i][:, :TT], rn[i][:, :TT], scale, None, ALU.mult)
                        rr = rnq if do_scale else rn
                        for c in range(NCH):
                            p.tt(dst[c][:, :TT], src[c][:, :TT], rr[c % 2][:, :TT],
                                 ALU.mult)
                elif kind == "l3":
                    for dst in (qb_t, kb_t):
                        for c in range(NCH):
                            psr = cm.ps[6]
                            p.mm(psr[:, :TT], rperm[:, :], dst[c][:, :TT])
                            p.tt(tr1[:, :TT], dst[c][:, :TT], ropeC[:, t0:t0 + TT],
                                 ALU.mult)
                            p.tt(tr2[:, :TT], psr[:, :TT], ropeS[:, t0:t0 + TT],
                                 ALU.mult)
                            p.tt(dst[c][:, :TT], tr1[:, :TT], tr2[:, :TT], ALU.add)
                else:
                    for c in range(NCH):
                        p.ts(qb_t[c][:, :TT], qb_t[c][:, :TT], scale, None, ALU.mult)
                for h in range(H):
                    ch, b = divmod(h, 2)
                    p.dma("qact", qsp[h, :, t0:t0 + TT],
                          qb_t[ch][b * 64:(b + 1) * 64, :TT], sem_hint=f"qs{h % 4}")
                    p.dma("qpool", ksp[h, :, t0:t0 + TT],
                          kb_t[ch][b * 64:(b + 1) * 64, :TT], sem_hint=f"ks{h % 4}")
                for c in range(NCH):
                    for j in range(TT // 128):
                        p.transpose(ps_tr[:, 0:128], vb_t[c][:, j * 128:(j + 1) * 128],
                                    ident_bf[:])
                        jg = (t0 + j * 128) // 128
                        p.copy(ve[2 * c][:, jg, 0:64], ps_tr[:, 0:64])
                        p.copy(ve[2 * c + 1][:, jg, 0:64], ps_tr[:, 64:128])
            p.barrier()

        WoH_t = [p.sbuf(f"WoH{h}", [64, D], BF16) for h in range(H)]
        if kind == "l3":
            cmask = p.sbuf("cmask_s", [128, 4 * 512], F32)
            p.dma("qact", cmask[:], ins["cmask"], sem_hint="cms")
            for h in range(H):
                p.dma("qpool", WoH_t[h][:], ins["WoH"][h], sem_hint="woh")
        elif kind == "l1":
            with ExitStack() as ph2:
                stage64b = ph2.enter_context(nc.sbuf_tensor("stage64b", [64, D], F32))
                absb2 = ph2.enter_context(nc.sbuf_tensor("absb2", [64, D], F32))
                arow2 = ph2.enter_context(nc.sbuf_tensor("arow2", [1, D], F32))
                psA2, psB2, pst2 = cm.ps[0], cm.ps[1], cm.ps[2]
                for h in range(H):
                    p.dma("qsp", stage64b[:], ins["WpH"][h], sem_hint="wp64")
                    p.act(WoH_t[h][:], stage64b[:], AF.Sign)
                    p.act(absb2[:, :], stage64b[:], AF.Abs)
                    for hv in range(2):
                        sl = slice(hv * 512, hv * 512 + 512)
                        p.mm(psA2[:1, :512] if hv == 0 else psB2[:1, :512],
                             cm.ones_col_f[0:64, :], absb2[:, sl],
                             start=(h == 0), stop=(h == H - 1))
                p.ts(arow2[:, 0:512], psA2[:1, :512], 1.0 / D, None, ALU.mult)
                p.ts(arow2[:, 512:1024], psB2[:1, :512], 1.0 / D, None, ALU.mult)
                acol = alpha_cols["Wp"]
                for c in range(NCH):
                    p.transpose(pst2[:, 0:1], arow2[:1, c * 128:(c + 1) * 128],
                                cm.ones_row_f[:1, 0:1])
                    p.copy(acol[:, c:c + 1], pst2[:, 0:1])
                p.barrier()
        else:
            QSIGN0 = {
                0: [(0, 1), (1, -1), (2, -1), (3, -1)],
                1: [(1, 1), (0, 1), (3, -1), (2, 1)],
                2: [(2, 1), (3, 1), (0, 1), (1, -1)],
                3: [(3, 1), (2, -1), (1, 1), (0, 1)],
            }
            with ExitStack() as ph2:
                stage64b = ph2.enter_context(nc.sbuf_tensor("stage64b", [64, 256], F32))
                for h in range(H):
                    cin, rest = divmod(h * 64, 256)
                    for comp in range(4):
                        w, sg = QSIGN0[comp][cin]
                        p.dma("qsp", stage64b[:], ins["oW"][w, rest:rest + 64, :],
                              sem_hint="qwo")
                        p.ts(WoH_t[h][:, 256 * comp:256 * comp + 256], stage64b[:],
                             float(sg), None, ALU.mult)
                p.barrier()

        ws_sq = p.sbuf("ws_sq", [64, 512], F32)
        maxk = p.sbuf("maxk", [1, H], F32)
        mrow = p.sbuf("mrow", [1, 64], F32)
        kbuf = [p.sbuf(f"kbuf{i}", [64, T], BF16) for i in range(2)]
        qbuf = [p.sbuf(f"qbuf{i}", [64, 512], BF16) for i in range(2)]
        qbias = p.sbuf("qbias", [1, 512], BF16)
        brow = p.sbuf("brow", [1, 512], F32)
        Pt = [p.sbuf(f"Pt{i}", [128, 512], BF16) for i in range(2)]
        Ot = [p.sbuf(f"Ot{h}", [64, 512], BF16) for h in range(H)]
        dbc = p.sbuf("dbc", [64, 512], F32)
        drow = p.sbuf("drow", [1, 512], F32)
        xe = p.sbuf("xe", [128, 512], F32)
        yw2 = p.sbuf("yw2", [128, 512], F32)
        psS = [cm.ps[0], cm.ps[1]]
        psO = cm.ps[2]
        psq2 = cm.ps[3]
        psb = cm.ps[4]
        psy = cm.ps[6]

        for h in range(H):
            p.dma("qsp", kbuf[h % 2][:, :], ksp[h], sem_hint=f"kb{h % 2}")
            for tt_ in range(T // 512):
                c0 = tt_ * 512
                p.act(ws_sq[:, :], kbuf[h % 2][:, c0:c0 + 512], AF.Square)
                p.mm(psq2[:1, :512], cm.ones_col_f[0:64, :], ws_sq[:, :],
                     start=True, stop=True)
                p.reduce(mrow[:, tt_:tt_ + 1], psq2[:1, :512], ALU.max)
            p.reduce(maxk[:, h:h + 1], mrow[:, 0:T // 512], ALU.max)
            p.act(maxk[:, h:h + 1], maxk[:, h:h + 1], AF.Sqrt)
            p.ts(maxk[:, h:h + 1], maxk[:, h:h + 1], -1.0, None, ALU.mult)

        nqt = T // 512
        for qt in range(nqt):
            q0 = qt * 512
            for h in range(H):
                p.dma("qsp", kbuf[h % 2][:, :], ksp[h], sem_hint=f"kb{h % 2}")
                p.dma("qact", qbuf[h % 2][:, :], qsp[h, :, q0:q0 + 512],
                      sem_hint=f"qb{h % 2}")
                p.act(ws_sq[:, :], qbuf[h % 2][:, :], AF.Square)
                p.mm(psq2[:1, :512], cm.ones_col_f[0:64, :], ws_sq[:, :],
                     start=True, stop=True)
                p.act(brow[:, :], psq2[:1, :512], AF.Sqrt)
                if kind == "l0":
                    p.ts(qbias[:, :], brow[:, :], maxk[:, h:h + 1], cj[:, 0:1],
                         ALU.mult, ALU.add)
                else:
                    p.ts(qbias[:, :], brow[:, :], maxk[:, h:h + 1], None, ALU.mult)
                kj_hi = (q0 + 512) // 128 if kind == "l3" else T // 128
                for j in range(kj_hi):
                    jp = j % 2
                    p.mm(psS[jp][:, :], kbuf[h % 2][:, j * 128:(j + 1) * 128],
                         qbuf[h % 2][:, :], start=True, stop=False, inc=True)
                    p.mm(psS[jp][:, :], ones_sq_bf[:1, :], qbias[:1, :],
                         start=False, stop=True)
                    if kind == "l3" and j * 128 >= q0:
                        dj = (j * 128 - q0) // 128
                        p.tt(psS[jp][:, :], psS[jp][:, :],
                             cmask[:, dj * 512:(dj + 1) * 512], ALU.add)
                    p.act(Pt[jp][:, :], psS[jp][:, :], AF.Exp)
                    p.mm(psO[0:65, :], ve[h][:, j, :], Pt[jp][:, :],
                         start=(j == 0), stop=(j == kj_hi - 1))
                p.reciprocal(drow[:, :], psO[64:65, :])
                p.mm(psb[:, :], cm.ones_row_f[:1, :], drow[:, :])
                p.copy(dbc[:, :], psb[0:64, :])
                p.tt(Ot[h][:, :], psO[0:64, :], dbc[:, :], ALU.mult)
            for m in range(NCH):
                for h in range(H):
                    p.mm(psy[:, :], WoH_t[h][:, m * 128:(m + 1) * 128], Ot[h][:, :],
                         start=(h == 0), stop=(h == H - 1))
                p.dma("qsp", xe[:, :], xT[m * 128:(m + 1) * 128, q0:q0 + 512],
                      sem_hint="xe")
                if kind == "l1":
                    p.ts(yw2[:, :], psy[:, :], alpha_cols["Wp"][:, m:m + 1],
                         bias_cols["bp"][:, m:m + 1], ALU.mult, ALU.add)
                else:
                    p.ts(yw2[:, :], psy[:, :], 1.0, bias_cols["bo"][:, m:m + 1],
                         ALU.mult, ALU.add)
                p.tt(yw2[:, :], yw2[:, :], xe[:, :], ALU.add)
                p.dma("qact", yT[m * 128:(m + 1) * 128, q0:q0 + 512], yw2[:, :],
                      sem_hint="yo")
        p.flush()
    return nc



MOE_CAP = {"l2": 4096, "l3": 5120}


def build_moe(kind, mode, T=4096, TT=512):
    assert kind in ("l1", "l2", "l3")
    assert mode in ("cnt", "main")
    E = 8
    nc = bass.Bass("TRN2", target_bir_lowering=False, debug=False)
    xT = nc.dram_tensor("xT", [D, T], F32, kind="ExternalInput").ap()
    rtr = nc.dram_tensor("router", [D, E], F32, kind="ExternalInput").ap()
    ins = {}
    if mode == "main":
        yT = nc.dram_tensor("yT", [D, T], F32, kind="ExternalOutput").ap()
        NH = 32 if kind != "l1" else 16
        w1d = nc.dram_tensor("w1e", [E, D, 4096], F32, kind="ExternalInput").ap()
        w2d = nc.dram_tensor("w2e", [E, NH * 128, D], F32, kind="ExternalInput").ap()
        if kind != "l1":
            b1d = nc.dram_tensor("b1T", [E, 128, 32], F32, kind="ExternalInput").ap()
            b2d = nc.dram_tensor("b2T", [E, 128, 8], F32, kind="ExternalInput").ap()
            ncnt = E if kind == "l2" else 2 * E
            cntsd = nc.dram_tensor("cnts", [8, ncnt], F32, kind="ExternalInput").ap()
            cmaskd = nc.dram_tensor("coremask", [8, 1], F32, kind="ExternalInput").ap()
        xnsp = nc.dram_tensor("xnsp", [D, T], BF16, kind="Internal").ap()
        accT = nc.dram_tensor("accT", [D, T], F32, kind="Internal").ap()
        w8sp = nc.dram_tensor("w8sp", [E, T], F32, kind="Internal").ap()
    else:
        ncnt = E if kind == "l2" else 2 * E
        cnt_out = nc.dram_tensor("cnt", [ncnt, 1], F32, kind="ExternalOutput").ap()

    nt = T // TT
    with ExitStack() as stack:
        p = Prog(nc, stack)
        cm = Common(p, T, TT)
        ones_col_bf = p.sbuf("ones_col_bf", [128, 1], BF16)
        onesc_d = nc.inline_tensor(np.ones((128, 1), np.float32).astype(ml_dtypes.bfloat16), "onesc_d")
        p.dma("qsp", ones_col_bf[:], onesc_d.ap())
        id2_d = nc.inline_tensor(np.concatenate([np.eye(64, dtype=np.float32)] * 2, 0), "id2_d")
        id2 = p.sbuf("id2", [128, 64], F32)
        p.dma("qsp", id2[:], id2_d.ap())

        rtr_s = [p.sbuf(f"rtr{c}", [128, E], F32) for c in range(NCH)]
        for c in range(NCH):
            p.dma("qsp", rtr_s[c][:], rtr[c * 128:(c + 1) * 128, :], sem_hint="rtr")

        w8 = p.sbuf("w8", [E, T], F32)

        with ExitStack() as phR:
            def sbR(name, shape, dt=F32):
                return phR.enter_context(nc.sbuf_tensor(name, shape, dt))
            xstage = [sbR(f"xr{c}", [128, TT]) for c in range(NCH)]
            xnf = [sbR(f"xnf{c}", [128, TT]) for c in range(NCH)]
            sqw32 = [sbR(f"sqr32{c}", [128, TT]) for c in range(NCH)]
            xn = [sbR(f"xnr{c}", [128, TT], BF16) for c in range(NCH)]
            rows_ws = sbR("rows_r", [1, 6 * TT])
            lg = sbR("lg", [128, E])
            col = sbR("colws", [128, 12])
            tm = sbR("tm", [128, 24])
            gA = sbR("gA", [E, T])
            gB = sbR("gB", [E, T]) if kind == "l3" else None
            gAg = sbR("gAg", [E, T]) if kind != "l1" else None
            gBg = sbR("gBg", [E, T]) if kind == "l3" else None
            pst = cm.ps[6]

            for t in range(nt):
                t0 = t * TT
                for c in range(NCH):
                    p.dma("qsp", xstage[c][:], xT[c * 128:(c + 1) * 128, t0:t0 + TT],
                          sem_hint=f"xr{c % 4}")
                cm.layernorm_tile(xstage, 0, TT, xn, sqw32, rows_ws, out_off=0,
                                  f32_out=xnf)
                if mode == "main":
                    for c in range(NCH):
                        p.dma("qact", xnsp[c * 128:(c + 1) * 128, t0:t0 + TT],
                              xn[c][:, :TT], sem_hint=f"xs{c % 4}")
                for ts_ in range(TT // 128):
                    tc = slice(ts_ * 128, (ts_ + 1) * 128)
                    psr, psr2 = cm.ps[0], cm.ps[1]
                    for c in range(NCH):
                        p.mm(psr[:, :E], xnf[c][0:64, tc], rtr_s[c][0:64, :],
                             start=(c == 0), stop=(c == NCH - 1))
                    for c in range(NCH):
                        p.mm(psr2[:, :E], xnf[c][64:128, tc], rtr_s[c][64:128, :],
                             start=(c == 0), stop=(c == NCH - 1))
                    p.copy(lg[:, :], psr[:, :E])
                    p.tt(lg[:, :], lg[:, :], psr2[:, :E], ALU.add)
                    m1 = col[:, 0:1]; m2 = col[:, 1:2]; dcol = col[:, 2:3]
                    s1 = col[:, 3:4]; s2 = col[:, 4:5]; neg1 = col[:, 5:6]
                    gc = col[:, 6:7]
                    A = tm[:, 0:E]; B = tm[:, 8:16]; Cc = tm[:, 16:24]

                    def tpose(colsl, dst):
                        for hf in range(2):
                            p.transpose(pst[0:E, 0:64], tm[hf * 64:(hf + 1) * 64, colsl],
                                        id2[hf * 64:(hf + 1) * 64, :])
                            p.copy(dst[0:E, t0 + ts_ * 128 + hf * 64:
                                       t0 + ts_ * 128 + (hf + 1) * 64],
                                   pst[0:E, 0:64])
                    p.reduce(m1, lg[:, :], ALU.max)
                    p.ts(A, lg[:, :], m1, None, ALU.is_equal)
                    if kind in ("l1", "l3"):
                        p.stt(B, A, -1e30, lg[:, :], ALU.mult, ALU.add)
                        p.reduce(m2, B, ALU.max)
                        p.ts(B, B, m2, None, ALU.is_equal)
                        p.tt(dcol, m1, m2, ALU.subtract)
                        p.act(s1, dcol, AF.Sigmoid)
                        p.ts(s2, s1, -1.0, 1.0, ALU.mult, ALU.add)
                        if kind == "l1":
                            p.ts(Cc, A, s1, None, ALU.mult)
                            p.stt(Cc, B, s2, Cc, ALU.mult, ALU.add)
                            tpose(slice(16, 24), gA)
                        else:
                            tpose(slice(0, E), gA)
                            tpose(slice(8, 16), gB)
                            p.ts(Cc, A, s1, None, ALU.mult)
                            tpose(slice(16, 24), gAg)
                            p.ts(Cc, B, s2, None, ALU.mult)
                            tpose(slice(16, 24), gBg)
                    else:
                        p.ts(neg1, m1, -1.0, None, ALU.mult)
                        p.act(B, lg[:, :], AF.Exp, bias=neg1)
                        p.reduce(gc, B, ALU.add)
                        p.reciprocal(gc, gc)
                        p.ts(B, A, gc, None, ALU.mult)
                        tpose(slice(0, E), gA)
                        tpose(slice(8, 16), gAg)
            if mode == "cnt":
                ccol = sbR("ccol", [16, 2])
                p.reduce(ccol[0:E, 0:1], gA[:, :], ALU.add)
                if kind == "l3":
                    p.reduce(ccol[0:E, 1:2], gB[:, :], ALU.add)
                    p.dma("qsp", cnt_out[0:E, :], ccol[0:E, 0:1])
                    p.dma("qsp", cnt_out[E:2 * E, :], ccol[0:E, 1:2])
                else:
                    p.dma("qsp", cnt_out[0:E, :], ccol[0:E, 0:1])
            else:
                if kind == "l1":
                    p.copy(w8[:, :], gA[:, :])
                else:
                    C = float(MOE_CAP[kind])
                    offs = sbR("offs", [E, 2])
                    cmsk = sbR("cmsk", [8, 1])
                    cnts_s = sbR("cnts_s", [8, 16])
                    p.dma("qsp", cmsk[:], cmaskd)
                    p.dma("qsp", cnts_s[:, 0:ncnt], cntsd)
                    pso = cm.ps[0]
                    def offcol(src_cols, dst):
                        p.mm(pso[:1, 0:E], cmsk[:, :], cnts_s[:, src_cols],
                             start=True, stop=True)
                        p.copy(col[0:1, 0:E], pso[0:1, 0:E])
                        p.transpose(cm.ps[1][0:E, 0:1], col[0:1, 0:E], id2[0:1, 0:1])
                        p.copy(dst, cm.ps[1][0:E, 0:1])
                    pos = sbR("pos", [E, T])
                    def capply(em, emg, oc_cols, wacc):
                        p.scan(pos[:, :], em[:, :], em[:, :], 0.0, ALU.add, ALU.bypass)
                        p.ts(pos[:, :], pos[:, :], oc_cols, C, ALU.add, ALU.is_le)
                        p.tt(pos[:, :], pos[:, :], em[:, :], ALU.mult)
                        if wacc:
                            p.tt(pos[:, :], pos[:, :], emg[:, :], ALU.mult)
                            p.tt(w8[:, :], w8[:, :], pos[:, :], ALU.add)
                        else:
                            p.tt(w8[:, :], pos[:, :], emg[:, :], ALU.mult)
                    if kind == "l2":
                        offcol(slice(0, 8), offs[:, 0:1])
                        capply(gA, gAg, offs[:, 0:1], False)
                    else:
                        offcol(slice(0, 8), offs[:, 0:1])
                        offcol(slice(8, 16), offs[:, 1:2])
                        capply(gA, gAg, offs[:, 0:1], False)
                        capply(gB, gBg, offs[:, 1:2], True)
                p.dma("qact", w8sp, w8[:, :], sem_hint="w8o")
            p.barrier()

        if mode == "cnt":
            p.flush()
            return nc

        NHH = 32 if kind != "l1" else 16
        NHC = 32 if kind != "l1" else 16
        with ExitStack() as phE:
            def sbE(name, shape, dt=F32):
                return phE.enter_context(nc.sbuf_tensor(name, shape, dt))
            w1t = [sbE(f"w1t{c}", [128, 4096], BF16) for c in range(NCH)]
            w2t = [sbE(f"w2t{c}", [128, D], BF16) for c in range(NHH)]
            hst = [sbE(f"hst{c}", [128, 512], BF16) for c in range(NHC)]
            xnsub = [sbE(f"xns{c}", [128, 512], BF16) for c in range(NCH)]
            gr = sbE("gr", [1, T])
            mask_sb = sbE("mask_sb", [128, 512])
            yb = sbE("yb", [128, 512])
            tmpg = sbE("tmpg", [128, 512]) if kind == "l3" else None
            b1c = sbE("b1c", [128, 32]) if kind != "l1" else None
            b2c = sbE("b2c", [128, 8]) if kind != "l1" else None
            psh = [cm.ps[4], cm.ps[5]]
            psu = cm.ps[6]
            psY = [cm.ps[i] for i in range(4)]
            psM = cm.ps[6]
            for e in range(E):
                for c in range(NCH):
                    p.dma("qpool", w1t[c][:], w1d[e, c * 128:(c + 1) * 128, :],
                          sem_hint=f"w1{c % 4}")
                for c in range(NHH):
                    p.dma("qpool", w2t[c][:], w2d[e, c * 128:(c + 1) * 128, :],
                          sem_hint=f"w2{c % 4}")
                if kind != "l1":
                    p.dma("qsp", b1c[:], b1d[e], sem_hint="b1")
                    p.dma("qsp", b2c[:], b2d[e], sem_hint="b2")
                p.dma("qsp", gr[:], w8sp[e:e + 1, :], sem_hint="gr")
                for tsub in range(T // 512):
                    s0 = tsub * 512
                    for c in range(NCH):
                        p.dma("qsp", xnsub[c][:], xnsp[c * 128:(c + 1) * 128, s0:s0 + 512],
                              sem_hint=f"xn{c % 4}")
                    p.mm(psM[:, :], cm.ones_row_f[:1, :], gr[:, s0:s0 + 512],
                         start=True, stop=True)
                    p.copy(mask_sb[:, :], psM[:, :])
                    for hc in range(NHC):
                        if kind == "l1":
                            pg, pu = psh[hc % 2], psu
                            for c in range(NCH):
                                p.mm(pg[:, :], w1t[c][:, hc * 128:(hc + 1) * 128],
                                     xnsub[c][:], start=(c == 0), stop=(c == NCH - 1))
                            for c in range(NCH):
                                p.mm(pu[:, :], w1t[c][:, 2048 + hc * 128:2048 + (hc + 1) * 128],
                                     xnsub[c][:], start=(c == 0), stop=(c == NCH - 1))
                            p.act(yb[:, :], pg[:, :], AF.Silu)
                            p.tt(hst[hc][:, :], yb[:, :], pu[:, :], ALU.mult)
                        else:
                            ph_ = psh[hc % 2]
                            for c in range(NCH):
                                p.mm(ph_[:, :], w1t[c][:, hc * 128:(hc + 1) * 128],
                                     xnsub[c][:], start=(c == 0), stop=(c == NCH - 1))
                            if kind == "l2":
                                p.ts(hst[hc][:, :], ph_[:, :], b1c[:, hc:hc + 1], 0.0,
                                     ALU.add, ALU.max)
                            else:
                                p.ts(tmpg[:, :], ph_[:, :], b1c[:, hc:hc + 1], None,
                                     ALU.add)
                                p.act(hst[hc][:, :], tmpg[:, :], AF.Gelu)
                    for oh in range(2):
                        for oc in range(4):
                            o = oh * 4 + oc
                            for hc in range(NHH):
                                p.mm(psY[oc][:, :], w2t[hc][:, o * 128:(o + 1) * 128],
                                     hst[hc][:, :], start=(hc == 0), stop=(hc == NHH - 1))
                        for oc in range(4):
                            o = oh * 4 + oc
                            if kind != "l1":
                                p.ts(yb[:, :], psY[oc][:, :], b2c[:, o:o + 1], None,
                                     ALU.add)
                                p.tt(yb[:, :], yb[:, :], mask_sb[:, :], ALU.mult)
                            else:
                                p.tt(yb[:, :], psY[oc][:, :], mask_sb[:, :], ALU.mult)
                            p.dma("qpool", accT[o * 128:(o + 1) * 128, s0:s0 + 512],
                                  yb[:, :],
                                  accum_op=(ALU.add if e > 0 else ALU.bypass))
            p.barrier()

        xe = p.sbuf("xef", [128, 512], F32)
        ae = p.sbuf("aef", [128, 512], F32)
        oe = p.sbuf("oef", [128, 512], F32)
        for c in range(NCH):
            for tsub in range(T // 512):
                s0 = tsub * 512
                p.dma("qsp", xe[:, :], xT[c * 128:(c + 1) * 128, s0:s0 + 512],
                      sem_hint="xf")
                p.dma("qact", ae[:, :], accT[c * 128:(c + 1) * 128, s0:s0 + 512],
                      sem_hint="af")
                p.stt(oe[:, :], ae[:, :], 0.5, xe[:, :], ALU.mult, ALU.add)
                p.dma("qsp", yT[c * 128:(c + 1) * 128, s0:s0 + 512], oe[:, :],
                      sem_hint="yf")
        p.flush()
    return nc



def build_l2attn(T=4096, TT=512):
    QC = 256
    nch_t = T // QC
    nc = bass.Bass("TRN2", target_bir_lowering=False, debug=False)
    xT = nc.dram_tensor("xT", [D, T], F32, kind="ExternalInput").ap()
    yT = nc.dram_tensor("yT", [D, T], F32, kind="ExternalOutput").ap()
    ins = {}
    for nm in ("WrT", "WkT", "WvT"):
        ins[nm] = nc.dram_tensor(nm, [D, D], F32, kind="ExternalInput").ap()
    ins["WoH"] = nc.dram_tensor("WoH", [H, DH, D], F32, kind="ExternalInput").ap()
    for nm in ("maaK", "maaV", "maaR", "brT", "bkT", "bvT", "boT"):
        ins[nm] = nc.dram_tensor(nm, [128, NCH], F32, kind="ExternalInput").ap()
    ins["tdec"] = nc.dram_tensor("tdec", [1, H], F32, kind="ExternalInput").ap()
    ins["tfaa"] = nc.dram_tensor("tfaa", [1, H], F32, kind="ExternalInput").ap()
    ins["DIF"] = nc.dram_tensor("DIF", [128, 512], F32, kind="ExternalInput").ap()
    ins["TRI01"] = nc.dram_tensor("TRI01", [128, 512], F32, kind="ExternalInput").ap()
    ins["DG"] = nc.dram_tensor("DG", [128, 512], F32, kind="ExternalInput").ap()
    ins["IOTC"] = nc.dram_tensor("IOTC", [128, 2], F32, kind="ExternalInput").ap()
    ins["JROW"] = nc.dram_tensor("JROW", [1, 256], F32, kind="ExternalInput").ap()
    rsp = nc.dram_tensor("rsp", [H, DH, T], BF16, kind="Internal").ap()
    ksp = nc.dram_tensor("ksp", [H, DH, T], BF16, kind="Internal").ap()
    vsp = nc.dram_tensor("vsp", [H, DH, T], BF16, kind="Internal").ap()

    nt = T // TT
    with ExitStack() as stack:
        p = Prog(nc, stack)
        cm = Common(p, T, TT)
        ones_col_bf = p.sbuf("ones_col_bf", [128, 1], BF16)
        onesc_d = nc.inline_tensor(np.ones((128, 1), np.float32).astype(ml_dtypes.bfloat16), "onesc_d")
        p.dma("qsp", ones_col_bf[:], onesc_d.ap())
        cols = {}
        for nm in ("maaK", "maaV", "maaR", "brT", "bkT", "bvT", "boT"):
            cols[nm] = p.sbuf(f"c_{nm}", [128, NCH], F32)
            p.dma("qsp", cols[nm][:], ins[nm], sem_hint="cols")
        m1c = {}
        for nm in ("maaK", "maaV", "maaR"):
            m1c[nm] = p.sbuf(f"m1_{nm}", [128, NCH], F32)
            p.ts(m1c[nm][:], cols[nm][:], -1.0, 1.0, ALU.mult, ALU.add)

        xnr = [p.sbuf(f"xnr{c}", [128, T], BF16) for c in range(NCH)]

        with ExitStack() as phA:
            def sbA(name, shape, dt=F32):
                return phA.enter_context(nc.sbuf_tensor(name, shape, dt))
            Wt = {}
            for nm in ("WrT", "WkT", "WvT"):
                tiles = [sbA(f"{nm}{c}", [128, D], BF16) for c in range(NCH)]
                for c in range(NCH):
                    p.dma("qpool", tiles[c][:], ins[nm][c * 128:(c + 1) * 128, :],
                          sem_hint=nm)
                Wt[nm] = tiles
            xstage = [sbA(f"xa{c}", [128, TT]) for c in range(NCH)]
            xbA = [sbA(f"xab{c}", [128, TT], BF16) for c in range(NCH)]
            sqwA = [sbA(f"sqa{c}", [128, TT], BF16) for c in range(NCH)]
            xnt = [sbA(f"xnt{c}", [128, TT], BF16) for c in range(NCH)]
            rows_ws = sbA("rows_a", [1, 6 * TT])
            for t in range(nt):
                t0 = t * TT
                for c in range(NCH):
                    p.dma("qsp", xstage[c][:], xT[c * 128:(c + 1) * 128, t0:t0 + TT],
                          sem_hint=f"xa{c % 4}")
                _ln_tile_v2(p, cm, xstage, xbA, sqwA, TT, xnt, ones_col_bf, rows_ws)
                for c in range(NCH):
                    p.copy(xnr[c][:, t0:t0 + TT], xnt[c][:, :TT])
            tmpf = sbA("tmpf", [128, TT])
            mix = {nm: [sbA(f"mx{nm}{c}", [128, TT], BF16) for c in range(NCH)]
                   for nm in ("K", "V", "R")}
            prj = [sbA(f"prj{c}", [128, TT], BF16) for c in range(NCH)]
            xsh0 = sbA("xsh0", [128, TT], BF16)
            for t in range(nt):
                t0 = t * TT
                for nm in ("K", "V", "R"):
                    maa = cols["maa" + nm]
                    maa1 = m1c["maa" + nm]
                    for c in range(NCH):
                        if t == 0:
                            p.copy(xsh0[:, 1:TT], xnr[c][:, 0:TT - 1])
                            p.memset(xsh0[:, 0:1], 0.0)
                            sh = xsh0[:, :TT]
                        else:
                            sh = xnr[c][:, t0 - 1:t0 + TT - 1]
                        p.ts(tmpf[:, :TT], sh, maa[:, c:c + 1], None, ALU.mult)
                        p.stt(mix[nm][c][:, :TT], xnr[c][:, t0:t0 + TT],
                              maa1[:, c:c + 1], tmpf[:, :TT], ALU.mult, ALU.add)
                for (nm, wnm, bnm, sp) in (("R", "WrT", "brT", rsp),
                                           ("K", "WkT", "bkT", ksp),
                                           ("V", "WvT", "bvT", vsp)):
                    for m in range(NCH):
                        ps = cm.ps[6]
                        for c in range(NCH):
                            p.mm(ps[:, :TT], Wt[wnm][c][:, m * 128:(m + 1) * 128],
                                 mix[nm][c][:, :TT], start=(c == 0), stop=(c == NCH - 1))
                        p.ts(prj[m][:, :TT], ps[:, :TT], 1.0, cols[bnm][:, m:m + 1],
                             ALU.mult, ALU.add)
                    for h in range(H):
                        ch, b = divmod(h, 2)
                        p.dma("qact", sp[h, :, t0:t0 + TT],
                              prj[ch][b * 64:(b + 1) * 64, :TT], sem_hint=f"sp{h % 4}")
            p.barrier()

        id2_d = nc.inline_tensor(np.concatenate([np.eye(64, dtype=np.float32)] * 2, 0).astype(ml_dtypes.bfloat16), "id2b_d")
        id2b = p.sbuf("id2b", [128, 64], BF16)
        p.dma("qsp", id2b[:], id2_d.ap())
        WoH_t = [p.sbuf(f"WoH{h}", [64, D], BF16) for h in range(H)]
        for h in range(H):
            p.dma("qpool", WoH_t[h][:], ins["WoH"][h], sem_hint="woh")
        dif = p.sbuf("dif", [128, 512], F32)
        tri = p.sbuf("tri", [128, 512], F32)
        dg = p.sbuf("dg", [128, 512], F32)
        iotc = p.sbuf("iotc", [128, 2], F32)
        p.dma("qact", dif[:], ins["DIF"], sem_hint="difc")
        p.dma("qact", tri[:], ins["TRI01"], sem_hint="difc")
        p.dma("qact", dg[:], ins["DG"], sem_hint="difc")
        p.dma("qact", iotc[:], ins["IOTC"], sem_hint="difc")
        jrow2 = p.sbuf("jrow2", [1, 256], F32)
        p.dma("qact", jrow2[:], ins["JROW"], sem_hint="difc")
        rowH = p.sbuf("rowH", [1, 8 * H], F32)
        p.dma("qsp", rowH[0:1, 0:H], ins["tdec"])
        p.dma("qsp", rowH[0:1, H:2 * H], ins["tfaa"])
        lnw = rowH[0:1, 2 * H:3 * H]
        wsr = rowH[0:1, 3 * H:4 * H]
        p.act(lnw, rowH[0:1, 0:H], AF.Exp)
        p.ts(lnw, lnw, -1.0, None, ALU.mult)
        p.ts(wsr, lnw, float(QC), None, ALU.mult)
        p.act(wsr, wsr, AF.Exp)
        hcols = p.sbuf("hcols", [128, 3 * H], F32)
        psb_ = cm.ps[5]
        for h in range(H):
            p.mm(psb_[:, 0:1], cm.ones_row_f[:1, :], lnw[0:1, h:h + 1])
            p.copy(hcols[:, h:h + 1], psb_[:, 0:1])
            p.mm(psb_[:, 0:1], cm.ones_row_f[:1, :], rowH[0:1, H + h:H + h + 1])
            p.copy(hcols[:, H + h:H + h + 1], psb_[:, 0:1])
            p.mm(psb_[:, 0:1], cm.ones_row_f[:1, :], wsr[0:1, h:h + 1])
            p.copy(hcols[:, 2 * H + h:2 * H + h + 1], psb_[:, 0:1])
        wmT = [p.sbuf(f"wmT{h}", [128, 512], BF16) for h in range(H)]
        wkcols = p.sbuf("wkcols", [128, 2 * H], F32)
        wbb = p.sbuf("wbb", [128, QC * H], F32)
        wf = p.sbuf("wf", [128, 512], F32)
        rQ = p.sbuf("rQ", [1, QC], F32)
        for h in range(H):
            p.ts(wf[:, :], dif[:, :], hcols[:, h:h + 1], None, ALU.mult)
            p.act(wf[:, :], wf[:, :], AF.Exp)
            p.tt(wf[:, :], wf[:, :], tri[:, :], ALU.mult)
            p.stt(wmT[h][:, :], dg[:, :], hcols[:, H + h:H + h + 1], wf[:, :],
                  ALU.mult, ALU.add)
            p.ts(wkcols[:, 2 * h:2 * h + 2], iotc[:, :], hcols[:, h:h + 1], None,
                 ALU.mult)
            p.act(wkcols[:, 2 * h:2 * h + 2], wkcols[:, 2 * h:2 * h + 2], AF.Exp)
            p.ts(rQ[0:1, :], jrow2[0:1, :], lnw[0:1, h:h + 1], None, ALU.mult)
            p.act(rQ[0:1, :], rQ[0:1, :], AF.Exp)
            p.mm(psb_[:, 0:QC], cm.ones_row_f[:1, :], rQ[0:1, :])
            p.copy(wbb[:, h * QC:(h + 1) * QC], psb_[:, 0:QC])
        stateF = p.sbuf("stateF", [64, H * 64], F32)
        stateB = p.sbuf("stateB", [64, H * 64], BF16)
        p.memset(stateF[:, :], 0.0)
        p.memset(stateB[:, :], 0.0)
        rbuf = [p.sbuf(f"rb{i}", [64, QC], BF16) for i in range(2)]
        kbuf2 = [p.sbuf(f"kb2{i}", [64, QC], BF16) for i in range(2)]
        vbuf = [p.sbuf(f"vb2{i}", [64, QC], BF16) for i in range(2)]
        ktm = [p.sbuf(f"ktm{i}", [128, 64], BF16) for i in range(2)]
        vtm = [p.sbuf(f"vtm{i}", [128, 64], BF16) for i in range(2)]
        As = [p.sbuf(f"As{i}", [128, QC], BF16) for i in range(2)]
        zt = p.sbuf("zt", [64, QC], F32)
        yh = [p.sbuf(f"yh{h}", [64, QC], BF16) for h in range(H)]
        xe2 = p.sbuf("xe2", [128, QC], F32)
        yw3 = p.sbuf("yw3", [128, QC], F32)
        ps_tr2 = p.psum("ps_tr2", [128, 512], BF16)
        psA_, psY, psZ, psS2, psO2 = cm.ps[0], cm.ps[1], cm.ps[2], cm.ps[3], cm.ps[4]

        for ci in range(nch_t):
            c0 = ci * QC
            for h in range(H):
                hp = h % 2
                p.dma("qsp", rbuf[hp][:, :], rsp[h, :, c0:c0 + QC], sem_hint=f"rr{hp}")
                p.dma("qact", kbuf2[hp][:, :], ksp[h, :, c0:c0 + QC], sem_hint=f"rk{hp}")
                p.dma("qpool", vbuf[hp][:, :], vsp[h, :, c0:c0 + QC], sem_hint=f"rv{hp}")
                for half in range(2):
                    p.transpose(ps_tr2[:, 0:64], kbuf2[hp][:, half * 128:(half + 1) * 128],
                                id2b[0:64, :])
                    p.ts(ktm[half][:, :], ps_tr2[:, 0:64],
                         wkcols[:, 2 * h + half:2 * h + half + 1], None, ALU.mult)
                    p.transpose(ps_tr2[:, 64:128], vbuf[hp][:, half * 128:(half + 1) * 128],
                                id2b[0:64, :])
                    p.copy(vtm[half][:, :], ps_tr2[:, 64:128])
                for half in range(2):
                    p.mm(psA_[:, 0:QC], kbuf2[hp][:, half * 128:(half + 1) * 128],
                         rbuf[hp][:, :], start=True, stop=True)
                    p.tt(As[half][:, :], psA_[:, 0:QC],
                         wmT[h][:, half * QC:(half + 1) * QC], ALU.mult)
                p.mm(psY[0:64, 0:QC], vtm[0][:, :], As[0][:, :], start=True, stop=False)
                p.mm(psY[0:64, 0:QC], vtm[1][:, :], As[1][:, :], start=False, stop=True)
                p.mm(psZ[0:64, 0:QC], stateB[:, h * 64:(h + 1) * 64], rbuf[hp][:, :],
                     start=True, stop=True)
                p.tt(zt[:, :], psZ[0:64, 0:QC], wbb[0:64, h * QC:(h + 1) * QC],
                     ALU.mult)
                p.tt(yh[h][:, :], zt[:, :], psY[0:64, 0:QC], ALU.add)
                p.mm(psS2[0:64, 0:64], ktm[0][:, :], vtm[0][:, :], start=True, stop=False)
                p.mm(psS2[0:64, 0:64], ktm[1][:, :], vtm[1][:, :], start=False, stop=True)
                p.ts(stateF[:, h * 64:(h + 1) * 64], stateF[:, h * 64:(h + 1) * 64],
                     hcols[0:64, 2 * H + h:2 * H + h + 1], None, ALU.mult, q="pool")
                p.tt(stateF[:, h * 64:(h + 1) * 64], stateF[:, h * 64:(h + 1) * 64],
                     psS2[0:64, 0:64], ALU.add)
                p.copy(stateB[:, h * 64:(h + 1) * 64], stateF[:, h * 64:(h + 1) * 64])
            for m in range(NCH):
                for h in range(H):
                    p.mm(psO2[:, 0:QC], WoH_t[h][:, m * 128:(m + 1) * 128], yh[h][:, :],
                         start=(h == 0), stop=(h == H - 1))
                p.dma("qsp", xe2[:, :], xT[m * 128:(m + 1) * 128, c0:c0 + QC],
                      sem_hint="xe2")
                p.ts(yw3[:, :], psO2[:, 0:QC], 1.0, cols["boT"][:, m:m + 1],
                     ALU.mult, ALU.add)
                p.tt(yw3[:, :], yw3[:, :], xe2[:, :], ALU.add)
                p.dma("qact", yT[m * 128:(m + 1) * 128, c0:c0 + QC], yw3[:, :],
                      sem_hint="yo2")
        p.flush()
    return nc


import os
from concourse.bass_utils import run_bass_kernel_spmd

NCORES = 8
TFULL = 4096


def _colsf(v):
    return np.ascontiguousarray(np.asarray(v, np.float32).reshape(8, 128).T)


def _host_consts():
    c = {}
    f = np.arange(1024)
    ang = 2 * np.pi * np.outer(f, f) / 1024.0
    c["dftC"] = np.cos(ang).astype(np.float32)
    c["dftS"] = np.sin(ang).astype(np.float32)
    half = DH // 2
    freqs = 1.0 / 10000.0 ** (np.arange(half) / half)
    a2 = np.arange(TFULL)[:, None] * freqs[None, :]
    cos, sin = np.cos(a2).astype(np.float32), np.sin(a2).astype(np.float32)
    CC = np.zeros((128, TFULL), np.float32)
    SS = np.zeros((128, TFULL), np.float32)
    for d_ in range(128):
        i = (d_ % 64) // 2
        CC[d_] = cos[:, i]
        SS[d_] = (-sin[:, i]) if d_ % 2 == 0 else sin[:, i]
    c["ropeC"] = CC.astype(ml_dtypes.bfloat16)
    c["ropeS"] = SS.astype(ml_dtypes.bfloat16)
    P = np.zeros((128, 128), np.float32)
    for i in range(64):
        P[2 * i, 2 * i + 1] = 1.0
        P[2 * i + 1, 2 * i] = 1.0
    c["rperm"] = P
    cmsk = np.zeros((128, 4, 512), np.float32)
    kr = np.arange(128)[:, None]
    qc = np.arange(512)[None, :]
    for dj in range(4):
        cmsk[:, dj, :] = np.where(dj * 128 + kr > qc, -1e30, 0.0)
    c["cmask"] = np.ascontiguousarray(cmsk.reshape(128, 4 * 512))
    DIF = np.zeros((128, 512), np.float32)
    TRI = np.zeros_like(DIF)
    DG = np.zeros_like(DIF)
    IOTC = np.zeros((128, 2), np.float32)
    for r_ in range(128):
        for hlf in range(2):
            p_ = r_ + 128 * hlf
            q_ = np.arange(256)
            DIF[r_, hlf * 256:(hlf + 1) * 256] = np.maximum(q_ - p_ - 1, 0)
            TRI[r_, hlf * 256:(hlf + 1) * 256] = (q_ > p_).astype(np.float32)
            DG[r_, hlf * 256:(hlf + 1) * 256] = (q_ == p_).astype(np.float32)
            IOTC[r_, hlf] = 255 - p_
    c["DIF"], c["TRI01"], c["DG"], c["IOTC"] = DIF, TRI, DG, IOTC
    c["JROW"] = np.ascontiguousarray(np.arange(256, dtype=np.float32).reshape(1, 256))
    return c


def _bE(b):
    b = np.asarray(b, np.float32)
    r = np.zeros((128, 8), np.float32)
    for w in range(4):
        for hf in range(2):
            r[:, w * 2 + hf] = b[w, hf * 128:(hf + 1) * 128]
    return np.ascontiguousarray(r)


_NC_CACHE = {}


def _get_nc(key, builder):
    if key not in _NC_CACHE:
        _NC_CACHE[key] = builder()
    return _NC_CACHE[key]


def _launch(nc, per_core_ins):
    res = run_bass_kernel_spmd(nc, per_core_ins, core_ids=list(range(NCORES)))
    return res.results


def kernel(**inputs):
    I = {k: np.asarray(v) for k, v in inputs.items()}
    x = I["L0_attn_x"]
    xTs = [np.ascontiguousarray(x[c].T) for c in range(NCORES)]
    C = _host_consts()
    H_, DH_ = H, DH

    l0f_w = {
        "wr": I["L0_ffn_w_freq_real"], "wi": I["L0_ffn_w_freq_imag"],
        "biasT": _colsf(I["L0_ffn_bias"]), "dftC": C["dftC"], "dftS": C["dftS"],
    }
    l0a_w = {
        "qW": I["L0_attn_qW"], "kW": I["L0_attn_kW"], "vW": I["L0_attn_vW"],
        "oW": I["L0_attn_oW"], "qbE": _bE(I["L0_attn_qb"]), "kbE": _bE(I["L0_attn_kb"]),
        "vbE": _bE(I["L0_attn_vb"]), "obE": _bE(I["L0_attn_ob"]),
        "J": np.ascontiguousarray(I["L0_attn_J"].reshape(1, 1024)),
    }
    l1f_w = {
        "router": I["L1_ffn_moe_router"], "w1e": I["L1_ffn_moe_w13"],
        "w2e": I["L1_ffn_moe_w2"],
    }
    l1a_w = {
        "WqT": np.ascontiguousarray(I["L1_attn_Wq"].T),
        "WkT": np.ascontiguousarray(I["L1_attn_Wk"].T),
        "WvT": np.ascontiguousarray(I["L1_attn_Wv"].T),
        "WpH": np.ascontiguousarray(I["L1_attn_Wp"].T.reshape(H_, DH_, 1024)),
        "bqT": _colsf(I["L1_attn_bq"]), "bkT": _colsf(I["L1_attn_bk"]),
        "bvT": _colsf(I["L1_attn_bv"]), "bpT": _colsf(I["L1_attn_bp"]),
    }
    l2f_w = {
        "router": I["L2_ffn_router_w"], "w1e": I["L2_ffn_w1"], "w2e": I["L2_ffn_w2"],
        "b1T": np.ascontiguousarray(I["L2_ffn_b1"].reshape(8, 32, 128).transpose(0, 2, 1)),
        "b2T": np.ascontiguousarray(I["L2_ffn_b2"].reshape(8, 8, 128).transpose(0, 2, 1)),
    }
    l2f_cnt_w = {"router": I["L2_ffn_router_w"]}
    l2a_w = {
        "WrT": np.ascontiguousarray(I["L2_attn_Wr"].T),
        "WkT": np.ascontiguousarray(I["L2_attn_Wk"].T),
        "WvT": np.ascontiguousarray(I["L2_attn_Wv"].T),
        "WoH": np.ascontiguousarray(I["L2_attn_Wo"].T.reshape(H_, DH_, 1024)),
        "maaK": _colsf(I["L2_attn_time_maa_k"]), "maaV": _colsf(I["L2_attn_time_maa_v"]),
        "maaR": _colsf(I["L2_attn_time_maa_r"]),
        "brT": _colsf(I["L2_attn_br"]), "bkT": _colsf(I["L2_attn_bk"]),
        "bvT": _colsf(I["L2_attn_bv"]), "boT": _colsf(I["L2_attn_bo"]),
        "tdec": np.ascontiguousarray(I["L2_attn_time_decay"].reshape(1, H_)),
        "tfaa": np.ascontiguousarray(I["L2_attn_time_faaaa"].reshape(1, H_)),
        "DIF": C["DIF"], "TRI01": C["TRI01"], "DG": C["DG"], "IOTC": C["IOTC"], "JROW": C["JROW"],
    }
    l3a_w = {
        "Wq": I["L3_attn_Wq"], "Wk": I["L3_attn_Wk"], "Wv": I["L3_attn_Wv"],
        "WoH": np.ascontiguousarray(I["L3_attn_Wo"].reshape(H_, DH_, 1024)),
        "boT": _colsf(I["L3_attn_bo"]),
        "ropeC": C["ropeC"], "ropeS": C["ropeS"], "rperm": C["rperm"],
        "cmask": C["cmask"],
    }
    l3f_w = {
        "router": I["L3_ffn_Wr"], "w1e": I["L3_ffn_W1"], "w2e": I["L3_ffn_W2"],
        "b1T": np.ascontiguousarray(I["L3_ffn_b1"].reshape(8, 32, 128).transpose(0, 2, 1)),
        "b2T": np.ascontiguousarray(I["L3_ffn_b2"].reshape(8, 8, 128).transpose(0, 2, 1)),
    }
    l3f_cnt_w = {"router": I["L3_ffn_Wr"]}

    coremasks = [np.ascontiguousarray(
        (np.arange(8) < c_).astype(np.float32).reshape(8, 1)) for c_ in range(NCORES)]

    def step(nc_key, builder, wmap, extra_per_core=None, out_key="yT"):
        nonlocal xTs
        nc = _get_nc(nc_key, builder)
        per = []
        for c_ in range(NCORES):
            m = dict(wmap)
            m["xT"] = xTs[c_]
            if extra_per_core is not None:
                m.update(extra_per_core[c_])
            per.append(m)
        res = _launch(nc, per)
        xTs = [np.ascontiguousarray(res[c_][out_key]) for c_ in range(NCORES)]
        if os.environ.get("MKDEBUG"):
            import sys as _sys
            fin = all(np.isfinite(t).all() for t in xTs)
            print(f"[mk] {nc_key}: finite={fin} rms={float(np.sqrt(np.mean(xTs[0]**2))):.4f}",
                  file=_sys.stderr)

    def cnt_step(nc_key, builder, wmap):
        nc = _get_nc(nc_key, builder)
        per = []
        for c_ in range(NCORES):
            m = dict(wmap)
            m["xT"] = xTs[c_]
            per.append(m)
        res = _launch(nc, per)
        cnts = np.ascontiguousarray(
            np.stack([res[c_]["cnt"][:, 0] for c_ in range(NCORES)]).astype(np.float32))
        return cnts

    def cap_ffn(kind, wmap, cnt_wmap):
        cnts = cnt_step(f"{kind}f_cnt", lambda: build_moe(kind, "cnt", T=TFULL), cnt_wmap)
        extra = [{"cnts": cnts, "coremask": coremasks[c_]} for c_ in range(NCORES)]
        step(f"{kind}f", lambda: build_moe(kind, "main", T=TFULL), wmap, extra)

    step("l0f", lambda: build_l0ffn(T=TFULL, TT=512), l0f_w)
    step("l0a", lambda: build_attn("l0", T=TFULL), l0a_w)
    step("l0f", lambda: build_l0ffn(T=TFULL, TT=512), l0f_w)
    step("l1f", lambda: build_moe("l1", "main", T=TFULL), l1f_w)
    step("l1a", lambda: build_attn("l1", T=TFULL), l1a_w)
    step("l1f", lambda: build_moe("l1", "main", T=TFULL), l1f_w)
    cap_ffn("l2", l2f_w, l2f_cnt_w)
    step("l2a", lambda: build_l2attn(T=TFULL), l2a_w)
    cap_ffn("l2", l2f_w, l2f_cnt_w)
    cap_ffn("l3", l3f_w, l3f_cnt_w)
    step("l3a", lambda: build_attn("l3", T=TFULL), l3a_w)
    cap_ffn("l3", l3f_w, l3f_cnt_w)

    out = np.stack([xTs[c_].T for c_ in range(NCORES)]).astype(np.float32)
    return out


# version 04
# speedup vs baseline: 1.1890x; 1.1890x over previous
import numpy as np
from contextlib import ExitStack, contextmanager

import concourse.bass as bass
import concourse.mybir as mybir

F32 = mybir.dt.float32
BF16 = mybir.dt.bfloat16
I32 = mybir.dt.int32
AF = mybir.ActivationFunctionType
ALU = mybir.AluOpType
AX = mybir.AxisListType

_QUEUES = {
    "pe": ("tensor", 1),
    "act": ("scalar", 1),
    "dve": ("vector", 1),
    "pool": ("gpsimd", 1),
    "sp": ("sync", 1),
    "qsp": ("sync", 16),
    "qact": ("scalar", 16),
    "qpool": ("gpsimd", 16),
}
_ENGINE_OF = {
    "pe": "tensor", "act": "scalar", "dve": "vector", "pool": "gpsimd",
    "sp": "sync", "qsp": "sync", "qact": "scalar", "qpool": "gpsimd",
}
_ENGINES = ("tensor", "scalar", "vector", "gpsimd", "sync")


def _dt_size(dt):
    try:
        return dt.size_bytes()
    except Exception:
        pass
    return {F32: 4, BF16: 2, I32: 4}.get(dt, 4)


def _ap_box(ap):
    t = ap.tensor
    shape = list(t.shape)
    esz = _dt_size(ap.dtype)
    rowsz = 1
    for s in shape[1:]:
        rowsz *= s
    off = ap.offset
    span = 1
    for step, cnt in ap.ap:
        span += (cnt - 1) * abs(step)
    lo, hi = off, off + span
    if rowsz > 0 and len(shape) > 1:
        p_lo = lo // rowsz
        p_hi = (hi - 1) // rowsz + 1
        if p_hi - p_lo <= 1:
            b_lo = (lo % rowsz) * esz
            b_hi = ((hi - 1) % rowsz + 1) * esz
        else:
            b_lo, b_hi = 0, rowsz * esz
    else:
        p_lo, p_hi, b_lo, b_hi = 0, 1, lo * esz, hi * esz
    base = getattr(t, "base_partition", 0) or 0
    return (t.name, base + p_lo, base + p_hi, b_lo, b_hi)


def _overlap(a, b):
    return (a[0] == b[0] and a[1] < b[2] and b[1] < a[2]
            and a[3] < b[4] and b[3] < a[4])


class _Access:
    __slots__ = ("box", "q", "val")
    def __init__(self, box, q, val):
        self.box = box; self.q = q; self.val = val


class Prog:
    NDMA = 8

    def __init__(self, nc: bass.Bass, stack: ExitStack):
        self.nc = nc
        self.stack = stack
        self.sems = {}
        self.counts = {}
        self.ethunks = {e: [] for e in _ENGINES}
        self.writes = {}
        self.reads = {}
        self.pruned_w = {}
        self.pruned_r = {}
        self._flushed = False
        self.skeys = []
        for q in _QUEUES:
            if q.startswith("q"):
                for i in range(self.NDMA):
                    k = f"{q}{i}"
                    self.sems[k] = stack.enter_context(nc.semaphore(f"S_{k}"))
                    self.counts[k] = 0
                    self.skeys.append(k)
            else:
                self.sems[q] = stack.enter_context(nc.semaphore(f"S_{q}"))
                self.counts[q] = 0
                self.skeys.append(q)
        self.clocks = {k: {} for k in self.skeys}
        self._rr = {q: 0 for q in _QUEUES if q.startswith("q")}

    @contextmanager
    def stage(self):
        st = ExitStack()
        old = self.stack
        self.stack = st
        try:
            with st:
                yield
                self.barrier()
        finally:
            self.stack = old

    def _un(self, name):
        if not hasattr(self, "_names"):
            self._names = set()
        if name in self._names:
            i = 2
            while f"{name}__{i}" in self._names:
                i += 1
            name = f"{name}__{i}"
        self._names.add(name)
        return name

    def sbuf(self, name, shape, dtype, stack=None):
        return (stack or self.stack).enter_context(
            self.nc.sbuf_tensor(self._un(name), shape, dtype))

    def psum(self, name, shape, dtype=F32, stack=None):
        return (stack or self.stack).enter_context(
            self.nc.psum_tensor(self._un(name), shape, dtype))

    def dram_i(self, name, shape, dtype):
        return self.nc.dram_tensor(self._un(name), shape, dtype, kind="Internal").ap()

    def inline(self, data, name):
        return self.nc.inline_tensor(np.ascontiguousarray(data), self._un(name))

    def collective_allgather(self, out_ap, in_ap, replica_groups):
        import concourse.mybir as _mb
        def fn(e):
            return e.collective_compute(
                "AllGather", _mb.AluOpType.bypass, replica_groups,
                [in_ap], [out_ap])
        return self.emit("qpool", fn, reads=[in_ap], writes=[out_ap])

    def _need_waits(self, skey, rboxes, wboxes):
        need = {}
        def add(acc):
            if skey == "pe" and acc.q == "pe":
                return
            v = acc.val
            if acc.q[0] == "q":
                v = self.counts[acc.q]
            if v > need.get(acc.q, 0):
                need[acc.q] = v
        def add_pruned(d, name):
            for oq, v in d.get(name, {}).items():
                if not (skey == "pe" and oq == "pe"):
                    vv = self.counts[oq] if oq[0] == "q" else v
                    if vv > need.get(oq, 0):
                        need[oq] = vv
        for box in rboxes:
            for acc in self.writes.get(box[0], ()):
                if _overlap(box, acc.box):
                    add(acc)
            add_pruned(self.pruned_w, box[0])
        for box in wboxes:
            for acc in self.writes.get(box[0], ()):
                if _overlap(box, acc.box):
                    add(acc)
            for acc in self.reads.get(box[0], ()):
                if _overlap(box, acc.box):
                    add(acc)
            add_pruned(self.pruned_w, box[0])
            add_pruned(self.pruned_r, box[0])
        clk = self.clocks[skey]
        return {oq: v for oq, v in need.items() if v > clk.get(oq, 0)}

    def _update_clock(self, q, waited):
        clk = self.clocks[q]
        for oq, val in waited.items():
            if val > clk.get(oq, 0):
                clk[oq] = val
            if val == self.counts[oq]:
                for tq, tv in self.clocks[oq].items():
                    if tq != q and tv > clk.get(tq, 0):
                        clk[tq] = tv

    def _prune(self, lst, summary, name):
        if len(lst) > 24:
            drop = lst[0:len(lst) - 16]
            del lst[0:len(lst) - 16]
            s = summary.setdefault(name, {})
            for acc in drop:
                if acc.val > s.get(acc.q, 0):
                    s[acc.q] = acc.val

    def _record(self, q, val, rboxes, wboxes):
        for box in rboxes:
            lst = self.reads.setdefault(box[0], [])
            lst.append(_Access(box, q, val))
            self._prune(lst, self.pruned_r, box[0])
        for box in wboxes:
            lst = self.writes.setdefault(box[0], [])
            lst.append(_Access(box, q, val))
            self._prune(lst, self.pruned_w, box[0])

    def emit(self, q, fn, reads=(), writes=(), inc=True, sem_hint=None):
        assert not self._flushed
        if q.startswith("q"):
            if sem_hint is None:
                i = self._rr[q]; self._rr[q] = (i + 1) % self.NDMA
            else:
                i = hash(sem_hint) % self.NDMA
            skey = f"{q}{i}"
        else:
            skey = q
        rboxes = [_ap_box(a) for a in reads]
        wboxes = [_ap_box(a) for a in writes]
        waits = self._need_waits(skey, rboxes, wboxes)
        for oq, v in waits.items():
            if v > self.counts[oq]:
                raise RuntimeError(
                    f"future-wait: {skey} op would wait {oq}>={v} but only "
                    f"{self.counts[oq]} emitted (open accumulation group read/"
                    f"written by later consumer?) reads={[b[0] for b in rboxes]} "
                    f"writes={[b[0] for b in wboxes]}")
        if q.startswith("q") and self.counts[skey] > 0:
            if waits.get(skey, 0) < self.counts[skey] and                self.clocks[skey].get(skey, 0) < self.counts[skey]:
                waits[skey] = self.counts[skey]
        self._update_clock(skey, waits)
        sem_inc = _QUEUES[q][1]
        val = self.counts[skey] + sem_inc
        if inc:
            self.counts[skey] = val
        sems = self.sems
        def thunk(eng, waits=waits, fn=fn, skey=skey, inc=inc, sem_inc=sem_inc):
            for oq, v in waits.items():
                eng.wait_ge(sems[oq], v)
            ins = fn(eng)
            if inc:
                if ins is None:
                    raise RuntimeError("emit(inc=True) got no instruction")
                ins.then_inc(sems[skey], sem_inc)
        self.ethunks[_ENGINE_OF[q]].append(thunk)
        self._record(skey, val, rboxes, wboxes)
        return (skey, val)

    def barrier(self):
        snapshot = {k: v for k, v in self.counts.items() if v > 0}
        sems = self.sems
        for eng in _ENGINES:
            def thunk(e, snapshot=snapshot):
                for oq, v in snapshot.items():
                    e.wait_ge(sems[oq], v)
            self.ethunks[eng].append(thunk)
        for k in self.skeys:
            clk = self.clocks[k]
            for oq, v in snapshot.items():
                if v > clk.get(oq, 0):
                    clk[oq] = v

    def dma(self, q, out, in_, sem_hint=None, **kw):
        assert q in ("qsp", "qact", "qpool")
        return self.emit(q, lambda e: e.dma_start(out=out, in_=in_, **kw),
                         reads=[in_], writes=[out], sem_hint=sem_hint)

    def mm(self, out, lhsT, rhs, start=True, stop=True, inc=None, **kw):
        if inc is None:
            inc = True
        return self.emit("pe", lambda e: e.matmul(out, lhsT, rhs, start=start, stop=stop, **kw),
                         reads=[lhsT, rhs], writes=[out], inc=inc)

    def mm_acc(self, out, pairs, **kw):
        n = len(pairs)
        for i, (l, r) in enumerate(pairs):
            self.mm(out, l, r, start=(i == 0), stop=(i == n - 1), inc=(i == n - 1), **kw)

    def transpose(self, out, in_, identity):
        return self.emit("pe", lambda e: e.transpose(out, in_, identity),
                         reads=[in_, identity], writes=[out])

    def act(self, out, in_, func, bias=0.0, scale=1.0, accum_out=None):
        reads = [in_] + [a for a in (bias, scale) if hasattr(a, "tensor")]
        writes = [out] + ([accum_out] if accum_out is not None else [])
        return self.emit("act", lambda e: e.activation(out, in_, func, bias=bias, scale=scale,
                                                       accum_out=accum_out),
                         reads=reads, writes=writes)

    def _vecop(self, q, name, out, ins, kwargs=None):
        kwargs = kwargs or {}
        def fn(e, name=name, out=out, ins=ins, kwargs=kwargs):
            return getattr(e, name)(out, *ins, **kwargs)
        reads = [a for a in ins if hasattr(a, "tensor")]
        return self.emit(q, fn, reads=reads, writes=[out])

    def copy(self, out, in_, q="dve"):
        return self._vecop(q, "tensor_copy", out, [in_])

    def tt(self, out, a, b, op, q="dve"):
        return self._vecop(q, "tensor_tensor", out, [a, b], kwargs={"op": op})

    def ts(self, out, in_, s1, s2=None, op0=ALU.mult, op1=ALU.bypass, q="dve"):
        def fn(e):
            return e.tensor_scalar(out, in_, s1, s2, op0, op1)
        reads = [in_] + [s for s in (s1, s2) if hasattr(s, "tensor")]
        return self.emit(q, fn, reads=reads, writes=[out])

    def stt(self, out, in0, scalar, in1, op0, op1, q="dve"):
        def fn(e):
            return e.scalar_tensor_tensor(out, in0, scalar, in1, op0, op1)
        reads = [in0, in1] + ([scalar] if hasattr(scalar, "tensor") else [])
        return self.emit(q, fn, reads=reads, writes=[out])

    def reduce(self, out, in_, op, q="dve", negate=None):
        kw = {"axis": AX.X, "op": op}
        if negate is not None:
            kw["negate"] = negate
        return self._vecop(q, "tensor_reduce", out, [in_], kwargs=kw)

    def memset(self, ap, val, q="dve"):
        return self.emit(q, lambda e: e.memset(ap, val), reads=[], writes=[ap])

    def select(self, out, mask, on_true, on_false):
        def fn(e):
            return e.select(out, mask, on_true, on_false)
        reads = [a for a in (mask, on_true, on_false) if hasattr(a, "tensor")]
        return self.emit("dve", fn, reads=reads, writes=[out])

    def reciprocal(self, out, in_):
        return self._vecop("dve", "reciprocal", out, [in_])

    def scan(self, out, d0, d1, initial, op0, op1, q="dve"):
        def fn(e):
            return e.tensor_tensor_scan(out, d0, d1, initial, op0, op1)
        reads = [a for a in (d0, d1, initial) if hasattr(a, "tensor")]
        return self.emit(q, fn, reads=reads, writes=[out])

    def iota(self, out, pattern, base=None, channel_multiplier=None):
        kw = {}
        if base is not None:
            kw["base"] = base
        if channel_multiplier is not None:
            kw["channel_multiplier"] = channel_multiplier
        return self.emit("pool", lambda e: e.iota(out, pattern, **kw),
                         reads=[], writes=[out])

    def flush(self):
        assert not self._flushed
        self._flushed = True
        nc = self.nc
        counts = dict(self.counts)
        sems = self.sems
        with nc.Block() as block:
            for eng_name in _ENGINES:
                thunks = self.ethunks[eng_name]
                if eng_name == "sync":
                    def final_sync(eng, thunks=thunks, counts=counts):
                        for th in thunks:
                            th(eng)
                        for q, v in counts.items():
                            if v > 0:
                                eng.wait_ge(sems[q], v)
                    block.sync(final_sync)
                else:
                    if not thunks:
                        continue
                    def body(eng, thunks=thunks):
                        for th in thunks:
                            th(eng)
                    getattr(block, eng_name)(body)
        return self


"""Sublayer launch builders for the macaron megakernel (per-core, feature-major).

Layout conventions (per core):
  - Residual stream in HBM: xT [1024, T] fp32 (feature-major; host transposes).
  - SBUF activation tiles: [128 partitions (feature chunk), cols].
  - Weights fed from host already transposed to [contraction, out] where needed.
  - bf16 matmuls: lhsT [128, M<=128] base 0, rhs [128, N<=512], PSUM fp32.
  - fp32 matmuls: K<=64; halves accumulated in separate base-0 / base-64 PSUM
    groups, then combined on DVE.
"""
import numpy as np
import ml_dtypes
from contextlib import ExitStack

import concourse.bass as bass
import concourse.mybir as mybir

D = 1024
NCH = 8
EPS = 1e-5


def bf(x):
    return np.asarray(x, dtype=np.float32).astype(ml_dtypes.bfloat16)


class Common:
    def __init__(self, p: Prog, T, TT=512):
        self.p = p
        self.T = T
        self.TT = TT
        self.nt = T // TT
        nc = p.nc
        self.ones_col_f = p.sbuf("ones_col_f", [128, 1], F32)
        self.ones_row_f = p.sbuf("ones_row_f", [1, 128], F32)
        ones_col_d = nc.inline_tensor(np.ones((128, 1), np.float32), "ones_col_d")
        ones_row_d = nc.inline_tensor(np.ones((1, 128), np.float32), "ones_row_d")
        p.dma("qsp", self.ones_col_f[:], ones_col_d.ap())
        p.dma("qsp", self.ones_row_f[:], ones_row_d.ap())
        self.ps = [p.psum(f"psb{i}", [128, 512], F32) for i in range(7)]

    def mm_f32(self, out, pos_pairs, neg_pairs=(), n=512, accumulate=False):
        p = self.p
        psA, psB, psC, psD = self.ps[0], self.ps[1], self.ps[2], self.ps[3]
        for i, (l, r) in enumerate(pos_pairs):
            lp = _half(l, 0); rp = _half(r, 0)
            p.mm(psA[:, :n], lp, rp, start=(i == 0), stop=(i == len(pos_pairs) - 1))
        for i, (l, r) in enumerate(pos_pairs):
            p.mm(psB[:, :n], _half(l, 1), _half(r, 1), start=(i == 0),
                 stop=(i == len(pos_pairs) - 1))
        for i, (l, r) in enumerate(neg_pairs):
            p.mm(psC[:, :n], _half(l, 0), _half(r, 0), start=(i == 0),
                 stop=(i == len(neg_pairs) - 1))
        for i, (l, r) in enumerate(neg_pairs):
            p.mm(psD[:, :n], _half(l, 1), _half(r, 1), start=(i == 0),
                 stop=(i == len(neg_pairs) - 1))
        if accumulate:
            p.tt(out, out, psA[:out.ap[0][1], :n], ALU.add)
        else:
            p.copy(out, psA[:out.ap[0][1], :n])
        p.tt(out, out, psB[:out.ap[0][1], :n], ALU.add)
        if neg_pairs:
            p.tt(out, out, psC[:out.ap[0][1], :n], ALU.subtract)
            p.tt(out, out, psD[:out.ap[0][1], :n], ALU.subtract)

    def layernorm_tile(self, xs, t0, cols, xn_out, sq_ws, rows_ws, out_off=None,
                       f32_out=None):
        p = self.p
        o0 = t0 if out_off is None else out_off
        psA, psB, sqA, sqB, bc_r, bc_m = (self.ps[i] for i in range(6))
        for c in range(NCH):
            x_t = xs[c][:, t0:t0 + cols]
            p.mm(psA[:1, :cols], self.ones_col_f[0:64, :], xs[c][0:64, t0:t0 + cols],
                 start=(c == 0), stop=(c == NCH - 1))
            p.mm(psB[:1, :cols], self.ones_col_f[64:128, :], xs[c][64:128, t0:t0 + cols],
                 start=(c == 0), stop=(c == NCH - 1))
            p.act(sq_ws[c][:, :cols], x_t, AF.Square)
            p.mm(sqA[:1, :cols], self.ones_col_f[0:64, :], sq_ws[c][0:64, :cols],
                 start=(c == 0), stop=(c == NCH - 1))
            p.mm(sqB[:1, :cols], self.ones_col_f[64:128, :], sq_ws[c][64:128, :cols],
                 start=(c == 0), stop=(c == NCH - 1))
        mean = rows_ws[:, 0:cols]
        s2 = rows_ws[:, cols:2 * cols]
        var = rows_ws[:, 2 * cols:3 * cols]
        rstd = rows_ws[:, 3 * cols:4 * cols]
        mr = rows_ws[:, 4 * cols:5 * cols]
        tmpr = rows_ws[:, 5 * cols:6 * cols]
        p.copy(mean, psA[:1, :cols])
        p.tt(mean, mean, psB[:1, :cols], ALU.add)
        p.ts(mean, mean, 1.0 / D, None, ALU.mult)
        p.copy(s2, sqA[:1, :cols])
        p.tt(s2, s2, sqB[:1, :cols], ALU.add)
        p.tt(tmpr, mean, mean, ALU.mult)
        p.ts(s2, s2, 1.0 / D, None, ALU.mult)
        p.tt(var, s2, tmpr, ALU.subtract)
        p.ts(var, var, EPS, None, ALU.add)
        p.act(tmpr, var, AF.Sqrt)
        p.reciprocal(rstd, tmpr)
        p.tt(mr, mean, rstd, ALU.mult)
        p.mm(bc_r[:, :cols], self.ones_row_f[:1, :], rstd)
        p.mm(bc_m[:, :cols], self.ones_row_f[:1, :], mr)
        for c in range(NCH):
            x_t = xs[c][:, t0:t0 + cols]
            p.tt(sq_ws[c][:, :cols], x_t, bc_r[:, :cols], ALU.mult)
            if f32_out is None:
                p.tt(xn_out[c][:, o0:o0 + cols], sq_ws[c][:, :cols], bc_m[:, :cols],
                     ALU.subtract)
            else:
                p.tt(f32_out[c][:, :cols], sq_ws[c][:, :cols], bc_m[:, :cols],
                     ALU.subtract)
                p.copy(xn_out[c][:, o0:o0 + cols], f32_out[c][:, :cols])

    def load_w_bf(self, name, w_hbm, rows, cols, q="qpool"):
        p = self.p
        tiles = []
        for c in range(rows // 128):
            t = p.sbuf(f"{name}_{c}", [128, cols], BF16)
            p.dma(q, t[:], w_hbm[c * 128:(c + 1) * 128, :], sem_hint=name)
            tiles.append(t)
        return tiles

    def load_x(self, xT):
        p = self.p
        xs = [p.sbuf(f"xs{c}", [128, self.T], F32) for c in range(NCH)]
        for c in range(NCH):
            p.dma("qsp", xs[c][:], xT[c * 128:(c + 1) * 128, :], sem_hint="x")
        return xs

    def store_x(self, xs, yT):
        p = self.p
        for c in range(NCH):
            p.dma("qsp", yT[c * 128:(c + 1) * 128, :], xs[c][:], sem_hint="y")


def _half(ap, h):
    return ap[h * 64:(h + 1) * 64, :]



L0FFN_INS = {"wr": [D, D], "wi": [D, D], "biasT": [128, NCH],
             "dftC": [D, D], "dftS": [D, D]}


def build_l0ffn(T=4096, TT=512):
    nc = bass.Bass("TRN2", target_bir_lowering=False, debug=False)
    ins = {k: nc.dram_tensor(k, s, F32, kind="ExternalInput").ap()
           for k, s in L0FFN_INS.items()}
    xT = nc.dram_tensor("xT", [D, T], F32, kind="ExternalInput").ap()
    yT = nc.dram_tensor("yT", [D, T], F32, kind="ExternalOutput").ap()
    with ExitStack() as stack:
        p = Prog(nc, stack)
        cm = Common(p, T, TT)
        emit_l0ffn(p, cm, ins, xT, yT, T, TT)
        p.flush()
    return nc


def emit_l0ffn(p, cm, ins, xT, yT, T, TT):
    nc = p.nc
    wr, wi, biasT = ins["wr"], ins["wi"], ins["biasT"]
    dftC, dftS = ins["dftC"], ins["dftS"]
    if True:
        rem = [p.sbuf(f"rem{c}", [128, D], BF16) for c in range(NCH)]
        bias_s = p.sbuf("bias_s", [128, NCH], F32)
        p.dma("qsp", bias_s[:], biasT)
        p.ts(bias_s[:], bias_s[:], 0.5, None, ALU.mult)

        with ExitStack() as ph0:
            def sb(name, shape, dt=F32):
                return p.sbuf(name, shape, dt, stack=ph0)
            wr_s = [sb(f"wr{c}", [128, D]) for c in range(NCH)]
            wi_s = [sb(f"wi{c}", [128, D]) for c in range(NCH)]
            C_s = [sb(f"C{c}", [128, D]) for c in range(NCH)]
            S_s = [sb(f"S{c}", [128, D]) for c in range(NCH)]
            Pr_s = [sb(f"Pr{c}", [128, 512]) for c in range(NCH)]
            Pi_s = [sb(f"Pi{c}", [128, 512]) for c in range(NCH)]
            ws_comb = sb("ws_comb", [128, 512])
            for c in range(NCH):
                p.dma("qsp", wr_s[c][:], wr[c * 128:(c + 1) * 128, :], sem_hint="wr")
                p.dma("qsp", wi_s[c][:], wi[c * 128:(c + 1) * 128, :], sem_hint="wi")
                p.dma("qact", C_s[c][:], dftC[c * 128:(c + 1) * 128, :], sem_hint="C")
                p.dma("qact", S_s[c][:], dftS[c * 128:(c + 1) * 128, :], sem_hint="S")
            for bt in range(D // 512):
                b0 = bt * 512
                for m in range(NCH):
                    mc = slice(m * 128, (m + 1) * 128)
                    cm.mm_f32(Pr_s[m][:, :],
                              [(wr_s[f][:, mc], C_s[f][:, b0:b0 + 512]) for f in range(NCH)]
                              + [(wi_s[f][:, mc], S_s[f][:, b0:b0 + 512]) for f in range(NCH)])
                    cm.mm_f32(Pi_s[m][:, :],
                              [(wi_s[f][:, mc], C_s[f][:, b0:b0 + 512]) for f in range(NCH)],
                              [(wr_s[f][:, mc], S_s[f][:, b0:b0 + 512]) for f in range(NCH)])
                for bc4 in range(4):
                    bcs = slice(bc4 * 128, (bc4 + 1) * 128)
                    bchunk = bt * 4 + bc4
                    for at in range(2):
                        a0 = at * 512
                        psA, psB, psC, psD = cm.ps[0], cm.ps[1], cm.ps[2], cm.ps[3]
                        for i in range(NCH):
                            p.mm(psA[:, :], _half(Pr_s[i][:, bcs], 0),
                                 _half(C_s[i][:, a0:a0 + 512], 0),
                                 start=(i == 0), stop=(i == NCH - 1))
                        for i in range(NCH):
                            p.mm(psB[:, :], _half(Pr_s[i][:, bcs], 1),
                                 _half(C_s[i][:, a0:a0 + 512], 1),
                                 start=(i == 0), stop=(i == NCH - 1))
                        for i in range(NCH):
                            p.mm(psC[:, :], _half(Pi_s[i][:, bcs], 0),
                                 _half(S_s[i][:, a0:a0 + 512], 0),
                                 start=(i == 0), stop=(i == NCH - 1))
                        for i in range(NCH):
                            p.mm(psD[:, :], _half(Pi_s[i][:, bcs], 1),
                                 _half(S_s[i][:, a0:a0 + 512], 1),
                                 start=(i == 0), stop=(i == NCH - 1))
                        ws = ws_comb
                        p.copy(ws[:, :], psA[:, :])
                        p.tt(ws[:, :], ws[:, :], psB[:, :], ALU.add)
                        p.tt(ws[:, :], ws[:, :], psC[:, :], ALU.subtract)
                        p.tt(ws[:, :], ws[:, :], psD[:, :], ALU.subtract)
                        p.ts(rem[bchunk][:, a0:a0 + 512], ws[:, :], 1.0 / D, None, ALU.mult)
            p.barrier()

        xs = cm.load_x(xT)
        xn = [p.sbuf(f"xn{c}", [128, TT], BF16) for c in range(NCH)]
        sq_ws = [p.sbuf(f"sq_ws{c}", [128, TT], F32) for c in range(NCH)]
        rows_ws = p.sbuf("rows_ws", [1, 6 * TT], F32)
        yw = p.sbuf("yw", [128, TT], F32)
        for t in range(T // TT):
            t0 = t * TT
            cm.layernorm_tile(xs, t0, TT, xn, sq_ws, rows_ws, out_off=0)
            for m in range(NCH):
                mc = slice(m * 128, (m + 1) * 128)
                ps = cm.ps[6]
                for c in range(NCH):
                    p.mm(ps[:, :], rem[c][:, mc], xn[c][:, :], start=(c == 0),
                         stop=(c == NCH - 1))
                p.ts(yw[:, :], ps[:, :], 0.5, bias_s[:, m:m + 1], ALU.mult, ALU.add)
                p.tt(xs[m][:, t0:t0 + TT], xs[m][:, t0:t0 + TT], yw[:, :], ALU.add)
        cm.store_x(xs, yT)
        p.barrier()



H = 16
DH = 64


def _ln_tile_v2(p, cm, xstage, xb, sq, TT, xn, ones_col_bf, rows_ws, from_bf=False,
                f32_out=None):
    psA, sqA, bc_r, bc_m = cm.ps[0], cm.ps[2], cm.ps[4], cm.ps[5]
    for c in range(NCH):
        if not from_bf:
            p.copy(xb[c][:, :TT], xstage[c][:, :TT])
        p.act(sq[c][:, :TT], xb[c][:, :TT] if from_bf else xstage[c][:, :TT],
              AF.Square)
        p.mm(psA[:1, :TT], ones_col_bf[:, 0:1], xb[c][:, :TT], start=(c == 0),
             stop=(c == NCH - 1))
        p.mm(sqA[:1, :TT], ones_col_bf[:, 0:1], sq[c][:, :TT], start=(c == 0),
             stop=(c == NCH - 1))
    mean = rows_ws[:, 0:TT]
    var = rows_ws[:, 2 * TT:3 * TT]
    rstd = rows_ws[:, 3 * TT:4 * TT]
    mr = rows_ws[:, 4 * TT:5 * TT]
    tmpr = rows_ws[:, 5 * TT:6 * TT]
    p.ts(mean, psA[:1, :TT], 1.0 / D, None, ALU.mult)
    p.ts(var, sqA[:1, :TT], 1.0 / D, None, ALU.mult)
    p.tt(tmpr, mean, mean, ALU.mult)
    p.tt(var, var, tmpr, ALU.subtract)
    p.ts(var, var, EPS, None, ALU.add)
    p.act(tmpr, var, AF.Sqrt)
    p.reciprocal(rstd, tmpr)
    p.tt(mr, mean, rstd, ALU.mult)
    p.mm(bc_r[:, :TT], cm.ones_row_f[:1, :], rstd)
    p.mm(bc_m[:, :TT], cm.ones_row_f[:1, :], mr)
    for c in range(NCH):
        srcc = xb[c] if from_bf else xstage[c]
        p.tt(sq[c][:, :TT], srcc[:, :TT], bc_r[:, :TT], ALU.mult)
        if f32_out is None:
            p.tt(xn[c][:, :TT], sq[c][:, :TT], bc_m[:, :TT], ALU.subtract)
        else:
            p.tt(f32_out[c][:, :TT], sq[c][:, :TT], bc_m[:, :TT], ALU.subtract)
            p.copy(xn[c][:, :TT], f32_out[c][:, :TT])


def attn_input_spec(kind, T):
    if kind == "l1":
        d = {nm: ([D, D], F32) for nm in ("WqT", "WkT", "WvT")}
        d["WpH"] = ([H, DH, D], F32)
        for nm in ("bqT", "bkT", "bvT", "bpT"):
            d[nm] = ([128, NCH], F32)
    elif kind == "l3":
        d = {nm: ([D, D], F32) for nm in ("Wq", "Wk", "Wv")}
        d["WoH"] = ([H, DH, D], F32)
        d["boT"] = ([128, NCH], F32)
        d["ropeC"] = ([128, T], BF16)
        d["ropeS"] = ([128, T], BF16)
        d["rperm"] = ([128, 128], F32)
        d["cmask"] = ([128, 4 * 512], F32)
    else:
        d = {nm: ([4, 256, 256], F32) for nm in ("qW", "kW", "vW", "oW")}
        for nm in ("qbE", "kbE", "vbE", "obE"):
            d[nm] = ([128, 8], F32)
        d["J"] = ([1, D], F32)
    return d


def build_attn(kind, T=4096, TT=256):
    assert kind in ("l0", "l1", "l3")
    nc = bass.Bass("TRN2", target_bir_lowering=False, debug=False)
    ins = {k: nc.dram_tensor(k, s, dt, kind="ExternalInput").ap()
           for k, (s, dt) in attn_input_spec(kind, T).items()}
    xT = nc.dram_tensor("xT", [D, T], F32, kind="ExternalInput").ap()
    yT = nc.dram_tensor("yT", [D, T], F32, kind="ExternalOutput").ap()
    with ExitStack() as stack:
        p = Prog(nc, stack)
        cm = Common(p, T, TT)
        emit_attn(p, cm, kind, ins, xT, yT, T, TT)
        p.flush()
    return nc


def emit_attn(p, cm, kind, ins, xT, yT, T, TT):
    nc = p.nc
    qsp = p.dram_i("qsp", [H, DH, T], BF16)
    ksp = p.dram_i("ksp", [H, DH, T], BF16)
    nt = T // TT
    scale = DH ** -0.5
    if True:
        ones_col_bf = p.sbuf("ones_col_bf", [128, 1], BF16)
        ones_sq_bf = p.sbuf("ones_sq_bf", [1, 128], BF16)
        ident_bf = p.sbuf("ident_bf", [128, 128], BF16)
        idf_d = p.inline(np.eye(128, dtype=np.float32).astype(ml_dtypes.bfloat16), "idf_d")
        onesb_d = p.inline(np.ones((1, 128), np.float32).astype(ml_dtypes.bfloat16), "onesb_d")
        onesc_d = p.inline(np.ones((128, 1), np.float32).astype(ml_dtypes.bfloat16), "onesc_d")
        p.dma("qsp", ident_bf[:], idf_d.ap())
        p.dma("qsp", ones_sq_bf[:], onesb_d.ap())
        p.dma("qsp", ones_col_bf[:], onesc_d.ap())

        ve = [p.sbuf(f"ve{h}", [128, T // 128, 65], BF16) for h in range(H)]
        for h in range(H):
            p.memset(ve[h][:, :, 64:65], 1.0)

        bias_cols = {}
        alpha_cols = {}
        if kind == "l1":
            for nm in ("Wq", "Wk", "Wv", "Wp"):
                alpha_cols[nm] = p.sbuf(f"{nm}_acol", [128, NCH], F32)
            for nm in ("bq", "bk", "bv", "bp"):
                bias_cols[nm] = p.sbuf(f"{nm}_col", [128, NCH], F32)
        elif kind == "l3":
            bias_cols["bo"] = p.sbuf("bo_col", [128, NCH], F32)
        else:
            for nm in ("bq", "bk", "bv", "bo"):
                bias_cols[nm] = p.sbuf(f"{nm}_eff", [128, NCH], F32)
        if kind == "l0":
            cj = p.sbuf("cj", [1, 2], F32)


        with ExitStack() as phW:
            def sbW(name, shape, dt=F32):
                return p.sbuf(name, shape, dt, stack=phW)

            if kind == "l1":
                stage = sbW("stageW", [128, D])
                absb = sbW("absW", [128, D])
                arow = sbW("arow", [1, D])
                stage64 = sbW("stage64", [64, D])
                Ws = {}
                psA, psB, pst = cm.ps[0], cm.ps[1], cm.ps[2]
                for nm, hb in (("Wq", "WqT"), ("Wk", "WkT"), ("Wv", "WvT")):
                    tiles = [sbW(f"{nm}s{c}", [128, D], BF16) for c in range(NCH)]
                    Ws[nm] = tiles
                    for c in range(NCH):
                        p.dma("qsp", stage[:], ins[hb][c * 128:(c + 1) * 128, :],
                              sem_hint="wstage")
                        p.act(tiles[c][:], stage[:], AF.Sign)
                        p.act(absb[:], stage[:], AF.Abs)
                        for hv in range(2):
                            sl = slice(hv * 512, hv * 512 + 512)
                            p.mm(psA[:1, sl] if hv == 0 else psB[:1, :512],
                                 cm.ones_col_f[0:64, :], absb[0:64, sl],
                                 start=(c == 0), stop=(c == NCH - 1))
                            p.mm(pst[:1, :512] if hv == 0 else cm.ps[3][:1, :512],
                                 cm.ones_col_f[64:128, :], absb[64:128, sl],
                                 start=(c == 0), stop=(c == NCH - 1))
                    p.copy(arow[:, 0:512], psA[:1, 0:512])
                    p.tt(arow[:, 0:512], arow[:, 0:512], pst[:1, :512], ALU.add)
                    p.copy(arow[:, 512:1024], psB[:1, :512])
                    p.tt(arow[:, 512:1024], arow[:, 512:1024], cm.ps[3][:1, :512], ALU.add)
                    p.ts(arow[:], arow[:], 1.0 / D, None, ALU.mult)
                    acol = alpha_cols[nm]
                    for c in range(NCH):
                        p.transpose(pst[:, 0:1], arow[:1, c * 128:(c + 1) * 128],
                                    cm.ones_row_f[:1, 0:1])
                        p.copy(acol[:, c:c + 1], pst[:, 0:1])
                for nm, hb in (("bq", "bqT"), ("bk", "bkT"), ("bv", "bvT"), ("bp", "bpT")):
                    p.dma("qsp", bias_cols[nm][:], ins[hb])
            elif kind == "l3":
                def ld(nm):
                    tiles = [sbW(f"{nm}s{c}", [128, D], BF16) for c in range(NCH)]
                    for c in range(NCH):
                        p.dma("qpool", tiles[c][:], ins[nm][c * 128:(c + 1) * 128, :],
                              sem_hint=nm)
                    return tiles
                Ws = {"Wq": ld("Wq"), "Wk": ld("Wk"), "Wv": ld("Wv")}
                p.dma("qsp", bias_cols["bo"][:], ins["boT"])
                ropeC = sbW("ropeC_s", [128, T], BF16)
                ropeS = sbW("ropeS_s", [128, T], BF16)
                p.dma("qact", ropeC[:], ins["ropeC"], sem_hint="rc")
                p.dma("qact", ropeS[:], ins["ropeS"], sem_hint="rs")
                rperm = sbW("rperm_s", [128, 128], BF16)
                rp_st = sbW("rp_st", [128, 128])
                p.dma("qsp", rp_st[:], ins["rperm"])
                p.copy(rperm[:], rp_st[:])
            else:
                QSIGN = {
                    0: [(0, 1), (1, -1), (2, -1), (3, -1)],
                    1: [(1, 1), (0, 1), (3, -1), (2, 1)],
                    2: [(2, 1), (3, 1), (0, 1), (1, -1)],
                    3: [(3, 1), (2, -1), (1, 1), (0, 1)],
                }
                BSIGN = {0: [1, -1, -1, -1], 1: [1, 1, 1, -1],
                         2: [1, -1, 1, 1], 3: [1, 1, -1, 1]}
                stage = sbW("stageQ", [128, 256])
                stage64 = sbW("stageQ64", [64, 256])
                Ws = {}
                for nm, hb in (("Wq", "qW"), ("Wk", "kW"), ("Wv", "vW")):
                    tiles = [sbW(f"{nm}s{ch}", [128, D], BF16) for ch in range(NCH)]
                    Ws[nm] = tiles
                    for cin in range(4):
                        for hf in range(2):
                            ch = 2 * cin + hf
                            for comp in range(4):
                                w, sg = QSIGN[comp][cin]
                                p.dma("qsp", stage[:],
                                      ins[hb][w, 128 * hf:128 * hf + 128, :],
                                      sem_hint="qw")
                                p.ts(tiles[ch][:, 256 * comp:256 * comp + 256],
                                     stage[:], float(sg), None, ALU.mult)
                for nm, hb in (("bq", "qbE"), ("bk", "kbE"), ("bv", "vbE"), ("bo", "obE")):
                    raw = sbW(f"{nm}_raw", [128, 8])
                    p.dma("qsp", raw[:], ins[hb])
                    eff = bias_cols[nm]
                    for comp in range(4):
                        for hf in range(2):
                            ch = comp * 2 + hf
                            s = BSIGN[comp]
                            p.ts(eff[:, ch:ch + 1], raw[:, hf:hf + 1],
                                 float(s[0]), None, ALU.mult)
                            for w in range(1, 4):
                                p.stt(eff[:, ch:ch + 1], raw[:, w * 2 + hf:w * 2 + hf + 1],
                                      float(s[w]), eff[:, ch:ch + 1], ALU.mult, ALU.add)
                    bias_cols[nm] = eff
                jrow = sbW("jrow", [1, D])
                jsin = sbW("jsin", [1, D])
                p.dma("qsp", jrow[:], ins["J"])
                p.act(jsin[:], jrow[:], AF.Sin)
                p.reduce(cj[:, 0:1], jsin[:], ALU.add)
                p.ts(cj[:, 0:1], cj[:, 0:1], 0.1 / D, None, ALU.mult)

            from_bf = (kind == "l0")
            if not from_bf:
                xstage = [sbW(f"xa{c}", [128, TT]) for c in range(NCH)]
            else:
                xstage = [sbW("xa_sh", [128, TT])]* NCH
            xbA = [sbW(f"xab{c}", [128, TT], BF16) for c in range(NCH)]
            sqw = [sbW(f"sqa{c}", [128, TT], BF16) for c in range(NCH)]
            xn = [sbW(f"xna{c}", [128, TT], BF16) for c in range(NCH)]
            rows_ws = sbW("rows_a", [1, 6 * TT])
            if kind == "l0":
                qf = [sbW(f"qf{c}", [128, TT]) for c in range(NCH)]
                kf = [sbW(f"kf{c}", [128, TT]) for c in range(NCH)]
                n2 = [sbW(f"n2_{i}", [128, TT]) for i in range(2)]
                rn = [sbW(f"rn_{i}", [128, TT]) for i in range(2)]
                rnq = [sbW(f"rnq_{i}", [128, TT]) for i in range(2)]
            else:
                qf = kf = None
            if kind == "l3":
                tr1 = sbW("tr1", [128, TT])
                tr2 = sbW("tr2", [128, TT])
            qb_t = [sbW(f"qbt{c}", [128, TT], BF16) for c in range(NCH)]
            kb_t = [sbW(f"kbt{c}", [128, TT], BF16) for c in range(NCH)]
            vb_t = [sbW(f"vbt{c}", [128, TT], BF16) for c in range(NCH)]
            ps_tr = p.psum("pstr_a", [128, 512], BF16)

            for t in range(nt):
                t0 = t * TT
                if from_bf:
                    for c in range(NCH):
                        p.dma("qpool", xbA[c][:], xT[c * 128:(c + 1) * 128, t0:t0 + TT],
                              sem_hint=f"xa{c % 4}")
                else:
                    for c in range(NCH):
                        p.dma("qsp", xstage[c][:], xT[c * 128:(c + 1) * 128, t0:t0 + TT],
                              sem_hint=f"xa{c % 4}")
                _ln_tile_v2(p, cm, xstage, xbA, sqw, TT, xn, ones_col_bf, rows_ws,
                            from_bf=from_bf)
                for (dst_f, dst_b, wname, bname) in (
                        (qf, qb_t, "Wq", "bq"), (kf, kb_t, "Wk", "bk"),
                        (None, vb_t, "Wv", "bv")):
                    W = Ws[wname]
                    for m in range(NCH):
                        ps = cm.ps[6]
                        for c in range(NCH):
                            p.mm(ps[:, :TT], W[c][:, m * 128:(m + 1) * 128],
                                 xn[c][:, :TT], start=(c == 0), stop=(c == NCH - 1))
                        if kind == "l1":
                            p.ts(dst_b[m][:, :TT], ps[:, :TT],
                                 alpha_cols[wname][:, m:m + 1],
                                 bias_cols[bname][:, m:m + 1], ALU.mult, ALU.add)
                        elif kind == "l3":
                            sc0 = scale if wname == "Wq" else 1.0
                            p.ts(dst_b[m][:, :TT], ps[:, :TT], sc0, None, ALU.mult)
                        else:
                            tgt = dst_b[m] if dst_f is None else dst_f[m]
                            p.ts(tgt[:, :TT], ps[:, :TT], 1.0,
                                 bias_cols[bname][:, m:m + 1], ALU.mult, ALU.add)
                if kind == "l0":
                    for (src, dst, do_scale) in ((qf, qb_t, True), (kf, kb_t, False)):
                        for i in range(2):
                            p.act(n2[i][:, :TT], src[i][:, :TT], AF.Square)
                            for cc in (i + 2, i + 4, i + 6):
                                p.act(rn[i][:, :TT], src[cc][:, :TT], AF.Square)
                                p.tt(n2[i][:, :TT], n2[i][:, :TT], rn[i][:, :TT], ALU.add)
                            p.ts(n2[i][:, :TT], n2[i][:, :TT], 1e-6, None, ALU.add)
                            p.act(rn[i][:, :TT], n2[i][:, :TT], AF.Sqrt)
                            p.reciprocal(rn[i][:, :TT], rn[i][:, :TT])
                            if do_scale:
                                p.ts(rnq[i][:, :TT], rn[i][:, :TT], scale, None, ALU.mult)
                        rr = rnq if do_scale else rn
                        for c in range(NCH):
                            p.tt(dst[c][:, :TT], src[c][:, :TT], rr[c % 2][:, :TT],
                                 ALU.mult)
                elif kind == "l3":
                    for dst in (qb_t, kb_t):
                        for c in range(NCH):
                            psr = cm.ps[6]
                            p.mm(psr[:, :TT], rperm[:, :], dst[c][:, :TT])
                            p.tt(tr1[:, :TT], dst[c][:, :TT], ropeC[:, t0:t0 + TT],
                                 ALU.mult)
                            p.tt(tr2[:, :TT], psr[:, :TT], ropeS[:, t0:t0 + TT],
                                 ALU.mult)
                            p.tt(dst[c][:, :TT], tr1[:, :TT], tr2[:, :TT], ALU.add)
                else:
                    for c in range(NCH):
                        p.ts(qb_t[c][:, :TT], qb_t[c][:, :TT], scale, None, ALU.mult)
                for h in range(H):
                    ch, b = divmod(h, 2)
                    p.dma("qact", qsp[h, :, t0:t0 + TT],
                          qb_t[ch][b * 64:(b + 1) * 64, :TT], sem_hint=f"qs{h % 4}")
                    p.dma("qpool", ksp[h, :, t0:t0 + TT],
                          kb_t[ch][b * 64:(b + 1) * 64, :TT], sem_hint=f"ks{h % 4}")
                for c in range(NCH):
                    for j in range(TT // 128):
                        p.transpose(ps_tr[:, 0:128], vb_t[c][:, j * 128:(j + 1) * 128],
                                    ident_bf[:])
                        jg = (t0 + j * 128) // 128
                        p.copy(ve[2 * c][:, jg, 0:64], ps_tr[:, 0:64])
                        p.copy(ve[2 * c + 1][:, jg, 0:64], ps_tr[:, 64:128])
            p.barrier()

        WoH_t = [p.sbuf(f"WoH{h}", [64, D], BF16) for h in range(H)]
        if kind == "l3":
            cmask = p.sbuf("cmask_s", [128, 4 * 512], F32)
            p.dma("qact", cmask[:], ins["cmask"], sem_hint="cms")
            for h in range(H):
                p.dma("qpool", WoH_t[h][:], ins["WoH"][h], sem_hint="woh")
        elif kind == "l1":
            with ExitStack() as ph2:
                stage64b = p.sbuf("stage64b", [64, D], F32, stack=ph2)
                absb2 = p.sbuf("absb2", [64, D], F32, stack=ph2)
                arow2 = p.sbuf("arow2", [1, D], F32, stack=ph2)
                psA2, psB2, pst2 = cm.ps[0], cm.ps[1], cm.ps[2]
                for h in range(H):
                    p.dma("qsp", stage64b[:], ins["WpH"][h], sem_hint="wp64")
                    p.act(WoH_t[h][:], stage64b[:], AF.Sign)
                    p.act(absb2[:, :], stage64b[:], AF.Abs)
                    for hv in range(2):
                        sl = slice(hv * 512, hv * 512 + 512)
                        p.mm(psA2[:1, :512] if hv == 0 else psB2[:1, :512],
                             cm.ones_col_f[0:64, :], absb2[:, sl],
                             start=(h == 0), stop=(h == H - 1))
                p.ts(arow2[:, 0:512], psA2[:1, :512], 1.0 / D, None, ALU.mult)
                p.ts(arow2[:, 512:1024], psB2[:1, :512], 1.0 / D, None, ALU.mult)
                acol = alpha_cols["Wp"]
                for c in range(NCH):
                    p.transpose(pst2[:, 0:1], arow2[:1, c * 128:(c + 1) * 128],
                                cm.ones_row_f[:1, 0:1])
                    p.copy(acol[:, c:c + 1], pst2[:, 0:1])
                p.barrier()
        else:
            QSIGN0 = {
                0: [(0, 1), (1, -1), (2, -1), (3, -1)],
                1: [(1, 1), (0, 1), (3, -1), (2, 1)],
                2: [(2, 1), (3, 1), (0, 1), (1, -1)],
                3: [(3, 1), (2, -1), (1, 1), (0, 1)],
            }
            with ExitStack() as ph2:
                stage64b = p.sbuf("stage64b", [64, 256], F32, stack=ph2)
                for h in range(H):
                    cin, rest = divmod(h * 64, 256)
                    for comp in range(4):
                        w, sg = QSIGN0[comp][cin]
                        p.dma("qsp", stage64b[:], ins["oW"][w, rest:rest + 64, :],
                              sem_hint="qwo")
                        p.ts(WoH_t[h][:, 256 * comp:256 * comp + 256], stage64b[:],
                             float(sg), None, ALU.mult)
                p.barrier()

        ws_sq = p.sbuf("ws_sq", [64, 512], F32)
        maxk = p.sbuf("maxk", [1, H], F32)
        mrow = p.sbuf("mrow", [1, 64], F32)
        kbuf = [p.sbuf(f"kbuf{i}", [64, T], BF16) for i in range(2)]
        qbuf = [p.sbuf(f"qbuf{i}", [64, 512], BF16) for i in range(2)]
        qbias = p.sbuf("qbias", [1, 512], BF16)
        brow = p.sbuf("brow", [1, 512], F32)
        Pt = [p.sbuf(f"Pt{i}", [128, 512], BF16) for i in range(2)]
        Ot = [p.sbuf(f"Ot{h}", [64, 512], BF16) for h in range(H)]
        dbc = p.sbuf("dbc", [64, 512], F32)
        drow = p.sbuf("drow", [1, 512], F32)
        xe = p.sbuf("xe", [128, 512], F32)
        yw2 = p.sbuf("yw2", [128, 512], F32)
        psS = [cm.ps[0], cm.ps[1]]
        psO = cm.ps[2]
        psq2 = cm.ps[3]
        psb = cm.ps[4]
        psy = cm.ps[6]

        for h in range(H):
            p.dma("qsp", kbuf[h % 2][:, :], ksp[h], sem_hint=f"kb{h % 2}")
            for tt_ in range(T // 512):
                c0 = tt_ * 512
                p.act(ws_sq[:, :], kbuf[h % 2][:, c0:c0 + 512], AF.Square)
                p.mm(psq2[:1, :512], cm.ones_col_f[0:64, :], ws_sq[:, :],
                     start=True, stop=True)
                p.reduce(mrow[:, tt_:tt_ + 1], psq2[:1, :512], ALU.max)
            p.reduce(maxk[:, h:h + 1], mrow[:, 0:T // 512], ALU.max)
            p.act(maxk[:, h:h + 1], maxk[:, h:h + 1], AF.Sqrt)
            p.ts(maxk[:, h:h + 1], maxk[:, h:h + 1], -1.0, None, ALU.mult)

        nqt = T // 512
        for qt in range(nqt):
            q0 = qt * 512
            for h in range(H):
                p.dma("qsp", kbuf[h % 2][:, :], ksp[h], sem_hint=f"kb{h % 2}")
                p.dma("qact", qbuf[h % 2][:, :], qsp[h, :, q0:q0 + 512],
                      sem_hint=f"qb{h % 2}")
                p.act(ws_sq[:, :], qbuf[h % 2][:, :], AF.Square)
                p.mm(psq2[:1, :512], cm.ones_col_f[0:64, :], ws_sq[:, :],
                     start=True, stop=True)
                p.act(brow[:, :], psq2[:1, :512], AF.Sqrt)
                if kind == "l0":
                    p.ts(qbias[:, :], brow[:, :], maxk[:, h:h + 1], cj[:, 0:1],
                         ALU.mult, ALU.add)
                else:
                    p.ts(qbias[:, :], brow[:, :], maxk[:, h:h + 1], None, ALU.mult)
                kj_hi = (q0 + 512) // 128 if kind == "l3" else T // 128
                for j in range(kj_hi):
                    jp = j % 2
                    p.mm(psS[jp][:, :], kbuf[h % 2][:, j * 128:(j + 1) * 128],
                         qbuf[h % 2][:, :], start=True, stop=False, inc=True)
                    p.mm(psS[jp][:, :], ones_sq_bf[:1, :], qbias[:1, :],
                         start=False, stop=True)
                    if kind == "l3" and j * 128 >= q0:
                        dj = (j * 128 - q0) // 128
                        p.tt(psS[jp][:, :], psS[jp][:, :],
                             cmask[:, dj * 512:(dj + 1) * 512], ALU.add)
                    p.act(Pt[jp][:, :], psS[jp][:, :], AF.Exp)
                    p.mm(psO[0:65, :], ve[h][:, j, :], Pt[jp][:, :],
                         start=(j == 0), stop=(j == kj_hi - 1))
                p.reciprocal(drow[:, :], psO[64:65, :])
                p.mm(psb[:, :], cm.ones_row_f[:1, :], drow[:, :])
                p.copy(dbc[:, :], psb[0:64, :])
                p.tt(Ot[h][:, :], psO[0:64, :], dbc[:, :], ALU.mult)
            for m in range(NCH):
                for h in range(H):
                    p.mm(psy[:, :], WoH_t[h][:, m * 128:(m + 1) * 128], Ot[h][:, :],
                         start=(h == 0), stop=(h == H - 1))
                p.dma("qsp", xe[:, :], xT[m * 128:(m + 1) * 128, q0:q0 + 512],
                      sem_hint="xe")
                if kind == "l1":
                    p.ts(yw2[:, :], psy[:, :], alpha_cols["Wp"][:, m:m + 1],
                         bias_cols["bp"][:, m:m + 1], ALU.mult, ALU.add)
                else:
                    p.ts(yw2[:, :], psy[:, :], 1.0, bias_cols["bo"][:, m:m + 1],
                         ALU.mult, ALU.add)
                p.tt(yw2[:, :], yw2[:, :], xe[:, :], ALU.add)
                p.dma("qact", yT[m * 128:(m + 1) * 128, q0:q0 + 512], yw2[:, :],
                      sem_hint="yo")
        p.barrier()



MOE_CAP = {"l2": 4096, "l3": 5120}


def moe_input_spec(kind, mode, T):
    E = 8
    d = {"router": ([D, E], F32)}
    if mode == "main":
        NH = 32 if kind != "l1" else 16
        d["w1e"] = ([E, D, 4096], F32)
        d["w2e"] = ([E, NH * 128, D], F32)
        if kind != "l1":
            ncnt = E if kind == "l2" else 2 * E
            d["b1T"] = ([E, 128, 32], F32)
            d["b2T"] = ([E, 128, 8], F32)
            d["cnts"] = ([8, ncnt], F32)
            d["coremask"] = ([8, 1], F32)
    return d


def build_moe(kind, mode, T=4096, TT=512):
    assert kind in ("l1", "l2", "l3")
    assert mode in ("cnt", "main")
    E = 8
    nc = bass.Bass("TRN2", target_bir_lowering=False, debug=False)
    spec = moe_input_spec(kind, mode, T)
    if mode == "cnt":
        spec.pop("cnts", None); spec.pop("coremask", None)
    ins = {k: nc.dram_tensor(k, s, dt, kind="ExternalInput").ap()
           for k, (s, dt) in spec.items()}
    xT = nc.dram_tensor("xT", [D, T], F32, kind="ExternalInput").ap()
    if mode == "main":
        yT = nc.dram_tensor("yT", [D, T], F32, kind="ExternalOutput").ap()
        cnt_out = None
    else:
        ncnt = E if kind == "l2" else 2 * E
        cnt_out = nc.dram_tensor("cnt", [ncnt, 1], F32, kind="ExternalOutput").ap()
        yT = None
    with ExitStack() as stack:
        p = Prog(nc, stack)
        cm = Common(p, T, TT)
        emit_moe(p, cm, kind, mode, ins, xT, yT, cnt_out, T, TT,
                 collective=False)
        p.flush()
    return nc


def emit_moe(p, cm, kind, mode, ins, xT, yT, cnt_out, T, TT, collective=False,
             replica_groups=None):
    nc = p.nc
    E = 8
    rtr = ins["router"]
    if mode == "main":
        w1d, w2d = ins["w1e"], ins["w2e"]
        if kind != "l1":
            b1d, b2d = ins["b1T"], ins["b2T"]
            ncnt = E if kind == "l2" else 2 * E
            cmaskd = ins["coremask"]
            if collective:
                cc_in = p.dram_i("cc_in", [ncnt, 1], F32)
                cc_out = p.dram_i("cc_out", [8 * ncnt, 1], F32)
            else:
                cntsd = ins["cnts"]
        xnsp = p.dram_i("xnsp", [D, T], BF16)
        accT = p.dram_i("accT", [D, T], F32)
        w8sp = p.dram_i("w8sp", [E, T], F32)
    else:
        ncnt = E if kind == "l2" else 2 * E

    nt = T // TT
    if True:
        ones_col_bf = p.sbuf("ones_col_bf", [128, 1], BF16)
        onesc_d = p.inline(np.ones((128, 1), np.float32).astype(ml_dtypes.bfloat16), "onesc_d")
        p.dma("qsp", ones_col_bf[:], onesc_d.ap())
        id2_d = p.inline(np.concatenate([np.eye(64, dtype=np.float32)] * 2, 0), "id2_d")
        id2 = p.sbuf("id2", [128, 64], F32)
        p.dma("qsp", id2[:], id2_d.ap())

        rtr_s = [p.sbuf(f"rtr{c}", [128, E], F32) for c in range(NCH)]
        for c in range(NCH):
            p.dma("qsp", rtr_s[c][:], rtr[c * 128:(c + 1) * 128, :], sem_hint="rtr")

        w8 = p.sbuf("w8", [E, T], F32)

        with ExitStack() as phR:
            def sbR(name, shape, dt=F32):
                return p.sbuf(name, shape, dt, stack=phR)
            xstage = [sbR(f"xr{c}", [128, TT]) for c in range(NCH)]
            xnf = [sbR(f"xnf{c}", [128, TT]) for c in range(NCH)]
            sqw32 = [sbR(f"sqr32{c}", [128, TT]) for c in range(NCH)]
            xn = [sbR(f"xnr{c}", [128, TT], BF16) for c in range(NCH)]
            rows_ws = sbR("rows_r", [1, 6 * TT])
            lg = sbR("lg", [128, E])
            col = sbR("colws", [128, 12])
            tm = sbR("tm", [128, 24])
            gA = sbR("gA", [E, T])
            gB = sbR("gB", [E, T]) if kind == "l3" else None
            gAg = sbR("gAg", [E, T]) if kind != "l1" else None
            gBg = sbR("gBg", [E, T]) if kind == "l3" else None
            pst = cm.ps[6]

            for t in range(nt):
                t0 = t * TT
                for c in range(NCH):
                    p.dma("qsp", xstage[c][:], xT[c * 128:(c + 1) * 128, t0:t0 + TT],
                          sem_hint=f"xr{c % 4}")
                cm.layernorm_tile(xstage, 0, TT, xn, sqw32, rows_ws, out_off=0,
                                  f32_out=xnf)
                if mode == "main":
                    for c in range(NCH):
                        p.dma("qact", xnsp[c * 128:(c + 1) * 128, t0:t0 + TT],
                              xn[c][:, :TT], sem_hint=f"xs{c % 4}")
                for ts_ in range(TT // 128):
                    tc = slice(ts_ * 128, (ts_ + 1) * 128)
                    psr, psr2 = cm.ps[0], cm.ps[1]
                    for c in range(NCH):
                        p.mm(psr[:, :E], xnf[c][0:64, tc], rtr_s[c][0:64, :],
                             start=(c == 0), stop=(c == NCH - 1))
                    for c in range(NCH):
                        p.mm(psr2[:, :E], xnf[c][64:128, tc], rtr_s[c][64:128, :],
                             start=(c == 0), stop=(c == NCH - 1))
                    p.copy(lg[:, :], psr[:, :E])
                    p.tt(lg[:, :], lg[:, :], psr2[:, :E], ALU.add)
                    m1 = col[:, 0:1]; m2 = col[:, 1:2]; dcol = col[:, 2:3]
                    s1 = col[:, 3:4]; s2 = col[:, 4:5]; neg1 = col[:, 5:6]
                    gc = col[:, 6:7]
                    A = tm[:, 0:E]; B = tm[:, 8:16]; Cc = tm[:, 16:24]

                    def tpose(colsl, dst):
                        for hf in range(2):
                            p.transpose(pst[0:E, 0:64], tm[hf * 64:(hf + 1) * 64, colsl],
                                        id2[hf * 64:(hf + 1) * 64, :])
                            p.copy(dst[0:E, t0 + ts_ * 128 + hf * 64:
                                       t0 + ts_ * 128 + (hf + 1) * 64],
                                   pst[0:E, 0:64])
                    p.reduce(m1, lg[:, :], ALU.max)
                    p.ts(A, lg[:, :], m1, None, ALU.is_equal)
                    if kind in ("l1", "l3"):
                        p.stt(B, A, -1e30, lg[:, :], ALU.mult, ALU.add)
                        p.reduce(m2, B, ALU.max)
                        p.ts(B, B, m2, None, ALU.is_equal)
                        p.tt(dcol, m1, m2, ALU.subtract)
                        p.act(s1, dcol, AF.Sigmoid)
                        p.ts(s2, s1, -1.0, 1.0, ALU.mult, ALU.add)
                        if kind == "l1":
                            p.ts(Cc, A, s1, None, ALU.mult)
                            p.stt(Cc, B, s2, Cc, ALU.mult, ALU.add)
                            tpose(slice(16, 24), gA)
                        else:
                            tpose(slice(0, E), gA)
                            tpose(slice(8, 16), gB)
                            p.ts(Cc, A, s1, None, ALU.mult)
                            tpose(slice(16, 24), gAg)
                            p.ts(Cc, B, s2, None, ALU.mult)
                            tpose(slice(16, 24), gBg)
                    else:
                        p.ts(neg1, m1, -1.0, None, ALU.mult)
                        p.act(B, lg[:, :], AF.Exp, bias=neg1)
                        p.reduce(gc, B, ALU.add)
                        p.reciprocal(gc, gc)
                        p.ts(B, A, gc, None, ALU.mult)
                        tpose(slice(0, E), gA)
                        tpose(slice(8, 16), gAg)
            if mode == "cnt":
                ccol = sbR("ccol", [16, 2])
                p.reduce(ccol[0:E, 0:1], gA[:, :], ALU.add)
                if kind == "l3":
                    p.reduce(ccol[0:E, 1:2], gB[:, :], ALU.add)
                    p.dma("qsp", cnt_out[0:E, :], ccol[0:E, 0:1])
                    p.dma("qsp", cnt_out[E:2 * E, :], ccol[0:E, 1:2])
                else:
                    p.dma("qsp", cnt_out[0:E, :], ccol[0:E, 0:1])
            else:
                if kind == "l1":
                    p.copy(w8[:, :], gA[:, :])
                else:
                    C = float(MOE_CAP[kind])
                    offs = sbR("offs", [E, 2])
                    cmsk = sbR("cmsk", [8, 1])
                    cnts_s = sbR("cnts_s", [8, 16])
                    p.dma("qsp", cmsk[:], cmaskd)
                    if collective:
                        ccol = sbR("ccol", [16, 2])
                        p.reduce(ccol[0:E, 0:1], gA[:, :], ALU.add)
                        p.dma("qsp", cc_in[0:E, :], ccol[0:E, 0:1])
                        if kind == "l3":
                            p.reduce(ccol[0:E, 1:2], gB[:, :], ALU.add)
                            p.dma("qsp", cc_in[E:2 * E, :], ccol[0:E, 1:2])
                        p.collective_allgather(cc_out, cc_in,
                                               replica_groups or [list(range(8))])
                        p.dma("qsp", cnts_s[:, 0:ncnt],
                              cc_out.rearrange("(r n) o -> r (n o)", n=ncnt))
                    else:
                        p.dma("qsp", cnts_s[:, 0:ncnt], cntsd)
                    pso = cm.ps[0]
                    def offcol(src_cols, dst):
                        p.mm(pso[:1, 0:E], cmsk[:, :], cnts_s[:, src_cols],
                             start=True, stop=True)
                        p.copy(col[0:1, 0:E], pso[0:1, 0:E])
                        p.transpose(cm.ps[1][0:E, 0:1], col[0:1, 0:E], id2[0:1, 0:1])
                        p.copy(dst, cm.ps[1][0:E, 0:1])
                    pos = sbR("pos", [E, T])
                    def capply(em, emg, oc_cols, wacc):
                        p.scan(pos[:, :], em[:, :], em[:, :], 0.0, ALU.add, ALU.bypass)
                        p.ts(pos[:, :], pos[:, :], oc_cols, C, ALU.add, ALU.is_le)
                        p.tt(pos[:, :], pos[:, :], em[:, :], ALU.mult)
                        if wacc:
                            p.tt(pos[:, :], pos[:, :], emg[:, :], ALU.mult)
                            p.tt(w8[:, :], w8[:, :], pos[:, :], ALU.add)
                        else:
                            p.tt(w8[:, :], pos[:, :], emg[:, :], ALU.mult)
                    if kind == "l2":
                        offcol(slice(0, 8), offs[:, 0:1])
                        capply(gA, gAg, offs[:, 0:1], False)
                    else:
                        offcol(slice(0, 8), offs[:, 0:1])
                        offcol(slice(8, 16), offs[:, 1:2])
                        capply(gA, gAg, offs[:, 0:1], False)
                        capply(gB, gBg, offs[:, 1:2], True)
                p.dma("qact", w8sp, w8[:, :], sem_hint="w8o")
            p.barrier()

        if mode == "cnt":
            return

        NHH = 32 if kind != "l1" else 16
        NHC = 32 if kind != "l1" else 16
        with ExitStack() as phE:
            def sbE(name, shape, dt=F32):
                return p.sbuf(name, shape, dt, stack=phE)
            w1t = [sbE(f"w1t{c}", [128, 4096], BF16) for c in range(NCH)]
            w2t = [sbE(f"w2t{c}", [128, D], BF16) for c in range(NHH)]
            hst = [sbE(f"hst{c}", [128, 512], BF16) for c in range(NHC)]
            xnsub = [sbE(f"xns{c}", [128, 512], BF16) for c in range(NCH)]
            gr = sbE("gr", [1, T])
            mask_sb = sbE("mask_sb", [128, 512])
            yb = sbE("yb", [128, 512])
            tmpg = sbE("tmpg", [128, 512]) if kind == "l3" else None
            b1c = sbE("b1c", [128, 32]) if kind != "l1" else None
            b2c = sbE("b2c", [128, 8]) if kind != "l1" else None
            psh = [cm.ps[4], cm.ps[5]]
            psu = cm.ps[6]
            psY = [cm.ps[i] for i in range(4)]
            psM = cm.ps[6]
            for e in range(E):
                for c in range(NCH):
                    p.dma("qpool", w1t[c][:], w1d[e, c * 128:(c + 1) * 128, :],
                          sem_hint=f"w1{c % 4}")
                for c in range(NHH):
                    p.dma("qpool", w2t[c][:], w2d[e, c * 128:(c + 1) * 128, :],
                          sem_hint=f"w2{c % 4}")
                if kind != "l1":
                    p.dma("qsp", b1c[:], b1d[e], sem_hint="b1")
                    p.dma("qsp", b2c[:], b2d[e], sem_hint="b2")
                p.dma("qsp", gr[:], w8sp[e:e + 1, :], sem_hint="gr")
                for tsub in range(T // 512):
                    s0 = tsub * 512
                    for c in range(NCH):
                        p.dma("qsp", xnsub[c][:], xnsp[c * 128:(c + 1) * 128, s0:s0 + 512],
                              sem_hint=f"xn{c % 4}")
                    p.mm(psM[:, :], cm.ones_row_f[:1, :], gr[:, s0:s0 + 512],
                         start=True, stop=True)
                    p.copy(mask_sb[:, :], psM[:, :])
                    for hc in range(NHC):
                        if kind == "l1":
                            pg, pu = psh[hc % 2], psu
                            for c in range(NCH):
                                p.mm(pg[:, :], w1t[c][:, hc * 128:(hc + 1) * 128],
                                     xnsub[c][:], start=(c == 0), stop=(c == NCH - 1))
                            for c in range(NCH):
                                p.mm(pu[:, :], w1t[c][:, 2048 + hc * 128:2048 + (hc + 1) * 128],
                                     xnsub[c][:], start=(c == 0), stop=(c == NCH - 1))
                            p.act(yb[:, :], pg[:, :], AF.Silu)
                            p.tt(hst[hc][:, :], yb[:, :], pu[:, :], ALU.mult)
                        else:
                            ph_ = psh[hc % 2]
                            for c in range(NCH):
                                p.mm(ph_[:, :], w1t[c][:, hc * 128:(hc + 1) * 128],
                                     xnsub[c][:], start=(c == 0), stop=(c == NCH - 1))
                            if kind == "l2":
                                p.ts(hst[hc][:, :], ph_[:, :], b1c[:, hc:hc + 1], 0.0,
                                     ALU.add, ALU.max)
                            else:
                                p.ts(tmpg[:, :], ph_[:, :], b1c[:, hc:hc + 1], None,
                                     ALU.add)
                                p.act(hst[hc][:, :], tmpg[:, :], AF.Gelu)
                    for oh in range(2):
                        for oc in range(4):
                            o = oh * 4 + oc
                            for hc in range(NHH):
                                p.mm(psY[oc][:, :], w2t[hc][:, o * 128:(o + 1) * 128],
                                     hst[hc][:, :], start=(hc == 0), stop=(hc == NHH - 1))
                        for oc in range(4):
                            o = oh * 4 + oc
                            if kind != "l1":
                                p.ts(yb[:, :], psY[oc][:, :], b2c[:, o:o + 1], None,
                                     ALU.add)
                                p.tt(yb[:, :], yb[:, :], mask_sb[:, :], ALU.mult)
                            else:
                                p.tt(yb[:, :], psY[oc][:, :], mask_sb[:, :], ALU.mult)
                            p.dma("qpool", accT[o * 128:(o + 1) * 128, s0:s0 + 512],
                                  yb[:, :],
                                  accum_op=(ALU.add if e > 0 else ALU.bypass))
            p.barrier()

        xe = p.sbuf("xef", [128, 512], F32)
        ae = p.sbuf("aef", [128, 512], F32)
        oe = p.sbuf("oef", [128, 512], F32)
        for c in range(NCH):
            for tsub in range(T // 512):
                s0 = tsub * 512
                p.dma("qsp", xe[:, :], xT[c * 128:(c + 1) * 128, s0:s0 + 512],
                      sem_hint="xf")
                p.dma("qact", ae[:, :], accT[c * 128:(c + 1) * 128, s0:s0 + 512],
                      sem_hint="af")
                p.stt(oe[:, :], ae[:, :], 0.5, xe[:, :], ALU.mult, ALU.add)
                p.dma("qsp", yT[c * 128:(c + 1) * 128, s0:s0 + 512], oe[:, :],
                      sem_hint="yf")
        p.barrier()



def wkv_input_spec(T):
    d = {nm: ([D, D], F32) for nm in ("WrT", "WkT", "WvT")}
    d["WoH"] = ([H, DH, D], F32)
    for nm in ("maaK", "maaV", "maaR", "brT", "bkT", "bvT", "boT"):
        d[nm] = ([128, NCH], F32)
    d["tdec"] = ([1, H], F32)
    d["tfaa"] = ([1, H], F32)
    d["DIF"] = ([128, 512], F32)
    d["TRI01"] = ([128, 512], F32)
    d["DG"] = ([128, 512], F32)
    d["IOTC"] = ([128, 2], F32)
    d["JROW"] = ([1, 256], F32)
    return d


def build_l2attn(T=4096, TT=512):
    nc = bass.Bass("TRN2", target_bir_lowering=False, debug=False)
    ins = {k: nc.dram_tensor(k, s, dt, kind="ExternalInput").ap()
           for k, (s, dt) in wkv_input_spec(T).items()}
    xT = nc.dram_tensor("xT", [D, T], F32, kind="ExternalInput").ap()
    yT = nc.dram_tensor("yT", [D, T], F32, kind="ExternalOutput").ap()
    with ExitStack() as stack:
        p = Prog(nc, stack)
        cm = Common(p, T, TT)
        emit_l2attn(p, cm, ins, xT, yT, T, TT)
        p.flush()
    return nc


def emit_l2attn(p, cm, ins, xT, yT, T, TT):
    nc = p.nc
    QC = 256
    nch_t = T // QC
    rsp = p.dram_i("rsp", [H, DH, T], BF16)
    ksp = p.dram_i("ksp", [H, DH, T], BF16)
    vsp = p.dram_i("vsp", [H, DH, T], BF16)
    nt = T // TT
    if True:
        ones_col_bf = p.sbuf("ones_col_bf", [128, 1], BF16)
        onesc_d = p.inline(np.ones((128, 1), np.float32).astype(ml_dtypes.bfloat16), "onesc_d")
        p.dma("qsp", ones_col_bf[:], onesc_d.ap())
        cols = {}
        for nm in ("maaK", "maaV", "maaR", "brT", "bkT", "bvT", "boT"):
            cols[nm] = p.sbuf(f"c_{nm}", [128, NCH], F32)
            p.dma("qsp", cols[nm][:], ins[nm], sem_hint="cols")
        m1c = {}
        for nm in ("maaK", "maaV", "maaR"):
            m1c[nm] = p.sbuf(f"m1_{nm}", [128, NCH], F32)
            p.ts(m1c[nm][:], cols[nm][:], -1.0, 1.0, ALU.mult, ALU.add)

        xnr = [p.sbuf(f"xnr{c}", [128, T], BF16) for c in range(NCH)]

        with ExitStack() as phA:
            def sbA(name, shape, dt=F32):
                return p.sbuf(name, shape, dt, stack=phA)
            Wt = {}
            for nm in ("WrT", "WkT", "WvT"):
                tiles = [sbA(f"{nm}{c}", [128, D], BF16) for c in range(NCH)]
                for c in range(NCH):
                    p.dma("qpool", tiles[c][:], ins[nm][c * 128:(c + 1) * 128, :],
                          sem_hint=nm)
                Wt[nm] = tiles
            xstage = [sbA(f"xa{c}", [128, TT]) for c in range(NCH)]
            xbA = [sbA(f"xab{c}", [128, TT], BF16) for c in range(NCH)]
            sqwA = [sbA(f"sqa{c}", [128, TT], BF16) for c in range(NCH)]
            xnt = [sbA(f"xnt{c}", [128, TT], BF16) for c in range(NCH)]
            rows_ws = sbA("rows_a", [1, 6 * TT])
            for t in range(nt):
                t0 = t * TT
                for c in range(NCH):
                    p.dma("qsp", xstage[c][:], xT[c * 128:(c + 1) * 128, t0:t0 + TT],
                          sem_hint=f"xa{c % 4}")
                _ln_tile_v2(p, cm, xstage, xbA, sqwA, TT, xnt, ones_col_bf, rows_ws)
                for c in range(NCH):
                    p.copy(xnr[c][:, t0:t0 + TT], xnt[c][:, :TT])
            tmpf = sbA("tmpf", [128, TT])
            mix = {nm: [sbA(f"mx{nm}{c}", [128, TT], BF16) for c in range(NCH)]
                   for nm in ("K", "V", "R")}
            prj = [sbA(f"prj{c}", [128, TT], BF16) for c in range(NCH)]
            xsh0 = sbA("xsh0", [128, TT], BF16)
            for t in range(nt):
                t0 = t * TT
                for nm in ("K", "V", "R"):
                    maa = cols["maa" + nm]
                    maa1 = m1c["maa" + nm]
                    for c in range(NCH):
                        if t == 0:
                            p.copy(xsh0[:, 1:TT], xnr[c][:, 0:TT - 1])
                            p.memset(xsh0[:, 0:1], 0.0)
                            sh = xsh0[:, :TT]
                        else:
                            sh = xnr[c][:, t0 - 1:t0 + TT - 1]
                        p.ts(tmpf[:, :TT], sh, maa[:, c:c + 1], None, ALU.mult)
                        p.stt(mix[nm][c][:, :TT], xnr[c][:, t0:t0 + TT],
                              maa1[:, c:c + 1], tmpf[:, :TT], ALU.mult, ALU.add)
                for (nm, wnm, bnm, sp) in (("R", "WrT", "brT", rsp),
                                           ("K", "WkT", "bkT", ksp),
                                           ("V", "WvT", "bvT", vsp)):
                    for m in range(NCH):
                        ps = cm.ps[6]
                        for c in range(NCH):
                            p.mm(ps[:, :TT], Wt[wnm][c][:, m * 128:(m + 1) * 128],
                                 mix[nm][c][:, :TT], start=(c == 0), stop=(c == NCH - 1))
                        p.ts(prj[m][:, :TT], ps[:, :TT], 1.0, cols[bnm][:, m:m + 1],
                             ALU.mult, ALU.add)
                    for h in range(H):
                        ch, b = divmod(h, 2)
                        p.dma("qact", sp[h, :, t0:t0 + TT],
                              prj[ch][b * 64:(b + 1) * 64, :TT], sem_hint=f"sp{h % 4}")
            p.barrier()

        id2_d = p.inline(np.concatenate([np.eye(64, dtype=np.float32)] * 2, 0).astype(ml_dtypes.bfloat16), "id2b_d")
        id2b = p.sbuf("id2b", [128, 64], BF16)
        p.dma("qsp", id2b[:], id2_d.ap())
        WoH_t = [p.sbuf(f"WoH{h}", [64, D], BF16) for h in range(H)]
        for h in range(H):
            p.dma("qpool", WoH_t[h][:], ins["WoH"][h], sem_hint="woh")
        dif = p.sbuf("dif", [128, 512], F32)
        tri = p.sbuf("tri", [128, 512], F32)
        dg = p.sbuf("dg", [128, 512], F32)
        iotc = p.sbuf("iotc", [128, 2], F32)
        p.dma("qact", dif[:], ins["DIF"], sem_hint="difc")
        p.dma("qact", tri[:], ins["TRI01"], sem_hint="difc")
        p.dma("qact", dg[:], ins["DG"], sem_hint="difc")
        p.dma("qact", iotc[:], ins["IOTC"], sem_hint="difc")
        jrow2 = p.sbuf("jrow2", [1, 256], F32)
        p.dma("qact", jrow2[:], ins["JROW"], sem_hint="difc")
        rowH = p.sbuf("rowH", [1, 8 * H], F32)
        p.dma("qsp", rowH[0:1, 0:H], ins["tdec"])
        p.dma("qsp", rowH[0:1, H:2 * H], ins["tfaa"])
        lnw = rowH[0:1, 2 * H:3 * H]
        wsr = rowH[0:1, 3 * H:4 * H]
        p.act(lnw, rowH[0:1, 0:H], AF.Exp)
        p.ts(lnw, lnw, -1.0, None, ALU.mult)
        p.ts(wsr, lnw, float(QC), None, ALU.mult)
        p.act(wsr, wsr, AF.Exp)
        hcols = p.sbuf("hcols", [128, 3 * H], F32)
        psb_ = cm.ps[5]
        for h in range(H):
            p.mm(psb_[:, 0:1], cm.ones_row_f[:1, :], lnw[0:1, h:h + 1])
            p.copy(hcols[:, h:h + 1], psb_[:, 0:1])
            p.mm(psb_[:, 0:1], cm.ones_row_f[:1, :], rowH[0:1, H + h:H + h + 1])
            p.copy(hcols[:, H + h:H + h + 1], psb_[:, 0:1])
            p.mm(psb_[:, 0:1], cm.ones_row_f[:1, :], wsr[0:1, h:h + 1])
            p.copy(hcols[:, 2 * H + h:2 * H + h + 1], psb_[:, 0:1])
        wmT = [p.sbuf(f"wmT{h}", [128, 512], BF16) for h in range(H)]
        wkcols = p.sbuf("wkcols", [128, 2 * H], F32)
        wbb = p.sbuf("wbb", [128, QC * H], F32)
        wf = p.sbuf("wf", [128, 512], F32)
        rQ = p.sbuf("rQ", [1, QC], F32)
        for h in range(H):
            p.ts(wf[:, :], dif[:, :], hcols[:, h:h + 1], None, ALU.mult)
            p.act(wf[:, :], wf[:, :], AF.Exp)
            p.tt(wf[:, :], wf[:, :], tri[:, :], ALU.mult)
            p.stt(wmT[h][:, :], dg[:, :], hcols[:, H + h:H + h + 1], wf[:, :],
                  ALU.mult, ALU.add)
            p.ts(wkcols[:, 2 * h:2 * h + 2], iotc[:, :], hcols[:, h:h + 1], None,
                 ALU.mult)
            p.act(wkcols[:, 2 * h:2 * h + 2], wkcols[:, 2 * h:2 * h + 2], AF.Exp)
            p.ts(rQ[0:1, :], jrow2[0:1, :], lnw[0:1, h:h + 1], None, ALU.mult)
            p.act(rQ[0:1, :], rQ[0:1, :], AF.Exp)
            p.mm(psb_[:, 0:QC], cm.ones_row_f[:1, :], rQ[0:1, :])
            p.copy(wbb[:, h * QC:(h + 1) * QC], psb_[:, 0:QC])
        stateF = p.sbuf("stateF", [64, H * 64], F32)
        stateB = p.sbuf("stateB", [64, H * 64], BF16)
        p.memset(stateF[:, :], 0.0)
        p.memset(stateB[:, :], 0.0)
        rbuf = [p.sbuf(f"rb{i}", [64, QC], BF16) for i in range(2)]
        kbuf2 = [p.sbuf(f"kb2{i}", [64, QC], BF16) for i in range(2)]
        vbuf = [p.sbuf(f"vb2{i}", [64, QC], BF16) for i in range(2)]
        ktm = [p.sbuf(f"ktm{i}", [128, 64], BF16) for i in range(2)]
        vtm = [p.sbuf(f"vtm{i}", [128, 64], BF16) for i in range(2)]
        As = [p.sbuf(f"As{i}", [128, QC], BF16) for i in range(2)]
        zt = p.sbuf("zt", [64, QC], F32)
        yh = [p.sbuf(f"yh{h}", [64, QC], BF16) for h in range(H)]
        xe2 = p.sbuf("xe2", [128, QC], F32)
        yw3 = p.sbuf("yw3", [128, QC], F32)
        ps_tr2 = p.psum("ps_tr2", [128, 512], BF16)
        psA_, psY, psZ, psS2, psO2 = cm.ps[0], cm.ps[1], cm.ps[2], cm.ps[3], cm.ps[4]

        for ci in range(nch_t):
            c0 = ci * QC
            for h in range(H):
                hp = h % 2
                p.dma("qsp", rbuf[hp][:, :], rsp[h, :, c0:c0 + QC], sem_hint=f"rr{hp}")
                p.dma("qact", kbuf2[hp][:, :], ksp[h, :, c0:c0 + QC], sem_hint=f"rk{hp}")
                p.dma("qpool", vbuf[hp][:, :], vsp[h, :, c0:c0 + QC], sem_hint=f"rv{hp}")
                for half in range(2):
                    p.transpose(ps_tr2[:, 0:64], kbuf2[hp][:, half * 128:(half + 1) * 128],
                                id2b[0:64, :])
                    p.ts(ktm[half][:, :], ps_tr2[:, 0:64],
                         wkcols[:, 2 * h + half:2 * h + half + 1], None, ALU.mult)
                    p.transpose(ps_tr2[:, 64:128], vbuf[hp][:, half * 128:(half + 1) * 128],
                                id2b[0:64, :])
                    p.copy(vtm[half][:, :], ps_tr2[:, 64:128])
                for half in range(2):
                    p.mm(psA_[:, 0:QC], kbuf2[hp][:, half * 128:(half + 1) * 128],
                         rbuf[hp][:, :], start=True, stop=True)
                    p.tt(As[half][:, :], psA_[:, 0:QC],
                         wmT[h][:, half * QC:(half + 1) * QC], ALU.mult)
                p.mm(psY[0:64, 0:QC], vtm[0][:, :], As[0][:, :], start=True, stop=False)
                p.mm(psY[0:64, 0:QC], vtm[1][:, :], As[1][:, :], start=False, stop=True)
                p.mm(psZ[0:64, 0:QC], stateB[:, h * 64:(h + 1) * 64], rbuf[hp][:, :],
                     start=True, stop=True)
                p.tt(zt[:, :], psZ[0:64, 0:QC], wbb[0:64, h * QC:(h + 1) * QC],
                     ALU.mult)
                p.tt(yh[h][:, :], zt[:, :], psY[0:64, 0:QC], ALU.add)
                p.mm(psS2[0:64, 0:64], ktm[0][:, :], vtm[0][:, :], start=True, stop=False)
                p.mm(psS2[0:64, 0:64], ktm[1][:, :], vtm[1][:, :], start=False, stop=True)
                p.ts(stateF[:, h * 64:(h + 1) * 64], stateF[:, h * 64:(h + 1) * 64],
                     hcols[0:64, 2 * H + h:2 * H + h + 1], None, ALU.mult, q="pool")
                p.tt(stateF[:, h * 64:(h + 1) * 64], stateF[:, h * 64:(h + 1) * 64],
                     psS2[0:64, 0:64], ALU.add)
                p.copy(stateB[:, h * 64:(h + 1) * 64], stateF[:, h * 64:(h + 1) * 64])
            for m in range(NCH):
                for h in range(H):
                    p.mm(psO2[:, 0:QC], WoH_t[h][:, m * 128:(m + 1) * 128], yh[h][:, :],
                         start=(h == 0), stop=(h == H - 1))
                p.dma("qsp", xe2[:, :], xT[m * 128:(m + 1) * 128, c0:c0 + QC],
                      sem_hint="xe2")
                p.ts(yw3[:, :], psO2[:, 0:QC], 1.0, cols["boT"][:, m:m + 1],
                     ALU.mult, ALU.add)
                p.tt(yw3[:, :], yw3[:, :], xe2[:, :], ALU.add)
                p.dma("qact", yT[m * 128:(m + 1) * 128, c0:c0 + QC], yw3[:, :],
                      sem_hint="yo2")
        p.barrier()



def fused_input_specs(T):
    specs = {}
    specs["l0f"] = {k: (s, F32) for k, s in L0FFN_INS.items()}
    specs["l0a"] = attn_input_spec("l0", T)
    specs["l1f"] = moe_input_spec("l1", "main", T)
    specs["l1a"] = attn_input_spec("l1", T)
    specs["l2f"] = moe_input_spec("l2", "main", T)
    specs["l2a"] = wkv_input_spec(T)
    specs["l3f"] = moe_input_spec("l3", "main", T)
    specs["l3a"] = attn_input_spec("l3", T)
    return specs


def build_fused(T=4096, collective=True, replica_groups=None):
    nc = bass.Bass("TRN2", target_bir_lowering=False, debug=False)
    specs = fused_input_specs(T)
    ins = {}
    for stg, spec in specs.items():
        if collective and stg in ("l2f", "l3f"):
            spec = {k: v for k, v in spec.items() if k != "cnts"}
        ins[stg] = {k: nc.dram_tensor(f"{stg}_{k}", s, dt, kind="ExternalInput").ap()
                    for k, (s, dt) in spec.items()}
    xT = nc.dram_tensor("xT", [D, T], F32, kind="ExternalInput").ap()
    yT = nc.dram_tensor("yT", [D, T], F32, kind="ExternalOutput").ap()
    xA = nc.dram_tensor("xbufA", [D, T], F32, kind="Internal").ap()
    xB = nc.dram_tensor("xbufB", [D, T], F32, kind="Internal").ap()

    with ExitStack() as stack:
        p = Prog(nc, stack)
        cm = Common(p, T, 512)

        def e_l0f(i, o):
            emit_l0ffn(p, cm, ins["l0f"], i, o, T, 512)

        def e_l0a(i, o):
            emit_attn(p, cm, "l0", ins["l0a"], i, o, T, 256)

        def e_l1f(i, o):
            emit_moe(p, cm, "l1", "main", ins["l1f"], i, o, None, T, 512,
                     collective=collective, replica_groups=replica_groups)

        def e_l1a(i, o):
            emit_attn(p, cm, "l1", ins["l1a"], i, o, T, 256)

        def e_l2f(i, o):
            emit_moe(p, cm, "l2", "main", ins["l2f"], i, o, None, T, 512,
                     collective=collective, replica_groups=replica_groups)

        def e_l2a(i, o):
            emit_l2attn(p, cm, ins["l2a"], i, o, T, 512)

        def e_l3f(i, o):
            emit_moe(p, cm, "l3", "main", ins["l3f"], i, o, None, T, 512,
                     collective=collective, replica_groups=replica_groups)

        def e_l3a(i, o):
            emit_attn(p, cm, "l3", ins["l3a"], i, o, T, 256)

        seq = [e_l0f, e_l0a, e_l0f, e_l1f, e_l1a, e_l1f,
               e_l2f, e_l2a, e_l2f, e_l3f, e_l3a, e_l3f]
        cur = xT
        for i, em in enumerate(seq):
            out = yT if i == len(seq) - 1 else (xA if i % 2 == 0 else xB)
            with p.stage():
                em(cur, out)
            cur = out
        p.flush()
    return nc



def build_group(stages, T=4096):
    nc = bass.Bass("TRN2", target_bir_lowering=False, debug=False)
    ins = {}

    def get_ins(tag, spec):
        if tag not in ins:
            ins[tag] = {k: nc.dram_tensor(f"{tag}_{k}", s, dt, kind="ExternalInput").ap()
                        for k, (s, dt) in spec.items()}
        return ins[tag]

    xT = nc.dram_tensor("xT", [D, T], F32, kind="ExternalInput").ap()
    n_main = sum(1 for _, ks in stages if not ks.endswith(".cnt"))
    yT = nc.dram_tensor("yT", [D, T], F32, kind="ExternalOutput").ap() if n_main else None
    cnt_outs = {}
    for tag, ks in stages:
        if ks.endswith(".cnt"):
            kind = ks[:2]
            ncnt = 8 if kind == "l2" else 16
            cnt_outs[tag] = nc.dram_tensor(f"{tag}_cnt", [ncnt, 1], F32,
                                           kind="ExternalOutput").ap()
    xA = nc.dram_tensor("xbufA", [D, T], F32, kind="Internal").ap()
    xB = nc.dram_tensor("xbufB", [D, T], F32, kind="Internal").ap()

    with ExitStack() as stack:
        p = Prog(nc, stack)
        cm = Common(p, T, 512)
        cur = xT
        mains_done = 0
        for si, (tag, ks) in enumerate(stages):
            is_cnt = ks.endswith(".cnt")
            kind = ks[:2]
            if not is_cnt:
                mains_done += 1
                out = yT if mains_done == n_main else (xA if mains_done % 2 else xB)
            with p.stage():
                if ks == "l0f":
                    emit_l0ffn(p, cm, get_ins(tag, {k: (s, F32) for k, s in L0FFN_INS.items()}),
                               cur, out, T, 512)
                elif ks in ("l0a", "l1a", "l3a"):
                    emit_attn(p, cm, ks[:2], get_ins(tag, attn_input_spec(ks[:2], T)),
                              cur, out, T, 256)
                elif ks == "l2a":
                    emit_l2attn(p, cm, get_ins(tag, wkv_input_spec(T)), cur, out, T, 512)
                elif ks == "l1f":
                    emit_moe(p, cm, "l1", "main", get_ins(tag, moe_input_spec("l1", "main", T)),
                             cur, out, None, T, 512)
                elif ks.endswith(".main"):
                    emit_moe(p, cm, kind, "main", get_ins(tag, moe_input_spec(kind, "main", T)),
                             cur, out, None, T, 512)
                else:
                    emit_moe(p, cm, kind, "cnt",
                             get_ins(tag, {"router": ([D, 8], F32)}),
                             cur, None, cnt_outs[tag], T, 512)
            if not is_cnt:
                cur = out
        p.flush()
    return nc


import os
from concourse.bass_utils import run_bass_kernel_spmd

NCORES = 8
TFULL = 4096


def _colsf(v):
    return np.ascontiguousarray(np.asarray(v, np.float32).reshape(8, 128).T)


def _host_consts():
    c = {}
    f = np.arange(1024)
    ang = 2 * np.pi * np.outer(f, f) / 1024.0
    c["dftC"] = np.cos(ang).astype(np.float32)
    c["dftS"] = np.sin(ang).astype(np.float32)
    half = DH // 2
    freqs = 1.0 / 10000.0 ** (np.arange(half) / half)
    a2 = np.arange(TFULL)[:, None] * freqs[None, :]
    cos, sin = np.cos(a2).astype(np.float32), np.sin(a2).astype(np.float32)
    CC = np.zeros((128, TFULL), np.float32)
    SS = np.zeros((128, TFULL), np.float32)
    for d_ in range(128):
        i = (d_ % 64) // 2
        CC[d_] = cos[:, i]
        SS[d_] = (-sin[:, i]) if d_ % 2 == 0 else sin[:, i]
    c["ropeC"] = CC.astype(ml_dtypes.bfloat16)
    c["ropeS"] = SS.astype(ml_dtypes.bfloat16)
    P = np.zeros((128, 128), np.float32)
    for i in range(64):
        P[2 * i, 2 * i + 1] = 1.0
        P[2 * i + 1, 2 * i] = 1.0
    c["rperm"] = P
    cmsk = np.zeros((128, 4, 512), np.float32)
    kr = np.arange(128)[:, None]
    qc = np.arange(512)[None, :]
    for dj in range(4):
        cmsk[:, dj, :] = np.where(dj * 128 + kr > qc, -1e30, 0.0)
    c["cmask"] = np.ascontiguousarray(cmsk.reshape(128, 4 * 512))
    DIF = np.zeros((128, 512), np.float32)
    TRI = np.zeros_like(DIF)
    DG = np.zeros_like(DIF)
    IOTC = np.zeros((128, 2), np.float32)
    for r_ in range(128):
        for hlf in range(2):
            p_ = r_ + 128 * hlf
            q_ = np.arange(256)
            DIF[r_, hlf * 256:(hlf + 1) * 256] = np.maximum(q_ - p_ - 1, 0)
            TRI[r_, hlf * 256:(hlf + 1) * 256] = (q_ > p_).astype(np.float32)
            DG[r_, hlf * 256:(hlf + 1) * 256] = (q_ == p_).astype(np.float32)
            IOTC[r_, hlf] = 255 - p_
    c["DIF"], c["TRI01"], c["DG"], c["IOTC"] = DIF, TRI, DG, IOTC
    c["JROW"] = np.ascontiguousarray(np.arange(256, dtype=np.float32).reshape(1, 256))
    return c


def _bE(b):
    b = np.asarray(b, np.float32)
    r = np.zeros((128, 8), np.float32)
    for w in range(4):
        for hf in range(2):
            r[:, w * 2 + hf] = b[w, hf * 128:(hf + 1) * 128]
    return np.ascontiguousarray(r)


_NC_CACHE = {}


def _get_nc(key, builder):
    if key not in _NC_CACHE:
        _NC_CACHE[key] = builder()
    return _NC_CACHE[key]


def _launch(nc, per_core_ins):
    res = run_bass_kernel_spmd(nc, per_core_ins, core_ids=list(range(NCORES)))
    return res.results


def kernel(**inputs):
    I = {k: np.asarray(v) for k, v in inputs.items()}
    x = I["L0_attn_x"]
    xTs = [np.ascontiguousarray(x[c].T) for c in range(NCORES)]
    C = _host_consts()
    H_, DH_ = H, DH

    l0f_w = {
        "wr": I["L0_ffn_w_freq_real"], "wi": I["L0_ffn_w_freq_imag"],
        "biasT": _colsf(I["L0_ffn_bias"]), "dftC": C["dftC"], "dftS": C["dftS"],
    }
    l0a_w = {
        "qW": I["L0_attn_qW"], "kW": I["L0_attn_kW"], "vW": I["L0_attn_vW"],
        "oW": I["L0_attn_oW"], "qbE": _bE(I["L0_attn_qb"]), "kbE": _bE(I["L0_attn_kb"]),
        "vbE": _bE(I["L0_attn_vb"]), "obE": _bE(I["L0_attn_ob"]),
        "J": np.ascontiguousarray(I["L0_attn_J"].reshape(1, 1024)),
    }
    l1f_w = {
        "router": I["L1_ffn_moe_router"], "w1e": I["L1_ffn_moe_w13"],
        "w2e": I["L1_ffn_moe_w2"],
    }
    l1a_w = {
        "WqT": np.ascontiguousarray(I["L1_attn_Wq"].T),
        "WkT": np.ascontiguousarray(I["L1_attn_Wk"].T),
        "WvT": np.ascontiguousarray(I["L1_attn_Wv"].T),
        "WpH": np.ascontiguousarray(I["L1_attn_Wp"].T.reshape(H_, DH_, 1024)),
        "bqT": _colsf(I["L1_attn_bq"]), "bkT": _colsf(I["L1_attn_bk"]),
        "bvT": _colsf(I["L1_attn_bv"]), "bpT": _colsf(I["L1_attn_bp"]),
    }
    l2f_w = {
        "router": I["L2_ffn_router_w"], "w1e": I["L2_ffn_w1"], "w2e": I["L2_ffn_w2"],
        "b1T": np.ascontiguousarray(I["L2_ffn_b1"].reshape(8, 32, 128).transpose(0, 2, 1)),
        "b2T": np.ascontiguousarray(I["L2_ffn_b2"].reshape(8, 8, 128).transpose(0, 2, 1)),
    }
    l2f_cnt_w = {"router": I["L2_ffn_router_w"]}
    l2a_w = {
        "WrT": np.ascontiguousarray(I["L2_attn_Wr"].T),
        "WkT": np.ascontiguousarray(I["L2_attn_Wk"].T),
        "WvT": np.ascontiguousarray(I["L2_attn_Wv"].T),
        "WoH": np.ascontiguousarray(I["L2_attn_Wo"].T.reshape(H_, DH_, 1024)),
        "maaK": _colsf(I["L2_attn_time_maa_k"]), "maaV": _colsf(I["L2_attn_time_maa_v"]),
        "maaR": _colsf(I["L2_attn_time_maa_r"]),
        "brT": _colsf(I["L2_attn_br"]), "bkT": _colsf(I["L2_attn_bk"]),
        "bvT": _colsf(I["L2_attn_bv"]), "boT": _colsf(I["L2_attn_bo"]),
        "tdec": np.ascontiguousarray(I["L2_attn_time_decay"].reshape(1, H_)),
        "tfaa": np.ascontiguousarray(I["L2_attn_time_faaaa"].reshape(1, H_)),
        "DIF": C["DIF"], "TRI01": C["TRI01"], "DG": C["DG"], "IOTC": C["IOTC"], "JROW": C["JROW"],
    }
    l3a_w = {
        "Wq": I["L3_attn_Wq"], "Wk": I["L3_attn_Wk"], "Wv": I["L3_attn_Wv"],
        "WoH": np.ascontiguousarray(I["L3_attn_Wo"].reshape(H_, DH_, 1024)),
        "boT": _colsf(I["L3_attn_bo"]),
        "ropeC": C["ropeC"], "ropeS": C["ropeS"], "rperm": C["rperm"],
        "cmask": C["cmask"],
    }
    l3f_w = {
        "router": I["L3_ffn_Wr"], "w1e": I["L3_ffn_W1"], "w2e": I["L3_ffn_W2"],
        "b1T": np.ascontiguousarray(I["L3_ffn_b1"].reshape(8, 32, 128).transpose(0, 2, 1)),
        "b2T": np.ascontiguousarray(I["L3_ffn_b2"].reshape(8, 8, 128).transpose(0, 2, 1)),
    }
    l3f_cnt_w = {"router": I["L3_ffn_Wr"]}

    coremasks = [np.ascontiguousarray(
        (np.arange(8) < c_).astype(np.float32).reshape(8, 1)) for c_ in range(NCORES)]

    def step(nc_key, builder, wmap, extra_per_core=None, out_key="yT"):
        nonlocal xTs
        nc = _get_nc(nc_key, builder)
        per = []
        for c_ in range(NCORES):
            m = dict(wmap)
            m["xT"] = xTs[c_]
            if extra_per_core is not None:
                m.update(extra_per_core[c_])
            per.append(m)
        res = _launch(nc, per)
        xTs = [np.ascontiguousarray(res[c_][out_key]) for c_ in range(NCORES)]
        if os.environ.get("MKDEBUG"):
            import sys as _sys
            fin = all(np.isfinite(t).all() for t in xTs)
            print(f"[mk] {nc_key}: finite={fin} rms={float(np.sqrt(np.mean(xTs[0]**2))):.4f}",
                  file=_sys.stderr)

    def cnt_step(nc_key, builder, wmap):
        nc = _get_nc(nc_key, builder)
        per = []
        for c_ in range(NCORES):
            m = dict(wmap)
            m["xT"] = xTs[c_]
            per.append(m)
        res = _launch(nc, per)
        cnts = np.ascontiguousarray(
            np.stack([res[c_]["cnt"][:, 0] for c_ in range(NCORES)]).astype(np.float32))
        return cnts

    def cap_ffn(kind, wmap, cnt_wmap):
        cnts = cnt_step(f"{kind}f_cnt", lambda: build_moe(kind, "cnt", T=TFULL), cnt_wmap)
        extra = [{"cnts": cnts, "coremask": coremasks[c_]} for c_ in range(NCORES)]
        step(f"{kind}f", lambda: build_moe(kind, "main", T=TFULL), wmap, extra)

    if os.environ.get("MK_UNFUSED"):
        step("l0f", lambda: build_l0ffn(T=TFULL, TT=512), l0f_w)
        step("l0a", lambda: build_attn("l0", T=TFULL), l0a_w)
        step("l0f", lambda: build_l0ffn(T=TFULL, TT=512), l0f_w)
        step("l1f", lambda: build_moe("l1", "main", T=TFULL), l1f_w)
        step("l1a", lambda: build_attn("l1", T=TFULL), l1a_w)
        step("l1f", lambda: build_moe("l1", "main", T=TFULL), l1f_w)
        cap_ffn("l2", l2f_w, l2f_cnt_w)
        step("l2a", lambda: build_l2attn(T=TFULL), l2a_w)
        cap_ffn("l2", l2f_w, l2f_cnt_w)
        cap_ffn("l3", l3f_w, l3f_cnt_w)
        step("l3a", lambda: build_attn("l3", T=TFULL), l3a_w)
        cap_ffn("l3", l3f_w, l3f_cnt_w)
        out = np.stack([xTs[c_].T for c_ in range(NCORES)]).astype(np.float32)
        return out

    stage_w = {"l0f": l0f_w, "l0a": l0a_w, "l1f": l1f_w, "l1a": l1a_w,
               "l2f": l2f_w, "l2a": l2a_w, "l3f": l3f_w, "l3a": l3a_w,
               "l2c": l2f_cnt_w, "l3c": l3f_cnt_w}
    GROUPS = [
        ("gA", [("l0f", "l0f"), ("l0a", "l0a"), ("l0f", "l0f"),
                 ("l1f", "l1f"), ("l1a", "l1a"), ("l1f", "l1f"),
                 ("l2c", "l2f.cnt")]),
        ("gB", [("l2f", "l2f.main"), ("l2a", "l2a"), ("l2c", "l2f.cnt")]),
        ("gC", [("l2f", "l2f.main"), ("l3c", "l3f.cnt")]),
        ("gD", [("l3f", "l3f.main"), ("l3a", "l3a"), ("l3c", "l3f.cnt")]),
        ("gE", [("l3f", "l3f.main")]),
    ]
    pend_cnts = None
    for gname, stages in GROUPS:
        nc = _get_nc(gname, lambda stages=stages: build_group(stages, T=TFULL))
        cnt_tags = [tag for tag, ks in stages if ks.endswith(".cnt")]
        per = []
        for c_ in range(NCORES):
            m = {"xT": xTs[c_]}
            for tag, ks in stages:
                for k, v in stage_w[tag].items():
                    m[f"{tag}_{k}"] = v
                if ks.endswith(".main"):
                    m[f"{tag}_cnts"] = pend_cnts
                    m[f"{tag}_coremask"] = coremasks[c_]
            per.append(m)
        res = _launch(nc, per)
        if any(not ks.endswith(".cnt") for _, ks in stages):
            xTs = [np.ascontiguousarray(res[c_]["yT"]) for c_ in range(NCORES)]
        if cnt_tags:
            tag = cnt_tags[-1]
            pend_cnts = np.ascontiguousarray(
                np.stack([res[c_][f"{tag}_cnt"][:, 0] for c_ in range(NCORES)]
                         ).astype(np.float32))
    out = np.stack([xTs[c_].T for c_ in range(NCORES)]).astype(np.float32)
    return out


# version 05 (final)
# speedup vs baseline: 1.2001x; 1.0093x over previous
import numpy as np
from contextlib import ExitStack, contextmanager

import concourse.bass as bass
import concourse.mybir as mybir

F32 = mybir.dt.float32
BF16 = mybir.dt.bfloat16
I32 = mybir.dt.int32
AF = mybir.ActivationFunctionType
ALU = mybir.AluOpType
AX = mybir.AxisListType

_QUEUES = {
    "pe": ("tensor", 1),
    "act": ("scalar", 1),
    "dve": ("vector", 1),
    "pool": ("gpsimd", 1),
    "sp": ("sync", 1),
    "qsp": ("sync", 16),
    "qact": ("scalar", 16),
    "qpool": ("gpsimd", 16),
}
_ENGINE_OF = {
    "pe": "tensor", "act": "scalar", "dve": "vector", "pool": "gpsimd",
    "sp": "sync", "qsp": "sync", "qact": "scalar", "qpool": "gpsimd",
}
_ENGINES = ("tensor", "scalar", "vector", "gpsimd", "sync")


def _dt_size(dt):
    try:
        return dt.size_bytes()
    except Exception:
        pass
    return {F32: 4, BF16: 2, I32: 4}.get(dt, 4)


def _ap_box(ap):
    t = ap.tensor
    shape = list(t.shape)
    esz = _dt_size(ap.dtype)
    rowsz = 1
    for s in shape[1:]:
        rowsz *= s
    off = ap.offset
    span = 1
    for step, cnt in ap.ap:
        span += (cnt - 1) * abs(step)
    lo, hi = off, off + span
    if rowsz > 0 and len(shape) > 1:
        p_lo = lo // rowsz
        p_hi = (hi - 1) // rowsz + 1
        if p_hi - p_lo <= 1:
            b_lo = (lo % rowsz) * esz
            b_hi = ((hi - 1) % rowsz + 1) * esz
        else:
            b_lo, b_hi = 0, rowsz * esz
    else:
        p_lo, p_hi, b_lo, b_hi = 0, 1, lo * esz, hi * esz
    base = getattr(t, "base_partition", 0) or 0
    return (t.name, base + p_lo, base + p_hi, b_lo, b_hi)


def _overlap(a, b):
    return (a[0] == b[0] and a[1] < b[2] and b[1] < a[2]
            and a[3] < b[4] and b[3] < a[4])


class _Access:
    __slots__ = ("box", "q", "val")
    def __init__(self, box, q, val):
        self.box = box; self.q = q; self.val = val


class Prog:
    NDMA = 8

    def __init__(self, nc: bass.Bass, stack: ExitStack):
        self.nc = nc
        self.stack = stack
        self.sems = {}
        self.counts = {}
        self.ethunks = {e: [] for e in _ENGINES}
        self.writes = {}
        self.reads = {}
        self.pruned_w = {}
        self.pruned_r = {}
        self._flushed = False
        self.skeys = []
        for q in _QUEUES:
            if q.startswith("q"):
                for i in range(self.NDMA):
                    k = f"{q}{i}"
                    self.sems[k] = stack.enter_context(nc.semaphore(f"S_{k}"))
                    self.counts[k] = 0
                    self.skeys.append(k)
            else:
                self.sems[q] = stack.enter_context(nc.semaphore(f"S_{q}"))
                self.counts[q] = 0
                self.skeys.append(q)
        self.clocks = {k: {} for k in self.skeys}
        self._rr = {q: 0 for q in _QUEUES if q.startswith("q")}

    @contextmanager
    def stage(self):
        st = ExitStack()
        old = self.stack
        self.stack = st
        try:
            with st:
                yield
                self.barrier()
        finally:
            self.stack = old

    def _un(self, name):
        if not hasattr(self, "_names"):
            self._names = set()
        if name in self._names:
            i = 2
            while f"{name}__{i}" in self._names:
                i += 1
            name = f"{name}__{i}"
        self._names.add(name)
        return name

    def sbuf(self, name, shape, dtype, stack=None):
        return (stack or self.stack).enter_context(
            self.nc.sbuf_tensor(self._un(name), shape, dtype))

    def psum(self, name, shape, dtype=F32, stack=None):
        return (stack or self.stack).enter_context(
            self.nc.psum_tensor(self._un(name), shape, dtype))

    def dram_i(self, name, shape, dtype):
        return self.nc.dram_tensor(self._un(name), shape, dtype, kind="Internal").ap()

    def inline(self, data, name):
        return self.nc.inline_tensor(np.ascontiguousarray(data), self._un(name))

    def collective_allgather(self, out_ap, in_ap, replica_groups):
        import concourse.mybir as _mb
        def fn(e):
            return e.collective_compute(
                "AllGather", _mb.AluOpType.bypass, replica_groups,
                [in_ap], [out_ap])
        return self.emit("qpool", fn, reads=[in_ap], writes=[out_ap])

    def _need_waits(self, skey, rboxes, wboxes):
        need = {}
        def add(acc):
            if skey == "pe" and acc.q == "pe":
                return
            v = acc.val
            if acc.q[0] == "q":
                v = self.counts[acc.q]
            if v > need.get(acc.q, 0):
                need[acc.q] = v
        def add_pruned(d, name):
            for oq, v in d.get(name, {}).items():
                if not (skey == "pe" and oq == "pe"):
                    vv = self.counts[oq] if oq[0] == "q" else v
                    if vv > need.get(oq, 0):
                        need[oq] = vv
        for box in rboxes:
            for acc in self.writes.get(box[0], ()):
                if _overlap(box, acc.box):
                    add(acc)
            add_pruned(self.pruned_w, box[0])
        for box in wboxes:
            for acc in self.writes.get(box[0], ()):
                if _overlap(box, acc.box):
                    add(acc)
            for acc in self.reads.get(box[0], ()):
                if _overlap(box, acc.box):
                    add(acc)
            add_pruned(self.pruned_w, box[0])
            add_pruned(self.pruned_r, box[0])
        clk = self.clocks[skey]
        return {oq: v for oq, v in need.items() if v > clk.get(oq, 0)}

    def _update_clock(self, q, waited):
        clk = self.clocks[q]
        for oq, val in waited.items():
            if val > clk.get(oq, 0):
                clk[oq] = val
            if val == self.counts[oq]:
                for tq, tv in self.clocks[oq].items():
                    if tq != q and tv > clk.get(tq, 0):
                        clk[tq] = tv

    def _prune(self, lst, summary, name):
        if len(lst) > 24:
            drop = lst[0:len(lst) - 16]
            del lst[0:len(lst) - 16]
            s = summary.setdefault(name, {})
            for acc in drop:
                if acc.val > s.get(acc.q, 0):
                    s[acc.q] = acc.val

    def _record(self, q, val, rboxes, wboxes):
        for box in rboxes:
            lst = self.reads.setdefault(box[0], [])
            lst.append(_Access(box, q, val))
            self._prune(lst, self.pruned_r, box[0])
        for box in wboxes:
            lst = self.writes.setdefault(box[0], [])
            lst.append(_Access(box, q, val))
            self._prune(lst, self.pruned_w, box[0])

    def emit(self, q, fn, reads=(), writes=(), inc=True, sem_hint=None):
        assert not self._flushed
        if q.startswith("q"):
            if sem_hint is None:
                i = self._rr[q]; self._rr[q] = (i + 1) % self.NDMA
            else:
                i = hash(sem_hint) % self.NDMA
            skey = f"{q}{i}"
        else:
            skey = q
        rboxes = [_ap_box(a) for a in reads]
        wboxes = [_ap_box(a) for a in writes]
        waits = self._need_waits(skey, rboxes, wboxes)
        for oq, v in waits.items():
            if v > self.counts[oq]:
                raise RuntimeError(
                    f"future-wait: {skey} op would wait {oq}>={v} but only "
                    f"{self.counts[oq]} emitted (open accumulation group read/"
                    f"written by later consumer?) reads={[b[0] for b in rboxes]} "
                    f"writes={[b[0] for b in wboxes]}")
        if q.startswith("q") and self.counts[skey] > 0:
            if waits.get(skey, 0) < self.counts[skey] and                self.clocks[skey].get(skey, 0) < self.counts[skey]:
                waits[skey] = self.counts[skey]
        self._update_clock(skey, waits)
        sem_inc = _QUEUES[q][1]
        val = self.counts[skey] + sem_inc
        if inc:
            self.counts[skey] = val
        sems = self.sems
        def thunk(eng, waits=waits, fn=fn, skey=skey, inc=inc, sem_inc=sem_inc):
            for oq, v in waits.items():
                eng.wait_ge(sems[oq], v)
            ins = fn(eng)
            if inc:
                if ins is None:
                    raise RuntimeError("emit(inc=True) got no instruction")
                ins.then_inc(sems[skey], sem_inc)
        self.ethunks[_ENGINE_OF[q]].append(thunk)
        self._record(skey, val, rboxes, wboxes)
        return (skey, val)

    def barrier(self):
        snapshot = {k: v for k, v in self.counts.items() if v > 0}
        sems = self.sems
        for eng in _ENGINES:
            def thunk(e, snapshot=snapshot):
                for oq, v in snapshot.items():
                    e.wait_ge(sems[oq], v)
            self.ethunks[eng].append(thunk)
        for k in self.skeys:
            clk = self.clocks[k]
            for oq, v in snapshot.items():
                if v > clk.get(oq, 0):
                    clk[oq] = v

    def dma(self, q, out, in_, sem_hint=None, **kw):
        assert q in ("qsp", "qact", "qpool")
        return self.emit(q, lambda e: e.dma_start(out=out, in_=in_, **kw),
                         reads=[in_], writes=[out], sem_hint=sem_hint)

    def mm(self, out, lhsT, rhs, start=True, stop=True, inc=None, **kw):
        if inc is None:
            inc = True
        return self.emit("pe", lambda e: e.matmul(out, lhsT, rhs, start=start, stop=stop, **kw),
                         reads=[lhsT, rhs], writes=[out], inc=inc)

    def mm_acc(self, out, pairs, **kw):
        n = len(pairs)
        for i, (l, r) in enumerate(pairs):
            self.mm(out, l, r, start=(i == 0), stop=(i == n - 1), inc=(i == n - 1), **kw)

    def transpose(self, out, in_, identity):
        return self.emit("pe", lambda e: e.transpose(out, in_, identity),
                         reads=[in_, identity], writes=[out])

    def act(self, out, in_, func, bias=0.0, scale=1.0, accum_out=None):
        reads = [in_] + [a for a in (bias, scale) if hasattr(a, "tensor")]
        writes = [out] + ([accum_out] if accum_out is not None else [])
        return self.emit("act", lambda e: e.activation(out, in_, func, bias=bias, scale=scale,
                                                       accum_out=accum_out),
                         reads=reads, writes=writes)

    def _vecop(self, q, name, out, ins, kwargs=None):
        kwargs = kwargs or {}
        def fn(e, name=name, out=out, ins=ins, kwargs=kwargs):
            return getattr(e, name)(out, *ins, **kwargs)
        reads = [a for a in ins if hasattr(a, "tensor")]
        return self.emit(q, fn, reads=reads, writes=[out])

    def copy(self, out, in_, q="dve"):
        return self._vecop(q, "tensor_copy", out, [in_])

    def tt(self, out, a, b, op, q="dve"):
        return self._vecop(q, "tensor_tensor", out, [a, b], kwargs={"op": op})

    def ts(self, out, in_, s1, s2=None, op0=ALU.mult, op1=ALU.bypass, q="dve"):
        def fn(e):
            return e.tensor_scalar(out, in_, s1, s2, op0, op1)
        reads = [in_] + [s for s in (s1, s2) if hasattr(s, "tensor")]
        return self.emit(q, fn, reads=reads, writes=[out])

    def stt(self, out, in0, scalar, in1, op0, op1, q="dve"):
        def fn(e):
            return e.scalar_tensor_tensor(out, in0, scalar, in1, op0, op1)
        reads = [in0, in1] + ([scalar] if hasattr(scalar, "tensor") else [])
        return self.emit(q, fn, reads=reads, writes=[out])

    def reduce(self, out, in_, op, q="dve", negate=None):
        kw = {"axis": AX.X, "op": op}
        if negate is not None:
            kw["negate"] = negate
        return self._vecop(q, "tensor_reduce", out, [in_], kwargs=kw)

    def memset(self, ap, val, q="dve"):
        return self.emit(q, lambda e: e.memset(ap, val), reads=[], writes=[ap])

    def select(self, out, mask, on_true, on_false):
        def fn(e):
            return e.select(out, mask, on_true, on_false)
        reads = [a for a in (mask, on_true, on_false) if hasattr(a, "tensor")]
        return self.emit("dve", fn, reads=reads, writes=[out])

    def reciprocal(self, out, in_):
        return self._vecop("dve", "reciprocal", out, [in_])

    def scan(self, out, d0, d1, initial, op0, op1, q="dve"):
        def fn(e):
            return e.tensor_tensor_scan(out, d0, d1, initial, op0, op1)
        reads = [a for a in (d0, d1, initial) if hasattr(a, "tensor")]
        return self.emit(q, fn, reads=reads, writes=[out])

    def iota(self, out, pattern, base=None, channel_multiplier=None):
        kw = {}
        if base is not None:
            kw["base"] = base
        if channel_multiplier is not None:
            kw["channel_multiplier"] = channel_multiplier
        return self.emit("pool", lambda e: e.iota(out, pattern, **kw),
                         reads=[], writes=[out])

    def flush(self):
        assert not self._flushed
        self._flushed = True
        nc = self.nc
        counts = dict(self.counts)
        sems = self.sems
        with nc.Block() as block:
            for eng_name in _ENGINES:
                thunks = self.ethunks[eng_name]
                if eng_name == "sync":
                    def final_sync(eng, thunks=thunks, counts=counts):
                        for th in thunks:
                            th(eng)
                        for q, v in counts.items():
                            if v > 0:
                                eng.wait_ge(sems[q], v)
                    block.sync(final_sync)
                else:
                    if not thunks:
                        continue
                    def body(eng, thunks=thunks):
                        for th in thunks:
                            th(eng)
                    getattr(block, eng_name)(body)
        return self


"""Sublayer launch builders for the macaron megakernel (per-core, feature-major).

Layout conventions (per core):
  - Residual stream in HBM: xT [1024, T] fp32 (feature-major; host transposes).
  - SBUF activation tiles: [128 partitions (feature chunk), cols].
  - Weights fed from host already transposed to [contraction, out] where needed.
  - bf16 matmuls: lhsT [128, M<=128] base 0, rhs [128, N<=512], PSUM fp32.
  - fp32 matmuls: K<=64; halves accumulated in separate base-0 / base-64 PSUM
    groups, then combined on DVE.
"""
import numpy as np
import ml_dtypes
from contextlib import ExitStack

import concourse.bass as bass
import concourse.mybir as mybir

D = 1024
NCH = 8
EPS = 1e-5


def bf(x):
    return np.asarray(x, dtype=np.float32).astype(ml_dtypes.bfloat16)


class Common:
    def __init__(self, p: Prog, T, TT=512):
        self.p = p
        self.T = T
        self.TT = TT
        self.nt = T // TT
        nc = p.nc
        self.ones_col_f = p.sbuf("ones_col_f", [128, 1], F32)
        self.ones_row_f = p.sbuf("ones_row_f", [1, 128], F32)
        ones_col_d = nc.inline_tensor(np.ones((128, 1), np.float32), "ones_col_d")
        ones_row_d = nc.inline_tensor(np.ones((1, 128), np.float32), "ones_row_d")
        p.dma("qsp", self.ones_col_f[:], ones_col_d.ap())
        p.dma("qsp", self.ones_row_f[:], ones_row_d.ap())
        self.ps = [p.psum(f"psb{i}", [128, 512], F32) for i in range(7)]

    def mm_f32(self, out, pos_pairs, neg_pairs=(), n=512, accumulate=False):
        p = self.p
        psA, psB, psC, psD = self.ps[0], self.ps[1], self.ps[2], self.ps[3]
        for i, (l, r) in enumerate(pos_pairs):
            lp = _half(l, 0); rp = _half(r, 0)
            p.mm(psA[:, :n], lp, rp, start=(i == 0), stop=(i == len(pos_pairs) - 1))
        for i, (l, r) in enumerate(pos_pairs):
            p.mm(psB[:, :n], _half(l, 1), _half(r, 1), start=(i == 0),
                 stop=(i == len(pos_pairs) - 1))
        for i, (l, r) in enumerate(neg_pairs):
            p.mm(psC[:, :n], _half(l, 0), _half(r, 0), start=(i == 0),
                 stop=(i == len(neg_pairs) - 1))
        for i, (l, r) in enumerate(neg_pairs):
            p.mm(psD[:, :n], _half(l, 1), _half(r, 1), start=(i == 0),
                 stop=(i == len(neg_pairs) - 1))
        if accumulate:
            p.tt(out, out, psA[:out.ap[0][1], :n], ALU.add)
        else:
            p.copy(out, psA[:out.ap[0][1], :n])
        p.tt(out, out, psB[:out.ap[0][1], :n], ALU.add)
        if neg_pairs:
            p.tt(out, out, psC[:out.ap[0][1], :n], ALU.subtract)
            p.tt(out, out, psD[:out.ap[0][1], :n], ALU.subtract)

    def layernorm_tile(self, xs, t0, cols, xn_out, sq_ws, rows_ws, out_off=None,
                       f32_out=None):
        p = self.p
        o0 = t0 if out_off is None else out_off
        psA, psB, sqA, sqB, bc_r, bc_m = (self.ps[i] for i in range(6))
        for c in range(NCH):
            x_t = xs[c][:, t0:t0 + cols]
            p.mm(psA[:1, :cols], self.ones_col_f[0:64, :], xs[c][0:64, t0:t0 + cols],
                 start=(c == 0), stop=(c == NCH - 1))
            p.mm(psB[:1, :cols], self.ones_col_f[64:128, :], xs[c][64:128, t0:t0 + cols],
                 start=(c == 0), stop=(c == NCH - 1))
            p.act(sq_ws[c][:, :cols], x_t, AF.Square)
            p.mm(sqA[:1, :cols], self.ones_col_f[0:64, :], sq_ws[c][0:64, :cols],
                 start=(c == 0), stop=(c == NCH - 1))
            p.mm(sqB[:1, :cols], self.ones_col_f[64:128, :], sq_ws[c][64:128, :cols],
                 start=(c == 0), stop=(c == NCH - 1))
        mean = rows_ws[:, 0:cols]
        s2 = rows_ws[:, cols:2 * cols]
        var = rows_ws[:, 2 * cols:3 * cols]
        rstd = rows_ws[:, 3 * cols:4 * cols]
        mr = rows_ws[:, 4 * cols:5 * cols]
        tmpr = rows_ws[:, 5 * cols:6 * cols]
        p.copy(mean, psA[:1, :cols])
        p.tt(mean, mean, psB[:1, :cols], ALU.add)
        p.ts(mean, mean, 1.0 / D, None, ALU.mult)
        p.copy(s2, sqA[:1, :cols])
        p.tt(s2, s2, sqB[:1, :cols], ALU.add)
        p.tt(tmpr, mean, mean, ALU.mult)
        p.ts(s2, s2, 1.0 / D, None, ALU.mult)
        p.tt(var, s2, tmpr, ALU.subtract)
        p.ts(var, var, EPS, None, ALU.add)
        p.act(tmpr, var, AF.Sqrt)
        p.reciprocal(rstd, tmpr)
        p.tt(mr, mean, rstd, ALU.mult)
        p.mm(bc_r[:, :cols], self.ones_row_f[:1, :], rstd)
        p.mm(bc_m[:, :cols], self.ones_row_f[:1, :], mr)
        for c in range(NCH):
            x_t = xs[c][:, t0:t0 + cols]
            p.tt(sq_ws[c][:, :cols], x_t, bc_r[:, :cols], ALU.mult)
            if f32_out is None:
                p.tt(xn_out[c][:, o0:o0 + cols], sq_ws[c][:, :cols], bc_m[:, :cols],
                     ALU.subtract)
            else:
                p.tt(f32_out[c][:, :cols], sq_ws[c][:, :cols], bc_m[:, :cols],
                     ALU.subtract)
                p.copy(xn_out[c][:, o0:o0 + cols], f32_out[c][:, :cols])

    def load_w_bf(self, name, w_hbm, rows, cols, q="qpool"):
        p = self.p
        tiles = []
        for c in range(rows // 128):
            t = p.sbuf(f"{name}_{c}", [128, cols], BF16)
            p.dma(q, t[:], w_hbm[c * 128:(c + 1) * 128, :], sem_hint=name)
            tiles.append(t)
        return tiles

    def load_x(self, xT):
        p = self.p
        xs = [p.sbuf(f"xs{c}", [128, self.T], F32) for c in range(NCH)]
        for c in range(NCH):
            p.dma("qsp", xs[c][:], xT[c * 128:(c + 1) * 128, :], sem_hint="x")
        return xs

    def store_x(self, xs, yT):
        p = self.p
        for c in range(NCH):
            p.dma("qsp", yT[c * 128:(c + 1) * 128, :], xs[c][:], sem_hint="y")


def _half(ap, h):
    return ap[h * 64:(h + 1) * 64, :]



L0FFN_INS = {"wr": [D, D], "wi": [D, D], "biasT": [128, NCH],
             "dftC": [D, D], "dftS": [D, D]}


def build_l0ffn(T=4096, TT=512):
    nc = bass.Bass("TRN2", target_bir_lowering=False, debug=False)
    ins = {k: nc.dram_tensor(k, s, F32, kind="ExternalInput").ap()
           for k, s in L0FFN_INS.items()}
    xT = nc.dram_tensor("xT", [D, T], F32, kind="ExternalInput").ap()
    yT = nc.dram_tensor("yT", [D, T], F32, kind="ExternalOutput").ap()
    with ExitStack() as stack:
        p = Prog(nc, stack)
        cm = Common(p, T, TT)
        emit_l0ffn(p, cm, ins, xT, yT, T, TT)
        p.flush()
    return nc


def emit_l0ffn(p, cm, ins, xT, yT, T, TT):
    nc = p.nc
    wr, wi, biasT = ins["wr"], ins["wi"], ins["biasT"]
    dftC, dftS = ins["dftC"], ins["dftS"]
    if True:
        rem = [p.sbuf(f"rem{c}", [128, D], BF16) for c in range(NCH)]
        bias_s = p.sbuf("bias_s", [128, NCH], F32)
        p.dma("qsp", bias_s[:], biasT)
        p.ts(bias_s[:], bias_s[:], 0.5, None, ALU.mult)

        with ExitStack() as ph0:
            def sb(name, shape, dt=F32):
                return p.sbuf(name, shape, dt, stack=ph0)
            wr_s = [sb(f"wr{c}", [128, D]) for c in range(NCH)]
            wi_s = [sb(f"wi{c}", [128, D]) for c in range(NCH)]
            C_s = [sb(f"C{c}", [128, D]) for c in range(NCH)]
            S_s = [sb(f"S{c}", [128, D]) for c in range(NCH)]
            Pr_s = [sb(f"Pr{c}", [128, 512]) for c in range(NCH)]
            Pi_s = [sb(f"Pi{c}", [128, 512]) for c in range(NCH)]
            ws_comb = sb("ws_comb", [128, 512])
            for c in range(NCH):
                p.dma("qsp", wr_s[c][:], wr[c * 128:(c + 1) * 128, :], sem_hint="wr")
                p.dma("qsp", wi_s[c][:], wi[c * 128:(c + 1) * 128, :], sem_hint="wi")
                p.dma("qact", C_s[c][:], dftC[c * 128:(c + 1) * 128, :], sem_hint="C")
                p.dma("qact", S_s[c][:], dftS[c * 128:(c + 1) * 128, :], sem_hint="S")
            for bt in range(D // 512):
                b0 = bt * 512
                for m in range(NCH):
                    mc = slice(m * 128, (m + 1) * 128)
                    cm.mm_f32(Pr_s[m][:, :],
                              [(wr_s[f][:, mc], C_s[f][:, b0:b0 + 512]) for f in range(NCH)]
                              + [(wi_s[f][:, mc], S_s[f][:, b0:b0 + 512]) for f in range(NCH)])
                    cm.mm_f32(Pi_s[m][:, :],
                              [(wi_s[f][:, mc], C_s[f][:, b0:b0 + 512]) for f in range(NCH)],
                              [(wr_s[f][:, mc], S_s[f][:, b0:b0 + 512]) for f in range(NCH)])
                for bc4 in range(4):
                    bcs = slice(bc4 * 128, (bc4 + 1) * 128)
                    bchunk = bt * 4 + bc4
                    for at in range(2):
                        a0 = at * 512
                        psA, psB, psC, psD = cm.ps[0], cm.ps[1], cm.ps[2], cm.ps[3]
                        for i in range(NCH):
                            p.mm(psA[:, :], _half(Pr_s[i][:, bcs], 0),
                                 _half(C_s[i][:, a0:a0 + 512], 0),
                                 start=(i == 0), stop=(i == NCH - 1))
                        for i in range(NCH):
                            p.mm(psB[:, :], _half(Pr_s[i][:, bcs], 1),
                                 _half(C_s[i][:, a0:a0 + 512], 1),
                                 start=(i == 0), stop=(i == NCH - 1))
                        for i in range(NCH):
                            p.mm(psC[:, :], _half(Pi_s[i][:, bcs], 0),
                                 _half(S_s[i][:, a0:a0 + 512], 0),
                                 start=(i == 0), stop=(i == NCH - 1))
                        for i in range(NCH):
                            p.mm(psD[:, :], _half(Pi_s[i][:, bcs], 1),
                                 _half(S_s[i][:, a0:a0 + 512], 1),
                                 start=(i == 0), stop=(i == NCH - 1))
                        ws = ws_comb
                        p.copy(ws[:, :], psA[:, :])
                        p.tt(ws[:, :], ws[:, :], psB[:, :], ALU.add)
                        p.tt(ws[:, :], ws[:, :], psC[:, :], ALU.subtract)
                        p.tt(ws[:, :], ws[:, :], psD[:, :], ALU.subtract)
                        p.ts(rem[bchunk][:, a0:a0 + 512], ws[:, :], 1.0 / D, None, ALU.mult)
            p.barrier()

        xs = cm.load_x(xT)
        xn = [p.sbuf(f"xn{c}", [128, TT], BF16) for c in range(NCH)]
        sq_ws = [p.sbuf(f"sq_ws{c}", [128, TT], F32) for c in range(NCH)]
        rows_ws = p.sbuf("rows_ws", [1, 6 * TT], F32)
        yw = p.sbuf("yw", [128, TT], F32)
        for t in range(T // TT):
            t0 = t * TT
            cm.layernorm_tile(xs, t0, TT, xn, sq_ws, rows_ws, out_off=0)
            for m in range(NCH):
                mc = slice(m * 128, (m + 1) * 128)
                ps = cm.ps[6]
                for c in range(NCH):
                    p.mm(ps[:, :], rem[c][:, mc], xn[c][:, :], start=(c == 0),
                         stop=(c == NCH - 1))
                p.ts(yw[:, :], ps[:, :], 0.5, bias_s[:, m:m + 1], ALU.mult, ALU.add)
                p.tt(xs[m][:, t0:t0 + TT], xs[m][:, t0:t0 + TT], yw[:, :], ALU.add)
        cm.store_x(xs, yT)
        p.barrier()



H = 16
DH = 64


def _ln_tile_v2(p, cm, xstage, xb, sq, TT, xn, ones_col_bf, rows_ws, from_bf=False,
                f32_out=None):
    psA, sqA, bc_r, bc_m = cm.ps[0], cm.ps[2], cm.ps[4], cm.ps[5]
    for c in range(NCH):
        if not from_bf:
            p.copy(xb[c][:, :TT], xstage[c][:, :TT])
        p.act(sq[c][:, :TT], xb[c][:, :TT] if from_bf else xstage[c][:, :TT],
              AF.Square)
        p.mm(psA[:1, :TT], ones_col_bf[:, 0:1], xb[c][:, :TT], start=(c == 0),
             stop=(c == NCH - 1))
        p.mm(sqA[:1, :TT], ones_col_bf[:, 0:1], sq[c][:, :TT], start=(c == 0),
             stop=(c == NCH - 1))
    mean = rows_ws[:, 0:TT]
    var = rows_ws[:, 2 * TT:3 * TT]
    rstd = rows_ws[:, 3 * TT:4 * TT]
    mr = rows_ws[:, 4 * TT:5 * TT]
    tmpr = rows_ws[:, 5 * TT:6 * TT]
    p.ts(mean, psA[:1, :TT], 1.0 / D, None, ALU.mult)
    p.ts(var, sqA[:1, :TT], 1.0 / D, None, ALU.mult)
    p.tt(tmpr, mean, mean, ALU.mult)
    p.tt(var, var, tmpr, ALU.subtract)
    p.ts(var, var, EPS, None, ALU.add)
    p.act(tmpr, var, AF.Sqrt)
    p.reciprocal(rstd, tmpr)
    p.tt(mr, mean, rstd, ALU.mult)
    p.mm(bc_r[:, :TT], cm.ones_row_f[:1, :], rstd)
    p.mm(bc_m[:, :TT], cm.ones_row_f[:1, :], mr)
    for c in range(NCH):
        srcc = xb[c] if from_bf else xstage[c]
        p.tt(sq[c][:, :TT], srcc[:, :TT], bc_r[:, :TT], ALU.mult)
        if f32_out is None:
            p.tt(xn[c][:, :TT], sq[c][:, :TT], bc_m[:, :TT], ALU.subtract)
        else:
            p.tt(f32_out[c][:, :TT], sq[c][:, :TT], bc_m[:, :TT], ALU.subtract)
            p.copy(xn[c][:, :TT], f32_out[c][:, :TT])


def attn_input_spec(kind, T):
    if kind == "l1":
        d = {nm: ([D, D], F32) for nm in ("WqT", "WkT", "WvT")}
        d["WpH"] = ([H, DH, D], F32)
        for nm in ("bqT", "bkT", "bvT", "bpT"):
            d[nm] = ([128, NCH], F32)
    elif kind == "l3":
        d = {nm: ([D, D], F32) for nm in ("Wq", "Wk", "Wv")}
        d["WoH"] = ([H, DH, D], F32)
        d["boT"] = ([128, NCH], F32)
        d["ropeC"] = ([128, T], BF16)
        d["ropeS"] = ([128, T], BF16)
        d["rperm"] = ([128, 128], F32)
        d["cmask"] = ([128, 4 * 512], F32)
    else:
        d = {nm: ([4, 256, 256], F32) for nm in ("qW", "kW", "vW", "oW")}
        for nm in ("qbE", "kbE", "vbE", "obE"):
            d[nm] = ([128, 8], F32)
        d["J"] = ([1, D], F32)
    return d


def build_attn(kind, T=4096, TT=256):
    assert kind in ("l0", "l1", "l3")
    nc = bass.Bass("TRN2", target_bir_lowering=False, debug=False)
    ins = {k: nc.dram_tensor(k, s, dt, kind="ExternalInput").ap()
           for k, (s, dt) in attn_input_spec(kind, T).items()}
    xT = nc.dram_tensor("xT", [D, T], F32, kind="ExternalInput").ap()
    yT = nc.dram_tensor("yT", [D, T], F32, kind="ExternalOutput").ap()
    with ExitStack() as stack:
        p = Prog(nc, stack)
        cm = Common(p, T, TT)
        emit_attn(p, cm, kind, ins, xT, yT, T, TT)
        p.flush()
    return nc


def emit_attn(p, cm, kind, ins, xT, yT, T, TT):
    nc = p.nc
    qsp = p.dram_i("qsp", [H, DH, T], BF16)
    ksp = p.dram_i("ksp", [H, DH, T], BF16)
    nt = T // TT
    scale = DH ** -0.5
    if True:
        ones_col_bf = p.sbuf("ones_col_bf", [128, 1], BF16)
        ones_sq_bf = p.sbuf("ones_sq_bf", [1, 128], BF16)
        ident_bf = p.sbuf("ident_bf", [128, 128], BF16)
        idf_d = p.inline(np.eye(128, dtype=np.float32).astype(ml_dtypes.bfloat16), "idf_d")
        onesb_d = p.inline(np.ones((1, 128), np.float32).astype(ml_dtypes.bfloat16), "onesb_d")
        onesc_d = p.inline(np.ones((128, 1), np.float32).astype(ml_dtypes.bfloat16), "onesc_d")
        p.dma("qsp", ident_bf[:], idf_d.ap())
        p.dma("qsp", ones_sq_bf[:], onesb_d.ap())
        p.dma("qsp", ones_col_bf[:], onesc_d.ap())

        ve = [p.sbuf(f"ve{h}", [128, T // 128, 65], BF16) for h in range(H)]
        for h in range(H):
            p.memset(ve[h][:, :, 64:65], 1.0)

        bias_cols = {}
        alpha_cols = {}
        if kind == "l1":
            for nm in ("Wq", "Wk", "Wv", "Wp"):
                alpha_cols[nm] = p.sbuf(f"{nm}_acol", [128, NCH], F32)
            for nm in ("bq", "bk", "bv", "bp"):
                bias_cols[nm] = p.sbuf(f"{nm}_col", [128, NCH], F32)
        elif kind == "l3":
            bias_cols["bo"] = p.sbuf("bo_col", [128, NCH], F32)
        else:
            for nm in ("bq", "bk", "bv", "bo"):
                bias_cols[nm] = p.sbuf(f"{nm}_eff", [128, NCH], F32)
        if kind == "l0":
            cj = p.sbuf("cj", [1, 2], F32)


        with ExitStack() as phW:
            def sbW(name, shape, dt=F32):
                return p.sbuf(name, shape, dt, stack=phW)

            if kind == "l1":
                stage = sbW("stageW", [128, D])
                absb = sbW("absW", [128, D])
                arow = sbW("arow", [1, D])
                stage64 = sbW("stage64", [64, D])
                Ws = {}
                psA, psB, pst = cm.ps[0], cm.ps[1], cm.ps[2]
                for nm, hb in (("Wq", "WqT"), ("Wk", "WkT"), ("Wv", "WvT")):
                    tiles = [sbW(f"{nm}s{c}", [128, D], BF16) for c in range(NCH)]
                    Ws[nm] = tiles
                    for c in range(NCH):
                        p.dma("qsp", stage[:], ins[hb][c * 128:(c + 1) * 128, :],
                              sem_hint="wstage")
                        p.act(tiles[c][:], stage[:], AF.Sign)
                        p.act(absb[:], stage[:], AF.Abs)
                        for hv in range(2):
                            sl = slice(hv * 512, hv * 512 + 512)
                            p.mm(psA[:1, sl] if hv == 0 else psB[:1, :512],
                                 cm.ones_col_f[0:64, :], absb[0:64, sl],
                                 start=(c == 0), stop=(c == NCH - 1))
                            p.mm(pst[:1, :512] if hv == 0 else cm.ps[3][:1, :512],
                                 cm.ones_col_f[64:128, :], absb[64:128, sl],
                                 start=(c == 0), stop=(c == NCH - 1))
                    p.copy(arow[:, 0:512], psA[:1, 0:512])
                    p.tt(arow[:, 0:512], arow[:, 0:512], pst[:1, :512], ALU.add)
                    p.copy(arow[:, 512:1024], psB[:1, :512])
                    p.tt(arow[:, 512:1024], arow[:, 512:1024], cm.ps[3][:1, :512], ALU.add)
                    p.ts(arow[:], arow[:], 1.0 / D, None, ALU.mult)
                    acol = alpha_cols[nm]
                    for c in range(NCH):
                        p.transpose(pst[:, 0:1], arow[:1, c * 128:(c + 1) * 128],
                                    cm.ones_row_f[:1, 0:1])
                        p.copy(acol[:, c:c + 1], pst[:, 0:1])
                for nm, hb in (("bq", "bqT"), ("bk", "bkT"), ("bv", "bvT"), ("bp", "bpT")):
                    p.dma("qsp", bias_cols[nm][:], ins[hb])
            elif kind == "l3":
                def ld(nm):
                    tiles = [sbW(f"{nm}s{c}", [128, D], BF16) for c in range(NCH)]
                    for c in range(NCH):
                        p.dma("qpool", tiles[c][:], ins[nm][c * 128:(c + 1) * 128, :],
                              sem_hint=nm)
                    return tiles
                Ws = {"Wq": ld("Wq"), "Wk": ld("Wk"), "Wv": ld("Wv")}
                p.dma("qsp", bias_cols["bo"][:], ins["boT"])
                ropeC = sbW("ropeC_s", [128, T], BF16)
                ropeS = sbW("ropeS_s", [128, T], BF16)
                p.dma("qact", ropeC[:], ins["ropeC"], sem_hint="rc")
                p.dma("qact", ropeS[:], ins["ropeS"], sem_hint="rs")
                rperm = sbW("rperm_s", [128, 128], BF16)
                rp_st = sbW("rp_st", [128, 128])
                p.dma("qsp", rp_st[:], ins["rperm"])
                p.copy(rperm[:], rp_st[:])
            else:
                QSIGN = {
                    0: [(0, 1), (1, -1), (2, -1), (3, -1)],
                    1: [(1, 1), (0, 1), (3, -1), (2, 1)],
                    2: [(2, 1), (3, 1), (0, 1), (1, -1)],
                    3: [(3, 1), (2, -1), (1, 1), (0, 1)],
                }
                BSIGN = {0: [1, -1, -1, -1], 1: [1, 1, 1, -1],
                         2: [1, -1, 1, 1], 3: [1, 1, -1, 1]}
                stage = sbW("stageQ", [128, 256])
                stage64 = sbW("stageQ64", [64, 256])
                Ws = {}
                for nm, hb in (("Wq", "qW"), ("Wk", "kW"), ("Wv", "vW")):
                    tiles = [sbW(f"{nm}s{ch}", [128, D], BF16) for ch in range(NCH)]
                    Ws[nm] = tiles
                    for cin in range(4):
                        for hf in range(2):
                            ch = 2 * cin + hf
                            for comp in range(4):
                                w, sg = QSIGN[comp][cin]
                                p.dma("qsp", stage[:],
                                      ins[hb][w, 128 * hf:128 * hf + 128, :],
                                      sem_hint="qw")
                                p.ts(tiles[ch][:, 256 * comp:256 * comp + 256],
                                     stage[:], float(sg), None, ALU.mult)
                for nm, hb in (("bq", "qbE"), ("bk", "kbE"), ("bv", "vbE"), ("bo", "obE")):
                    raw = sbW(f"{nm}_raw", [128, 8])
                    p.dma("qsp", raw[:], ins[hb])
                    eff = bias_cols[nm]
                    for comp in range(4):
                        for hf in range(2):
                            ch = comp * 2 + hf
                            s = BSIGN[comp]
                            p.ts(eff[:, ch:ch + 1], raw[:, hf:hf + 1],
                                 float(s[0]), None, ALU.mult)
                            for w in range(1, 4):
                                p.stt(eff[:, ch:ch + 1], raw[:, w * 2 + hf:w * 2 + hf + 1],
                                      float(s[w]), eff[:, ch:ch + 1], ALU.mult, ALU.add)
                    bias_cols[nm] = eff
                jrow = sbW("jrow", [1, D])
                jsin = sbW("jsin", [1, D])
                p.dma("qsp", jrow[:], ins["J"])
                p.act(jsin[:], jrow[:], AF.Sin)
                p.reduce(cj[:, 0:1], jsin[:], ALU.add)
                p.ts(cj[:, 0:1], cj[:, 0:1], 0.1 / D, None, ALU.mult)

            from_bf = (kind == "l0")
            if not from_bf:
                xstage = [sbW(f"xa{c}", [128, TT]) for c in range(NCH)]
            else:
                xstage = [sbW("xa_sh", [128, TT])]* NCH
            xbA = [sbW(f"xab{c}", [128, TT], BF16) for c in range(NCH)]
            sqw = [sbW(f"sqa{c}", [128, TT], BF16) for c in range(NCH)]
            xn = [sbW(f"xna{c}", [128, TT], BF16) for c in range(NCH)]
            rows_ws = sbW("rows_a", [1, 6 * TT])
            if kind == "l0":
                qf = [sbW(f"qf{c}", [128, TT]) for c in range(NCH)]
                kf = [sbW(f"kf{c}", [128, TT]) for c in range(NCH)]
                n2 = [sbW(f"n2_{i}", [128, TT]) for i in range(2)]
                rn = [sbW(f"rn_{i}", [128, TT]) for i in range(2)]
                rnq = [sbW(f"rnq_{i}", [128, TT]) for i in range(2)]
            else:
                qf = kf = None
            if kind == "l3":
                tr1 = sbW("tr1", [128, TT])
                tr2 = sbW("tr2", [128, TT])
            qb_t = [sbW(f"qbt{c}", [128, TT], BF16) for c in range(NCH)]
            kb_t = [sbW(f"kbt{c}", [128, TT], BF16) for c in range(NCH)]
            vb_t = [sbW(f"vbt{c}", [128, TT], BF16) for c in range(NCH)]
            ps_tr = p.psum("pstr_a", [128, 512], BF16)

            for t in range(nt):
                t0 = t * TT
                if from_bf:
                    for c in range(NCH):
                        p.dma("qpool", xbA[c][:], xT[c * 128:(c + 1) * 128, t0:t0 + TT],
                              sem_hint=f"xa{c % 4}")
                else:
                    for c in range(NCH):
                        p.dma("qsp", xstage[c][:], xT[c * 128:(c + 1) * 128, t0:t0 + TT],
                              sem_hint=f"xa{c % 4}")
                _ln_tile_v2(p, cm, xstage, xbA, sqw, TT, xn, ones_col_bf, rows_ws,
                            from_bf=from_bf)
                for (dst_f, dst_b, wname, bname) in (
                        (qf, qb_t, "Wq", "bq"), (kf, kb_t, "Wk", "bk"),
                        (None, vb_t, "Wv", "bv")):
                    W = Ws[wname]
                    for m in range(NCH):
                        ps = cm.ps[6]
                        for c in range(NCH):
                            p.mm(ps[:, :TT], W[c][:, m * 128:(m + 1) * 128],
                                 xn[c][:, :TT], start=(c == 0), stop=(c == NCH - 1))
                        if kind == "l1":
                            p.ts(dst_b[m][:, :TT], ps[:, :TT],
                                 alpha_cols[wname][:, m:m + 1],
                                 bias_cols[bname][:, m:m + 1], ALU.mult, ALU.add)
                        elif kind == "l3":
                            sc0 = scale if wname == "Wq" else 1.0
                            p.ts(dst_b[m][:, :TT], ps[:, :TT], sc0, None, ALU.mult)
                        else:
                            tgt = dst_b[m] if dst_f is None else dst_f[m]
                            p.ts(tgt[:, :TT], ps[:, :TT], 1.0,
                                 bias_cols[bname][:, m:m + 1], ALU.mult, ALU.add)
                if kind == "l0":
                    for (src, dst, do_scale) in ((qf, qb_t, True), (kf, kb_t, False)):
                        for i in range(2):
                            p.act(n2[i][:, :TT], src[i][:, :TT], AF.Square)
                            for cc in (i + 2, i + 4, i + 6):
                                p.act(rn[i][:, :TT], src[cc][:, :TT], AF.Square)
                                p.tt(n2[i][:, :TT], n2[i][:, :TT], rn[i][:, :TT], ALU.add)
                            p.ts(n2[i][:, :TT], n2[i][:, :TT], 1e-6, None, ALU.add)
                            p.act(rn[i][:, :TT], n2[i][:, :TT], AF.Sqrt)
                            p.reciprocal(rn[i][:, :TT], rn[i][:, :TT])
                            if do_scale:
                                p.ts(rnq[i][:, :TT], rn[i][:, :TT], scale, None, ALU.mult)
                        rr = rnq if do_scale else rn
                        for c in range(NCH):
                            p.tt(dst[c][:, :TT], src[c][:, :TT], rr[c % 2][:, :TT],
                                 ALU.mult)
                elif kind == "l3":
                    for dst in (qb_t, kb_t):
                        for c in range(NCH):
                            psr = cm.ps[6]
                            p.mm(psr[:, :TT], rperm[:, :], dst[c][:, :TT])
                            p.tt(tr1[:, :TT], dst[c][:, :TT], ropeC[:, t0:t0 + TT],
                                 ALU.mult)
                            p.tt(tr2[:, :TT], psr[:, :TT], ropeS[:, t0:t0 + TT],
                                 ALU.mult)
                            p.tt(dst[c][:, :TT], tr1[:, :TT], tr2[:, :TT], ALU.add)
                else:
                    for c in range(NCH):
                        p.ts(qb_t[c][:, :TT], qb_t[c][:, :TT], scale, None, ALU.mult)
                for h in range(H):
                    ch, b = divmod(h, 2)
                    p.dma("qact", qsp[h, :, t0:t0 + TT],
                          qb_t[ch][b * 64:(b + 1) * 64, :TT], sem_hint=f"qs{h % 4}")
                    p.dma("qpool", ksp[h, :, t0:t0 + TT],
                          kb_t[ch][b * 64:(b + 1) * 64, :TT], sem_hint=f"ks{h % 4}")
                for c in range(NCH):
                    for j in range(TT // 128):
                        p.transpose(ps_tr[:, 0:128], vb_t[c][:, j * 128:(j + 1) * 128],
                                    ident_bf[:])
                        jg = (t0 + j * 128) // 128
                        p.copy(ve[2 * c][:, jg, 0:64], ps_tr[:, 0:64])
                        p.copy(ve[2 * c + 1][:, jg, 0:64], ps_tr[:, 64:128])
            p.barrier()

        WoH_t = [p.sbuf(f"WoH{h}", [64, D], BF16) for h in range(H)]
        if kind == "l3":
            cmask = p.sbuf("cmask_s", [128, 4 * 512], F32)
            p.dma("qact", cmask[:], ins["cmask"], sem_hint="cms")
            for h in range(H):
                p.dma("qpool", WoH_t[h][:], ins["WoH"][h], sem_hint="woh")
        elif kind == "l1":
            with ExitStack() as ph2:
                stage64b = p.sbuf("stage64b", [64, D], F32, stack=ph2)
                absb2 = p.sbuf("absb2", [64, D], F32, stack=ph2)
                arow2 = p.sbuf("arow2", [1, D], F32, stack=ph2)
                psA2, psB2, pst2 = cm.ps[0], cm.ps[1], cm.ps[2]
                for h in range(H):
                    p.dma("qsp", stage64b[:], ins["WpH"][h], sem_hint="wp64")
                    p.act(WoH_t[h][:], stage64b[:], AF.Sign)
                    p.act(absb2[:, :], stage64b[:], AF.Abs)
                    for hv in range(2):
                        sl = slice(hv * 512, hv * 512 + 512)
                        p.mm(psA2[:1, :512] if hv == 0 else psB2[:1, :512],
                             cm.ones_col_f[0:64, :], absb2[:, sl],
                             start=(h == 0), stop=(h == H - 1))
                p.ts(arow2[:, 0:512], psA2[:1, :512], 1.0 / D, None, ALU.mult)
                p.ts(arow2[:, 512:1024], psB2[:1, :512], 1.0 / D, None, ALU.mult)
                acol = alpha_cols["Wp"]
                for c in range(NCH):
                    p.transpose(pst2[:, 0:1], arow2[:1, c * 128:(c + 1) * 128],
                                cm.ones_row_f[:1, 0:1])
                    p.copy(acol[:, c:c + 1], pst2[:, 0:1])
                p.barrier()
        else:
            QSIGN0 = {
                0: [(0, 1), (1, -1), (2, -1), (3, -1)],
                1: [(1, 1), (0, 1), (3, -1), (2, 1)],
                2: [(2, 1), (3, 1), (0, 1), (1, -1)],
                3: [(3, 1), (2, -1), (1, 1), (0, 1)],
            }
            with ExitStack() as ph2:
                stage64b = p.sbuf("stage64b", [64, 256], F32, stack=ph2)
                for h in range(H):
                    cin, rest = divmod(h * 64, 256)
                    for comp in range(4):
                        w, sg = QSIGN0[comp][cin]
                        p.dma("qsp", stage64b[:], ins["oW"][w, rest:rest + 64, :],
                              sem_hint="qwo")
                        p.ts(WoH_t[h][:, 256 * comp:256 * comp + 256], stage64b[:],
                             float(sg), None, ALU.mult)
                p.barrier()

        ws_sq = p.sbuf("ws_sq", [64, 512], F32)
        maxk = p.sbuf("maxk", [1, H], F32)
        mrow = p.sbuf("mrow", [1, 64], F32)
        kbuf = [p.sbuf(f"kbuf{i}", [64, T], BF16) for i in range(2)]
        qbuf = [p.sbuf(f"qbuf{i}", [64, 512], BF16) for i in range(2)]
        qbias = p.sbuf("qbias", [1, 512], BF16)
        brow = p.sbuf("brow", [1, 512], F32)
        Pt = [p.sbuf(f"Pt{i}", [128, 512], BF16) for i in range(3)]
        Ot = [p.sbuf(f"Ot{h}", [64, 512], BF16) for h in range(H)]
        dbc = p.sbuf("dbc", [64, 512], F32)
        drow = p.sbuf("drow", [1, 512], F32)
        xe = p.sbuf("xe", [128, 512], F32)
        yw2 = p.sbuf("yw2", [128, 512], F32)
        psS = [cm.ps[0], cm.ps[1], cm.ps[5]]
        psO = cm.ps[2]
        psq2 = cm.ps[3]
        psb = cm.ps[4]
        psy = cm.ps[6]

        for h in range(H):
            p.dma("qsp", kbuf[h % 2][:, :], ksp[h], sem_hint=f"kb{h % 2}")
            for tt_ in range(T // 512):
                c0 = tt_ * 512
                p.act(ws_sq[:, :], kbuf[h % 2][:, c0:c0 + 512], AF.Square)
                p.mm(psq2[:1, :512], cm.ones_col_f[0:64, :], ws_sq[:, :],
                     start=True, stop=True)
                p.reduce(mrow[:, tt_:tt_ + 1], psq2[:1, :512], ALU.max)
            p.reduce(maxk[:, h:h + 1], mrow[:, 0:T // 512], ALU.max)
            p.act(maxk[:, h:h + 1], maxk[:, h:h + 1], AF.Sqrt)
            p.ts(maxk[:, h:h + 1], maxk[:, h:h + 1], -1.0, None, ALU.mult)

        nqt = T // 512
        ccell = p.sbuf("ccell", [1, 2], F32)
        ccol = p.sbuf("ccol_b", [128, 1], F32)
        for qt in range(nqt):
            q0 = qt * 512
            for h in range(H):
                p.dma("qsp", kbuf[h % 2][:, :], ksp[h], sem_hint=f"kb{h % 2}")
                p.dma("qact", qbuf[h % 2][:, :], qsp[h, :, q0:q0 + 512],
                      sem_hint=f"qb{h % 2}")
                p.act(ws_sq[:, :], qbuf[h % 2][:, :], AF.Square)
                p.mm(psq2[:1, :512], cm.ones_col_f[0:64, :], ws_sq[:, :],
                     start=True, stop=True)
                p.reduce(ccell[:, 0:1], psq2[:1, :512], ALU.max)
                p.act(ccell[:, 0:1], ccell[:, 0:1], AF.Sqrt)
                if kind == "l0":
                    p.ts(ccell[:, 0:1], ccell[:, 0:1], maxk[:, h:h + 1], cj[:, 0:1],
                         ALU.mult, ALU.add)
                else:
                    p.ts(ccell[:, 0:1], ccell[:, 0:1], maxk[:, h:h + 1], None,
                         ALU.mult)
                p.mm(psb[0:128, 0:1], cm.ones_row_f[:1, :], ccell[0:1, 0:1])
                p.copy(ccol[:, :], psb[0:128, 0:1])
                kj_hi = (q0 + 512) // 128 if kind == "l3" else T // 128

                def do_S(j):
                    p.mm(psS[j % 3][:, :], kbuf[h % 2][:, j * 128:(j + 1) * 128],
                         qbuf[h % 2][:, :], start=True, stop=True)
                    if kind == "l3" and j * 128 >= q0:
                        dj = (j * 128 - q0) // 128
                        p.tt(psS[j % 3][:, :], psS[j % 3][:, :],
                             cmask[:, dj * 512:(dj + 1) * 512], ALU.add)

                def do_exp(j):
                    p.act(Pt[j % 3][:, :], psS[j % 3][:, :], AF.Exp,
                          bias=ccol[:, 0:1])

                def do_pv(j):
                    p.mm(psO[0:65, :], ve[h][:, j, :], Pt[j % 3][:, :],
                         start=(j == 0), stop=(j == kj_hi - 1))

                for j in range(kj_hi + 2):
                    if j < kj_hi:
                        do_S(j)
                    if 1 <= j <= kj_hi:
                        do_exp(j - 1)
                    if j >= 2:
                        do_pv(j - 2)
                p.reciprocal(drow[:, :], psO[64:65, :])
                p.mm(psb[:, :], cm.ones_row_f[:1, :], drow[:, :])
                p.copy(dbc[:, :], psb[0:64, :])
                p.tt(Ot[h][:, :], psO[0:64, :], dbc[:, :], ALU.mult)
            for m in range(NCH):
                for h in range(H):
                    p.mm(psy[:, :], WoH_t[h][:, m * 128:(m + 1) * 128], Ot[h][:, :],
                         start=(h == 0), stop=(h == H - 1))
                p.dma("qsp", xe[:, :], xT[m * 128:(m + 1) * 128, q0:q0 + 512],
                      sem_hint="xe")
                if kind == "l1":
                    p.ts(yw2[:, :], psy[:, :], alpha_cols["Wp"][:, m:m + 1],
                         bias_cols["bp"][:, m:m + 1], ALU.mult, ALU.add)
                else:
                    p.ts(yw2[:, :], psy[:, :], 1.0, bias_cols["bo"][:, m:m + 1],
                         ALU.mult, ALU.add)
                p.tt(yw2[:, :], yw2[:, :], xe[:, :], ALU.add)
                p.dma("qact", yT[m * 128:(m + 1) * 128, q0:q0 + 512], yw2[:, :],
                      sem_hint="yo")
        p.barrier()



MOE_CAP = {"l2": 4096, "l3": 5120}


def moe_input_spec(kind, mode, T):
    E = 8
    d = {"router": ([D, E], F32)}
    if mode == "main":
        NH = 32 if kind != "l1" else 16
        d["w1e"] = ([E, D, 4096], F32)
        d["w2e"] = ([E, NH * 128, D], F32)
        if kind != "l1":
            ncnt = E if kind == "l2" else 2 * E
            d["b1T"] = ([E, 128, 32], F32)
            d["b2T"] = ([E, 128, 8], F32)
            d["cnts"] = ([8, ncnt], F32)
            d["coremask"] = ([8, 1], F32)
    return d


def build_moe(kind, mode, T=4096, TT=512):
    assert kind in ("l1", "l2", "l3")
    assert mode in ("cnt", "main")
    E = 8
    nc = bass.Bass("TRN2", target_bir_lowering=False, debug=False)
    spec = moe_input_spec(kind, mode, T)
    if mode == "cnt":
        spec.pop("cnts", None); spec.pop("coremask", None)
    ins = {k: nc.dram_tensor(k, s, dt, kind="ExternalInput").ap()
           for k, (s, dt) in spec.items()}
    xT = nc.dram_tensor("xT", [D, T], F32, kind="ExternalInput").ap()
    if mode == "main":
        yT = nc.dram_tensor("yT", [D, T], F32, kind="ExternalOutput").ap()
        cnt_out = None
    else:
        ncnt = E if kind == "l2" else 2 * E
        cnt_out = nc.dram_tensor("cnt", [ncnt, 1], F32, kind="ExternalOutput").ap()
        yT = None
    with ExitStack() as stack:
        p = Prog(nc, stack)
        cm = Common(p, T, TT)
        emit_moe(p, cm, kind, mode, ins, xT, yT, cnt_out, T, TT,
                 collective=False)
        p.flush()
    return nc


def emit_moe(p, cm, kind, mode, ins, xT, yT, cnt_out, T, TT, collective=False,
             replica_groups=None):
    nc = p.nc
    E = 8
    rtr = ins["router"]
    if mode == "main":
        w1d, w2d = ins["w1e"], ins["w2e"]
        if kind != "l1":
            b1d, b2d = ins["b1T"], ins["b2T"]
            ncnt = E if kind == "l2" else 2 * E
            cmaskd = ins["coremask"]
            if collective:
                cc_in = p.dram_i("cc_in", [ncnt, 1], F32)
                cc_out = p.dram_i("cc_out", [8 * ncnt, 1], F32)
            else:
                cntsd = ins["cnts"]
        xnsp = p.dram_i("xnsp", [D, T], BF16)
        accT = p.dram_i("accT", [D, T], F32)
        w8sp = p.dram_i("w8sp", [E, T], F32)
    else:
        ncnt = E if kind == "l2" else 2 * E

    nt = T // TT
    if True:
        ones_col_bf = p.sbuf("ones_col_bf", [128, 1], BF16)
        onesc_d = p.inline(np.ones((128, 1), np.float32).astype(ml_dtypes.bfloat16), "onesc_d")
        p.dma("qsp", ones_col_bf[:], onesc_d.ap())
        id2_d = p.inline(np.concatenate([np.eye(64, dtype=np.float32)] * 2, 0), "id2_d")
        id2 = p.sbuf("id2", [128, 64], F32)
        p.dma("qsp", id2[:], id2_d.ap())

        rtr_s = [p.sbuf(f"rtr{c}", [128, E], F32) for c in range(NCH)]
        for c in range(NCH):
            p.dma("qsp", rtr_s[c][:], rtr[c * 128:(c + 1) * 128, :], sem_hint="rtr")

        w8 = p.sbuf("w8", [E, T], F32)

        with ExitStack() as phR:
            def sbR(name, shape, dt=F32):
                return p.sbuf(name, shape, dt, stack=phR)
            xstage = [sbR(f"xr{c}", [128, TT]) for c in range(NCH)]
            xnf = [sbR(f"xnf{c}", [128, TT]) for c in range(NCH)]
            sqw32 = [sbR(f"sqr32{c}", [128, TT]) for c in range(NCH)]
            xn = [sbR(f"xnr{c}", [128, TT], BF16) for c in range(NCH)]
            rows_ws = sbR("rows_r", [1, 6 * TT])
            lg = sbR("lg", [128, E])
            col = sbR("colws", [128, 12])
            tm = sbR("tm", [128, 24])
            gA = sbR("gA", [E, T])
            gB = sbR("gB", [E, T]) if kind == "l3" else None
            gAg = sbR("gAg", [E, T]) if kind != "l1" else None
            gBg = sbR("gBg", [E, T]) if kind == "l3" else None
            pst = cm.ps[6]

            for t in range(nt):
                t0 = t * TT
                for c in range(NCH):
                    p.dma("qsp", xstage[c][:], xT[c * 128:(c + 1) * 128, t0:t0 + TT],
                          sem_hint=f"xr{c % 4}")
                cm.layernorm_tile(xstage, 0, TT, xn, sqw32, rows_ws, out_off=0,
                                  f32_out=xnf)
                if mode == "main":
                    for c in range(NCH):
                        p.dma("qact", xnsp[c * 128:(c + 1) * 128, t0:t0 + TT],
                              xn[c][:, :TT], sem_hint=f"xs{c % 4}")
                for ts_ in range(TT // 128):
                    tc = slice(ts_ * 128, (ts_ + 1) * 128)
                    psr, psr2 = cm.ps[0], cm.ps[1]
                    for c in range(NCH):
                        p.mm(psr[:, :E], xnf[c][0:64, tc], rtr_s[c][0:64, :],
                             start=(c == 0), stop=(c == NCH - 1))
                    for c in range(NCH):
                        p.mm(psr2[:, :E], xnf[c][64:128, tc], rtr_s[c][64:128, :],
                             start=(c == 0), stop=(c == NCH - 1))
                    p.copy(lg[:, :], psr[:, :E])
                    p.tt(lg[:, :], lg[:, :], psr2[:, :E], ALU.add)
                    m1 = col[:, 0:1]; m2 = col[:, 1:2]; dcol = col[:, 2:3]
                    s1 = col[:, 3:4]; s2 = col[:, 4:5]; neg1 = col[:, 5:6]
                    gc = col[:, 6:7]
                    A = tm[:, 0:E]; B = tm[:, 8:16]; Cc = tm[:, 16:24]

                    def tpose(colsl, dst):
                        for hf in range(2):
                            p.transpose(pst[0:E, 0:64], tm[hf * 64:(hf + 1) * 64, colsl],
                                        id2[hf * 64:(hf + 1) * 64, :])
                            p.copy(dst[0:E, t0 + ts_ * 128 + hf * 64:
                                       t0 + ts_ * 128 + (hf + 1) * 64],
                                   pst[0:E, 0:64])
                    p.reduce(m1, lg[:, :], ALU.max)
                    p.ts(A, lg[:, :], m1, None, ALU.is_equal)
                    if kind in ("l1", "l3"):
                        p.stt(B, A, -1e30, lg[:, :], ALU.mult, ALU.add)
                        p.reduce(m2, B, ALU.max)
                        p.ts(B, B, m2, None, ALU.is_equal)
                        p.tt(dcol, m1, m2, ALU.subtract)
                        p.act(s1, dcol, AF.Sigmoid)
                        p.ts(s2, s1, -1.0, 1.0, ALU.mult, ALU.add)
                        if kind == "l1":
                            p.ts(Cc, A, s1, None, ALU.mult)
                            p.stt(Cc, B, s2, Cc, ALU.mult, ALU.add)
                            tpose(slice(16, 24), gA)
                        else:
                            tpose(slice(0, E), gA)
                            tpose(slice(8, 16), gB)
                            p.ts(Cc, A, s1, None, ALU.mult)
                            tpose(slice(16, 24), gAg)
                            p.ts(Cc, B, s2, None, ALU.mult)
                            tpose(slice(16, 24), gBg)
                    else:
                        p.ts(neg1, m1, -1.0, None, ALU.mult)
                        p.act(B, lg[:, :], AF.Exp, bias=neg1)
                        p.reduce(gc, B, ALU.add)
                        p.reciprocal(gc, gc)
                        p.ts(B, A, gc, None, ALU.mult)
                        tpose(slice(0, E), gA)
                        tpose(slice(8, 16), gAg)
            if mode == "cnt":
                ccol = sbR("ccol", [16, 2])
                p.reduce(ccol[0:E, 0:1], gA[:, :], ALU.add)
                if kind == "l3":
                    p.reduce(ccol[0:E, 1:2], gB[:, :], ALU.add)
                    p.dma("qsp", cnt_out[0:E, :], ccol[0:E, 0:1])
                    p.dma("qsp", cnt_out[E:2 * E, :], ccol[0:E, 1:2])
                else:
                    p.dma("qsp", cnt_out[0:E, :], ccol[0:E, 0:1])
            else:
                if kind == "l1":
                    p.copy(w8[:, :], gA[:, :])
                else:
                    C = float(MOE_CAP[kind])
                    offs = sbR("offs", [E, 2])
                    cmsk = sbR("cmsk", [8, 1])
                    cnts_s = sbR("cnts_s", [8, 16])
                    p.dma("qsp", cmsk[:], cmaskd)
                    if collective:
                        ccol = sbR("ccol", [16, 2])
                        p.reduce(ccol[0:E, 0:1], gA[:, :], ALU.add)
                        p.dma("qsp", cc_in[0:E, :], ccol[0:E, 0:1])
                        if kind == "l3":
                            p.reduce(ccol[0:E, 1:2], gB[:, :], ALU.add)
                            p.dma("qsp", cc_in[E:2 * E, :], ccol[0:E, 1:2])
                        p.collective_allgather(cc_out, cc_in,
                                               replica_groups or [list(range(8))])
                        p.dma("qsp", cnts_s[:, 0:ncnt],
                              cc_out.rearrange("(r n) o -> r (n o)", n=ncnt))
                    else:
                        p.dma("qsp", cnts_s[:, 0:ncnt], cntsd)
                    pso = cm.ps[0]
                    def offcol(src_cols, dst):
                        p.mm(pso[:1, 0:E], cmsk[:, :], cnts_s[:, src_cols],
                             start=True, stop=True)
                        p.copy(col[0:1, 0:E], pso[0:1, 0:E])
                        p.transpose(cm.ps[1][0:E, 0:1], col[0:1, 0:E], id2[0:1, 0:1])
                        p.copy(dst, cm.ps[1][0:E, 0:1])
                    pos = sbR("pos", [E, T])
                    def capply(em, emg, oc_cols, wacc):
                        p.scan(pos[:, :], em[:, :], em[:, :], 0.0, ALU.add, ALU.bypass)
                        p.ts(pos[:, :], pos[:, :], oc_cols, C, ALU.add, ALU.is_le)
                        p.tt(pos[:, :], pos[:, :], em[:, :], ALU.mult)
                        if wacc:
                            p.tt(pos[:, :], pos[:, :], emg[:, :], ALU.mult)
                            p.tt(w8[:, :], w8[:, :], pos[:, :], ALU.add)
                        else:
                            p.tt(w8[:, :], pos[:, :], emg[:, :], ALU.mult)
                    if kind == "l2":
                        offcol(slice(0, 8), offs[:, 0:1])
                        capply(gA, gAg, offs[:, 0:1], False)
                    else:
                        offcol(slice(0, 8), offs[:, 0:1])
                        offcol(slice(8, 16), offs[:, 1:2])
                        capply(gA, gAg, offs[:, 0:1], False)
                        capply(gB, gBg, offs[:, 1:2], True)
                p.dma("qact", w8sp, w8[:, :], sem_hint="w8o")
            p.barrier()

        if mode == "cnt":
            return

        NHH = 32 if kind != "l1" else 16
        NHC = 32 if kind != "l1" else 16
        with ExitStack() as phE:
            def sbE(name, shape, dt=F32):
                return p.sbuf(name, shape, dt, stack=phE)
            w1t = [sbE(f"w1t{c}", [128, 4096], BF16) for c in range(NCH)]
            w2t = [sbE(f"w2t{c}", [128, D], BF16) for c in range(NHH)]
            hst = [sbE(f"hst{c}", [128, 512], BF16) for c in range(NHC)]
            xnsub = [sbE(f"xns{c}", [128, 512], BF16) for c in range(NCH)]
            gr = sbE("gr", [1, T])
            mask_sb = sbE("mask_sb", [128, 512])
            yb = sbE("yb", [128, 512])
            tmpg = sbE("tmpg", [128, 512]) if kind == "l3" else None
            b1c = sbE("b1c", [128, 32]) if kind != "l1" else None
            b2c = sbE("b2c", [128, 8]) if kind != "l1" else None
            psh = [cm.ps[4], cm.ps[5]]
            psu = cm.ps[6]
            psY = [cm.ps[i] for i in range(4)]
            psM = cm.ps[6]
            for e in range(E):
                for c in range(NCH):
                    p.dma("qpool", w1t[c][:], w1d[e, c * 128:(c + 1) * 128, :],
                          sem_hint=f"w1{c % 4}")
                for c in range(NHH):
                    p.dma("qpool", w2t[c][:], w2d[e, c * 128:(c + 1) * 128, :],
                          sem_hint=f"w2{c % 4}")
                if kind != "l1":
                    p.dma("qsp", b1c[:], b1d[e], sem_hint="b1")
                    p.dma("qsp", b2c[:], b2d[e], sem_hint="b2")
                p.dma("qsp", gr[:], w8sp[e:e + 1, :], sem_hint="gr")
                for tsub in range(T // 512):
                    s0 = tsub * 512
                    for c in range(NCH):
                        p.dma("qsp", xnsub[c][:], xnsp[c * 128:(c + 1) * 128, s0:s0 + 512],
                              sem_hint=f"xn{c % 4}")
                    p.mm(psM[:, :], cm.ones_row_f[:1, :], gr[:, s0:s0 + 512],
                         start=True, stop=True)
                    p.copy(mask_sb[:, :], psM[:, :])
                    for hc in range(NHC):
                        if kind == "l1":
                            pg, pu = psh[hc % 2], psu
                            for c in range(NCH):
                                p.mm(pg[:, :], w1t[c][:, hc * 128:(hc + 1) * 128],
                                     xnsub[c][:], start=(c == 0), stop=(c == NCH - 1))
                            for c in range(NCH):
                                p.mm(pu[:, :], w1t[c][:, 2048 + hc * 128:2048 + (hc + 1) * 128],
                                     xnsub[c][:], start=(c == 0), stop=(c == NCH - 1))
                            p.act(yb[:, :], pg[:, :], AF.Silu)
                            p.tt(hst[hc][:, :], yb[:, :], pu[:, :], ALU.mult)
                        else:
                            ph_ = psh[hc % 2]
                            for c in range(NCH):
                                p.mm(ph_[:, :], w1t[c][:, hc * 128:(hc + 1) * 128],
                                     xnsub[c][:], start=(c == 0), stop=(c == NCH - 1))
                            if kind == "l2":
                                p.ts(hst[hc][:, :], ph_[:, :], b1c[:, hc:hc + 1], 0.0,
                                     ALU.add, ALU.max)
                            else:
                                p.ts(tmpg[:, :], ph_[:, :], b1c[:, hc:hc + 1], None,
                                     ALU.add)
                                p.act(hst[hc][:, :], tmpg[:, :], AF.Gelu)
                    for oh in range(2):
                        for oc in range(4):
                            o = oh * 4 + oc
                            for hc in range(NHH):
                                p.mm(psY[oc][:, :], w2t[hc][:, o * 128:(o + 1) * 128],
                                     hst[hc][:, :], start=(hc == 0), stop=(hc == NHH - 1))
                        for oc in range(4):
                            o = oh * 4 + oc
                            if kind != "l1":
                                p.ts(yb[:, :], psY[oc][:, :], b2c[:, o:o + 1], None,
                                     ALU.add)
                                p.tt(yb[:, :], yb[:, :], mask_sb[:, :], ALU.mult)
                            else:
                                p.tt(yb[:, :], psY[oc][:, :], mask_sb[:, :], ALU.mult)
                            p.dma("qpool", accT[o * 128:(o + 1) * 128, s0:s0 + 512],
                                  yb[:, :],
                                  accum_op=(ALU.add if e > 0 else ALU.bypass))
            p.barrier()

        xe = p.sbuf("xef", [128, 512], F32)
        ae = p.sbuf("aef", [128, 512], F32)
        oe = p.sbuf("oef", [128, 512], F32)
        for c in range(NCH):
            for tsub in range(T // 512):
                s0 = tsub * 512
                p.dma("qsp", xe[:, :], xT[c * 128:(c + 1) * 128, s0:s0 + 512],
                      sem_hint="xf")
                p.dma("qact", ae[:, :], accT[c * 128:(c + 1) * 128, s0:s0 + 512],
                      sem_hint="af")
                p.stt(oe[:, :], ae[:, :], 0.5, xe[:, :], ALU.mult, ALU.add)
                p.dma("qsp", yT[c * 128:(c + 1) * 128, s0:s0 + 512], oe[:, :],
                      sem_hint="yf")
        p.barrier()



def wkv_input_spec(T):
    d = {nm: ([D, D], F32) for nm in ("WrT", "WkT", "WvT")}
    d["WoH"] = ([H, DH, D], F32)
    for nm in ("maaK", "maaV", "maaR", "brT", "bkT", "bvT", "boT"):
        d[nm] = ([128, NCH], F32)
    d["tdec"] = ([1, H], F32)
    d["tfaa"] = ([1, H], F32)
    d["DIF"] = ([128, 512], F32)
    d["TRI01"] = ([128, 512], F32)
    d["DG"] = ([128, 512], F32)
    d["IOTC"] = ([128, 2], F32)
    d["JROW"] = ([1, 256], F32)
    return d


def build_l2attn(T=4096, TT=512):
    nc = bass.Bass("TRN2", target_bir_lowering=False, debug=False)
    ins = {k: nc.dram_tensor(k, s, dt, kind="ExternalInput").ap()
           for k, (s, dt) in wkv_input_spec(T).items()}
    xT = nc.dram_tensor("xT", [D, T], F32, kind="ExternalInput").ap()
    yT = nc.dram_tensor("yT", [D, T], F32, kind="ExternalOutput").ap()
    with ExitStack() as stack:
        p = Prog(nc, stack)
        cm = Common(p, T, TT)
        emit_l2attn(p, cm, ins, xT, yT, T, TT)
        p.flush()
    return nc


def emit_l2attn(p, cm, ins, xT, yT, T, TT):
    nc = p.nc
    QC = 256
    nch_t = T // QC
    rsp = p.dram_i("rsp", [H, DH, T], BF16)
    ksp = p.dram_i("ksp", [H, DH, T], BF16)
    vsp = p.dram_i("vsp", [H, DH, T], BF16)
    nt = T // TT
    if True:
        ones_col_bf = p.sbuf("ones_col_bf", [128, 1], BF16)
        onesc_d = p.inline(np.ones((128, 1), np.float32).astype(ml_dtypes.bfloat16), "onesc_d")
        p.dma("qsp", ones_col_bf[:], onesc_d.ap())
        cols = {}
        for nm in ("maaK", "maaV", "maaR", "brT", "bkT", "bvT", "boT"):
            cols[nm] = p.sbuf(f"c_{nm}", [128, NCH], F32)
            p.dma("qsp", cols[nm][:], ins[nm], sem_hint="cols")
        m1c = {}
        for nm in ("maaK", "maaV", "maaR"):
            m1c[nm] = p.sbuf(f"m1_{nm}", [128, NCH], F32)
            p.ts(m1c[nm][:], cols[nm][:], -1.0, 1.0, ALU.mult, ALU.add)

        xnr = [p.sbuf(f"xnr{c}", [128, T], BF16) for c in range(NCH)]

        with ExitStack() as phA:
            def sbA(name, shape, dt=F32):
                return p.sbuf(name, shape, dt, stack=phA)
            Wt = {}
            for nm in ("WrT", "WkT", "WvT"):
                tiles = [sbA(f"{nm}{c}", [128, D], BF16) for c in range(NCH)]
                for c in range(NCH):
                    p.dma("qpool", tiles[c][:], ins[nm][c * 128:(c + 1) * 128, :],
                          sem_hint=nm)
                Wt[nm] = tiles
            xstage = [sbA(f"xa{c}", [128, TT]) for c in range(NCH)]
            xbA = [sbA(f"xab{c}", [128, TT], BF16) for c in range(NCH)]
            sqwA = [sbA(f"sqa{c}", [128, TT], BF16) for c in range(NCH)]
            xnt = [sbA(f"xnt{c}", [128, TT], BF16) for c in range(NCH)]
            rows_ws = sbA("rows_a", [1, 6 * TT])
            for t in range(nt):
                t0 = t * TT
                for c in range(NCH):
                    p.dma("qsp", xstage[c][:], xT[c * 128:(c + 1) * 128, t0:t0 + TT],
                          sem_hint=f"xa{c % 4}")
                _ln_tile_v2(p, cm, xstage, xbA, sqwA, TT, xnt, ones_col_bf, rows_ws)
                for c in range(NCH):
                    p.copy(xnr[c][:, t0:t0 + TT], xnt[c][:, :TT])
            tmpf = sbA("tmpf", [128, TT])
            mix = {nm: [sbA(f"mx{nm}{c}", [128, TT], BF16) for c in range(NCH)]
                   for nm in ("K", "V", "R")}
            prj = [sbA(f"prj{c}", [128, TT], BF16) for c in range(NCH)]
            xsh0 = sbA("xsh0", [128, TT], BF16)
            for t in range(nt):
                t0 = t * TT
                for nm in ("K", "V", "R"):
                    maa = cols["maa" + nm]
                    maa1 = m1c["maa" + nm]
                    for c in range(NCH):
                        if t == 0:
                            p.copy(xsh0[:, 1:TT], xnr[c][:, 0:TT - 1])
                            p.memset(xsh0[:, 0:1], 0.0)
                            sh = xsh0[:, :TT]
                        else:
                            sh = xnr[c][:, t0 - 1:t0 + TT - 1]
                        p.ts(tmpf[:, :TT], sh, maa[:, c:c + 1], None, ALU.mult)
                        p.stt(mix[nm][c][:, :TT], xnr[c][:, t0:t0 + TT],
                              maa1[:, c:c + 1], tmpf[:, :TT], ALU.mult, ALU.add)
                for (nm, wnm, bnm, sp) in (("R", "WrT", "brT", rsp),
                                           ("K", "WkT", "bkT", ksp),
                                           ("V", "WvT", "bvT", vsp)):
                    for m in range(NCH):
                        ps = cm.ps[6]
                        for c in range(NCH):
                            p.mm(ps[:, :TT], Wt[wnm][c][:, m * 128:(m + 1) * 128],
                                 mix[nm][c][:, :TT], start=(c == 0), stop=(c == NCH - 1))
                        p.ts(prj[m][:, :TT], ps[:, :TT], 1.0, cols[bnm][:, m:m + 1],
                             ALU.mult, ALU.add)
                    for h in range(H):
                        ch, b = divmod(h, 2)
                        p.dma("qact", sp[h, :, t0:t0 + TT],
                              prj[ch][b * 64:(b + 1) * 64, :TT], sem_hint=f"sp{h % 4}")
            p.barrier()

        id2_d = p.inline(np.concatenate([np.eye(64, dtype=np.float32)] * 2, 0).astype(ml_dtypes.bfloat16), "id2b_d")
        id2b = p.sbuf("id2b", [128, 64], BF16)
        p.dma("qsp", id2b[:], id2_d.ap())
        WoH_t = [p.sbuf(f"WoH{h}", [64, D], BF16) for h in range(H)]
        for h in range(H):
            p.dma("qpool", WoH_t[h][:], ins["WoH"][h], sem_hint="woh")
        dif = p.sbuf("dif", [128, 512], F32)
        tri = p.sbuf("tri", [128, 512], F32)
        dg = p.sbuf("dg", [128, 512], F32)
        iotc = p.sbuf("iotc", [128, 2], F32)
        p.dma("qact", dif[:], ins["DIF"], sem_hint="difc")
        p.dma("qact", tri[:], ins["TRI01"], sem_hint="difc")
        p.dma("qact", dg[:], ins["DG"], sem_hint="difc")
        p.dma("qact", iotc[:], ins["IOTC"], sem_hint="difc")
        jrow2 = p.sbuf("jrow2", [1, 256], F32)
        p.dma("qact", jrow2[:], ins["JROW"], sem_hint="difc")
        rowH = p.sbuf("rowH", [1, 8 * H], F32)
        p.dma("qsp", rowH[0:1, 0:H], ins["tdec"])
        p.dma("qsp", rowH[0:1, H:2 * H], ins["tfaa"])
        lnw = rowH[0:1, 2 * H:3 * H]
        wsr = rowH[0:1, 3 * H:4 * H]
        p.act(lnw, rowH[0:1, 0:H], AF.Exp)
        p.ts(lnw, lnw, -1.0, None, ALU.mult)
        p.ts(wsr, lnw, float(QC), None, ALU.mult)
        p.act(wsr, wsr, AF.Exp)
        hcols = p.sbuf("hcols", [128, 3 * H], F32)
        psb_ = cm.ps[5]
        for h in range(H):
            p.mm(psb_[:, 0:1], cm.ones_row_f[:1, :], lnw[0:1, h:h + 1])
            p.copy(hcols[:, h:h + 1], psb_[:, 0:1])
            p.mm(psb_[:, 0:1], cm.ones_row_f[:1, :], rowH[0:1, H + h:H + h + 1])
            p.copy(hcols[:, H + h:H + h + 1], psb_[:, 0:1])
            p.mm(psb_[:, 0:1], cm.ones_row_f[:1, :], wsr[0:1, h:h + 1])
            p.copy(hcols[:, 2 * H + h:2 * H + h + 1], psb_[:, 0:1])
        wmT = [p.sbuf(f"wmT{h}", [128, 512], BF16) for h in range(H)]
        wkcols = p.sbuf("wkcols", [128, 2 * H], F32)
        wbb = p.sbuf("wbb", [128, QC * H], F32)
        wf = p.sbuf("wf", [128, 512], F32)
        rQ = p.sbuf("rQ", [1, QC], F32)
        for h in range(H):
            p.ts(wf[:, :], dif[:, :], hcols[:, h:h + 1], None, ALU.mult)
            p.act(wf[:, :], wf[:, :], AF.Exp)
            p.tt(wf[:, :], wf[:, :], tri[:, :], ALU.mult)
            p.stt(wmT[h][:, :], dg[:, :], hcols[:, H + h:H + h + 1], wf[:, :],
                  ALU.mult, ALU.add)
            p.ts(wkcols[:, 2 * h:2 * h + 2], iotc[:, :], hcols[:, h:h + 1], None,
                 ALU.mult)
            p.act(wkcols[:, 2 * h:2 * h + 2], wkcols[:, 2 * h:2 * h + 2], AF.Exp)
            p.ts(rQ[0:1, :], jrow2[0:1, :], lnw[0:1, h:h + 1], None, ALU.mult)
            p.act(rQ[0:1, :], rQ[0:1, :], AF.Exp)
            p.mm(psb_[:, 0:QC], cm.ones_row_f[:1, :], rQ[0:1, :])
            p.copy(wbb[:, h * QC:(h + 1) * QC], psb_[:, 0:QC])
        stateF = p.sbuf("stateF", [64, H * 64], F32)
        stateB = p.sbuf("stateB", [64, H * 64], BF16)
        p.memset(stateF[:, :], 0.0)
        p.memset(stateB[:, :], 0.0)
        rbuf = [p.sbuf(f"rb{i}", [64, QC], BF16) for i in range(2)]
        kbuf2 = [p.sbuf(f"kb2{i}", [64, QC], BF16) for i in range(2)]
        vbuf = [p.sbuf(f"vb2{i}", [64, QC], BF16) for i in range(2)]
        ktm = [p.sbuf(f"ktm{i}", [128, 64], BF16) for i in range(2)]
        vtm = [p.sbuf(f"vtm{i}", [128, 64], BF16) for i in range(2)]
        As = [p.sbuf(f"As{i}", [128, QC], BF16) for i in range(2)]
        zt = p.sbuf("zt", [64, QC], F32)
        yh = [p.sbuf(f"yh{h}", [64, QC], BF16) for h in range(H)]
        xe2 = p.sbuf("xe2", [128, QC], F32)
        yw3 = p.sbuf("yw3", [128, QC], F32)
        ps_tr2 = p.psum("ps_tr2", [128, 512], BF16)
        psA_, psY, psZ, psS2, psO2 = cm.ps[0], cm.ps[1], cm.ps[2], cm.ps[3], cm.ps[4]

        for ci in range(nch_t):
            c0 = ci * QC
            for h in range(H):
                hp = h % 2
                p.dma("qsp", rbuf[hp][:, :], rsp[h, :, c0:c0 + QC], sem_hint=f"rr{hp}")
                p.dma("qact", kbuf2[hp][:, :], ksp[h, :, c0:c0 + QC], sem_hint=f"rk{hp}")
                p.dma("qpool", vbuf[hp][:, :], vsp[h, :, c0:c0 + QC], sem_hint=f"rv{hp}")
                for half in range(2):
                    p.transpose(ps_tr2[:, 0:64], kbuf2[hp][:, half * 128:(half + 1) * 128],
                                id2b[0:64, :])
                    p.ts(ktm[half][:, :], ps_tr2[:, 0:64],
                         wkcols[:, 2 * h + half:2 * h + half + 1], None, ALU.mult)
                    p.transpose(ps_tr2[:, 64:128], vbuf[hp][:, half * 128:(half + 1) * 128],
                                id2b[0:64, :])
                    p.copy(vtm[half][:, :], ps_tr2[:, 64:128])
                for half in range(2):
                    p.mm(psA_[:, 0:QC], kbuf2[hp][:, half * 128:(half + 1) * 128],
                         rbuf[hp][:, :], start=True, stop=True)
                    p.tt(As[half][:, :], psA_[:, 0:QC],
                         wmT[h][:, half * QC:(half + 1) * QC], ALU.mult)
                p.mm(psY[0:64, 0:QC], vtm[0][:, :], As[0][:, :], start=True, stop=False)
                p.mm(psY[0:64, 0:QC], vtm[1][:, :], As[1][:, :], start=False, stop=True)
                p.mm(psZ[0:64, 0:QC], stateB[:, h * 64:(h + 1) * 64], rbuf[hp][:, :],
                     start=True, stop=True)
                p.tt(zt[:, :], psZ[0:64, 0:QC], wbb[0:64, h * QC:(h + 1) * QC],
                     ALU.mult)
                p.tt(yh[h][:, :], zt[:, :], psY[0:64, 0:QC], ALU.add)
                p.mm(psS2[0:64, 0:64], ktm[0][:, :], vtm[0][:, :], start=True, stop=False)
                p.mm(psS2[0:64, 0:64], ktm[1][:, :], vtm[1][:, :], start=False, stop=True)
                p.ts(stateF[:, h * 64:(h + 1) * 64], stateF[:, h * 64:(h + 1) * 64],
                     hcols[0:64, 2 * H + h:2 * H + h + 1], None, ALU.mult, q="pool")
                p.tt(stateF[:, h * 64:(h + 1) * 64], stateF[:, h * 64:(h + 1) * 64],
                     psS2[0:64, 0:64], ALU.add)
                p.copy(stateB[:, h * 64:(h + 1) * 64], stateF[:, h * 64:(h + 1) * 64])
            for m in range(NCH):
                for h in range(H):
                    p.mm(psO2[:, 0:QC], WoH_t[h][:, m * 128:(m + 1) * 128], yh[h][:, :],
                         start=(h == 0), stop=(h == H - 1))
                p.dma("qsp", xe2[:, :], xT[m * 128:(m + 1) * 128, c0:c0 + QC],
                      sem_hint="xe2")
                p.ts(yw3[:, :], psO2[:, 0:QC], 1.0, cols["boT"][:, m:m + 1],
                     ALU.mult, ALU.add)
                p.tt(yw3[:, :], yw3[:, :], xe2[:, :], ALU.add)
                p.dma("qact", yT[m * 128:(m + 1) * 128, c0:c0 + QC], yw3[:, :],
                      sem_hint="yo2")
        p.barrier()



def fused_input_specs(T):
    specs = {}
    specs["l0f"] = {k: (s, F32) for k, s in L0FFN_INS.items()}
    specs["l0a"] = attn_input_spec("l0", T)
    specs["l1f"] = moe_input_spec("l1", "main", T)
    specs["l1a"] = attn_input_spec("l1", T)
    specs["l2f"] = moe_input_spec("l2", "main", T)
    specs["l2a"] = wkv_input_spec(T)
    specs["l3f"] = moe_input_spec("l3", "main", T)
    specs["l3a"] = attn_input_spec("l3", T)
    return specs


def build_fused(T=4096, collective=True, replica_groups=None):
    nc = bass.Bass("TRN2", target_bir_lowering=False, debug=False)
    specs = fused_input_specs(T)
    ins = {}
    for stg, spec in specs.items():
        if collective and stg in ("l2f", "l3f"):
            spec = {k: v for k, v in spec.items() if k != "cnts"}
        ins[stg] = {k: nc.dram_tensor(f"{stg}_{k}", s, dt, kind="ExternalInput").ap()
                    for k, (s, dt) in spec.items()}
    xT = nc.dram_tensor("xT", [D, T], F32, kind="ExternalInput").ap()
    yT = nc.dram_tensor("yT", [D, T], F32, kind="ExternalOutput").ap()
    xA = nc.dram_tensor("xbufA", [D, T], F32, kind="Internal").ap()
    xB = nc.dram_tensor("xbufB", [D, T], F32, kind="Internal").ap()

    with ExitStack() as stack:
        p = Prog(nc, stack)
        cm = Common(p, T, 512)

        def e_l0f(i, o):
            emit_l0ffn(p, cm, ins["l0f"], i, o, T, 512)

        def e_l0a(i, o):
            emit_attn(p, cm, "l0", ins["l0a"], i, o, T, 256)

        def e_l1f(i, o):
            emit_moe(p, cm, "l1", "main", ins["l1f"], i, o, None, T, 512,
                     collective=collective, replica_groups=replica_groups)

        def e_l1a(i, o):
            emit_attn(p, cm, "l1", ins["l1a"], i, o, T, 256)

        def e_l2f(i, o):
            emit_moe(p, cm, "l2", "main", ins["l2f"], i, o, None, T, 512,
                     collective=collective, replica_groups=replica_groups)

        def e_l2a(i, o):
            emit_l2attn(p, cm, ins["l2a"], i, o, T, 512)

        def e_l3f(i, o):
            emit_moe(p, cm, "l3", "main", ins["l3f"], i, o, None, T, 512,
                     collective=collective, replica_groups=replica_groups)

        def e_l3a(i, o):
            emit_attn(p, cm, "l3", ins["l3a"], i, o, T, 256)

        seq = [e_l0f, e_l0a, e_l0f, e_l1f, e_l1a, e_l1f,
               e_l2f, e_l2a, e_l2f, e_l3f, e_l3a, e_l3f]
        cur = xT
        for i, em in enumerate(seq):
            out = yT if i == len(seq) - 1 else (xA if i % 2 == 0 else xB)
            with p.stage():
                em(cur, out)
            cur = out
        p.flush()
    return nc



def build_group(stages, T=4096):
    nc = bass.Bass("TRN2", target_bir_lowering=False, debug=False)
    ins = {}

    def get_ins(tag, spec):
        if tag not in ins:
            ins[tag] = {k: nc.dram_tensor(f"{tag}_{k}", s, dt, kind="ExternalInput").ap()
                        for k, (s, dt) in spec.items()}
        return ins[tag]

    xT = nc.dram_tensor("xT", [D, T], F32, kind="ExternalInput").ap()
    n_main = sum(1 for _, ks in stages if not ks.endswith(".cnt"))
    yT = nc.dram_tensor("yT", [D, T], F32, kind="ExternalOutput").ap() if n_main else None
    cnt_outs = {}
    for tag, ks in stages:
        if ks.endswith(".cnt"):
            kind = ks[:2]
            ncnt = 8 if kind == "l2" else 16
            cnt_outs[tag] = nc.dram_tensor(f"{tag}_cnt", [ncnt, 1], F32,
                                           kind="ExternalOutput").ap()
    xA = nc.dram_tensor("xbufA", [D, T], F32, kind="Internal").ap()
    xB = nc.dram_tensor("xbufB", [D, T], F32, kind="Internal").ap()

    with ExitStack() as stack:
        p = Prog(nc, stack)
        cm = Common(p, T, 512)
        cur = xT
        mains_done = 0
        for si, (tag, ks) in enumerate(stages):
            is_cnt = ks.endswith(".cnt")
            kind = ks[:2]
            if not is_cnt:
                mains_done += 1
                out = yT if mains_done == n_main else (xA if mains_done % 2 else xB)
            with p.stage():
                if ks == "l0f":
                    emit_l0ffn(p, cm, get_ins(tag, {k: (s, F32) for k, s in L0FFN_INS.items()}),
                               cur, out, T, 512)
                elif ks in ("l0a", "l1a", "l3a"):
                    emit_attn(p, cm, ks[:2], get_ins(tag, attn_input_spec(ks[:2], T)),
                              cur, out, T, 256)
                elif ks == "l2a":
                    emit_l2attn(p, cm, get_ins(tag, wkv_input_spec(T)), cur, out, T, 512)
                elif ks == "l1f":
                    emit_moe(p, cm, "l1", "main", get_ins(tag, moe_input_spec("l1", "main", T)),
                             cur, out, None, T, 512)
                elif ks.endswith(".main"):
                    emit_moe(p, cm, kind, "main", get_ins(tag, moe_input_spec(kind, "main", T)),
                             cur, out, None, T, 512)
                else:
                    emit_moe(p, cm, kind, "cnt",
                             get_ins(tag, {"router": ([D, 8], F32)}),
                             cur, None, cnt_outs[tag], T, 512)
            if not is_cnt:
                cur = out
        p.flush()
    return nc


import os
from concourse.bass_utils import run_bass_kernel_spmd

NCORES = 8
TFULL = 4096


def _colsf(v):
    return np.ascontiguousarray(np.asarray(v, np.float32).reshape(8, 128).T)


def _host_consts():
    c = {}
    f = np.arange(1024)
    ang = 2 * np.pi * np.outer(f, f) / 1024.0
    c["dftC"] = np.cos(ang).astype(np.float32)
    c["dftS"] = np.sin(ang).astype(np.float32)
    half = DH // 2
    freqs = 1.0 / 10000.0 ** (np.arange(half) / half)
    a2 = np.arange(TFULL)[:, None] * freqs[None, :]
    cos, sin = np.cos(a2).astype(np.float32), np.sin(a2).astype(np.float32)
    CC = np.zeros((128, TFULL), np.float32)
    SS = np.zeros((128, TFULL), np.float32)
    for d_ in range(128):
        i = (d_ % 64) // 2
        CC[d_] = cos[:, i]
        SS[d_] = (-sin[:, i]) if d_ % 2 == 0 else sin[:, i]
    c["ropeC"] = CC.astype(ml_dtypes.bfloat16)
    c["ropeS"] = SS.astype(ml_dtypes.bfloat16)
    P = np.zeros((128, 128), np.float32)
    for i in range(64):
        P[2 * i, 2 * i + 1] = 1.0
        P[2 * i + 1, 2 * i] = 1.0
    c["rperm"] = P
    cmsk = np.zeros((128, 4, 512), np.float32)
    kr = np.arange(128)[:, None]
    qc = np.arange(512)[None, :]
    for dj in range(4):
        cmsk[:, dj, :] = np.where(dj * 128 + kr > qc, -1e30, 0.0)
    c["cmask"] = np.ascontiguousarray(cmsk.reshape(128, 4 * 512))
    DIF = np.zeros((128, 512), np.float32)
    TRI = np.zeros_like(DIF)
    DG = np.zeros_like(DIF)
    IOTC = np.zeros((128, 2), np.float32)
    for r_ in range(128):
        for hlf in range(2):
            p_ = r_ + 128 * hlf
            q_ = np.arange(256)
            DIF[r_, hlf * 256:(hlf + 1) * 256] = np.maximum(q_ - p_ - 1, 0)
            TRI[r_, hlf * 256:(hlf + 1) * 256] = (q_ > p_).astype(np.float32)
            DG[r_, hlf * 256:(hlf + 1) * 256] = (q_ == p_).astype(np.float32)
            IOTC[r_, hlf] = 255 - p_
    c["DIF"], c["TRI01"], c["DG"], c["IOTC"] = DIF, TRI, DG, IOTC
    c["JROW"] = np.ascontiguousarray(np.arange(256, dtype=np.float32).reshape(1, 256))
    return c


def _bE(b):
    b = np.asarray(b, np.float32)
    r = np.zeros((128, 8), np.float32)
    for w in range(4):
        for hf in range(2):
            r[:, w * 2 + hf] = b[w, hf * 128:(hf + 1) * 128]
    return np.ascontiguousarray(r)


_NC_CACHE = {}


def _get_nc(key, builder):
    if key not in _NC_CACHE:
        _NC_CACHE[key] = builder()
    return _NC_CACHE[key]


def _launch(nc, per_core_ins):
    res = run_bass_kernel_spmd(nc, per_core_ins, core_ids=list(range(NCORES)))
    return res.results


def kernel(**inputs):
    I = {k: np.asarray(v) for k, v in inputs.items()}
    x = I["L0_attn_x"]
    xTs = [np.ascontiguousarray(x[c].T) for c in range(NCORES)]
    C = _host_consts()
    H_, DH_ = H, DH

    l0f_w = {
        "wr": I["L0_ffn_w_freq_real"], "wi": I["L0_ffn_w_freq_imag"],
        "biasT": _colsf(I["L0_ffn_bias"]), "dftC": C["dftC"], "dftS": C["dftS"],
    }
    l0a_w = {
        "qW": I["L0_attn_qW"], "kW": I["L0_attn_kW"], "vW": I["L0_attn_vW"],
        "oW": I["L0_attn_oW"], "qbE": _bE(I["L0_attn_qb"]), "kbE": _bE(I["L0_attn_kb"]),
        "vbE": _bE(I["L0_attn_vb"]), "obE": _bE(I["L0_attn_ob"]),
        "J": np.ascontiguousarray(I["L0_attn_J"].reshape(1, 1024)),
    }
    l1f_w = {
        "router": I["L1_ffn_moe_router"], "w1e": I["L1_ffn_moe_w13"],
        "w2e": I["L1_ffn_moe_w2"],
    }
    l1a_w = {
        "WqT": np.ascontiguousarray(I["L1_attn_Wq"].T),
        "WkT": np.ascontiguousarray(I["L1_attn_Wk"].T),
        "WvT": np.ascontiguousarray(I["L1_attn_Wv"].T),
        "WpH": np.ascontiguousarray(I["L1_attn_Wp"].T.reshape(H_, DH_, 1024)),
        "bqT": _colsf(I["L1_attn_bq"]), "bkT": _colsf(I["L1_attn_bk"]),
        "bvT": _colsf(I["L1_attn_bv"]), "bpT": _colsf(I["L1_attn_bp"]),
    }
    l2f_w = {
        "router": I["L2_ffn_router_w"], "w1e": I["L2_ffn_w1"], "w2e": I["L2_ffn_w2"],
        "b1T": np.ascontiguousarray(I["L2_ffn_b1"].reshape(8, 32, 128).transpose(0, 2, 1)),
        "b2T": np.ascontiguousarray(I["L2_ffn_b2"].reshape(8, 8, 128).transpose(0, 2, 1)),
    }
    l2f_cnt_w = {"router": I["L2_ffn_router_w"]}
    l2a_w = {
        "WrT": np.ascontiguousarray(I["L2_attn_Wr"].T),
        "WkT": np.ascontiguousarray(I["L2_attn_Wk"].T),
        "WvT": np.ascontiguousarray(I["L2_attn_Wv"].T),
        "WoH": np.ascontiguousarray(I["L2_attn_Wo"].T.reshape(H_, DH_, 1024)),
        "maaK": _colsf(I["L2_attn_time_maa_k"]), "maaV": _colsf(I["L2_attn_time_maa_v"]),
        "maaR": _colsf(I["L2_attn_time_maa_r"]),
        "brT": _colsf(I["L2_attn_br"]), "bkT": _colsf(I["L2_attn_bk"]),
        "bvT": _colsf(I["L2_attn_bv"]), "boT": _colsf(I["L2_attn_bo"]),
        "tdec": np.ascontiguousarray(I["L2_attn_time_decay"].reshape(1, H_)),
        "tfaa": np.ascontiguousarray(I["L2_attn_time_faaaa"].reshape(1, H_)),
        "DIF": C["DIF"], "TRI01": C["TRI01"], "DG": C["DG"], "IOTC": C["IOTC"], "JROW": C["JROW"],
    }
    l3a_w = {
        "Wq": I["L3_attn_Wq"], "Wk": I["L3_attn_Wk"], "Wv": I["L3_attn_Wv"],
        "WoH": np.ascontiguousarray(I["L3_attn_Wo"].reshape(H_, DH_, 1024)),
        "boT": _colsf(I["L3_attn_bo"]),
        "ropeC": C["ropeC"], "ropeS": C["ropeS"], "rperm": C["rperm"],
        "cmask": C["cmask"],
    }
    l3f_w = {
        "router": I["L3_ffn_Wr"], "w1e": I["L3_ffn_W1"], "w2e": I["L3_ffn_W2"],
        "b1T": np.ascontiguousarray(I["L3_ffn_b1"].reshape(8, 32, 128).transpose(0, 2, 1)),
        "b2T": np.ascontiguousarray(I["L3_ffn_b2"].reshape(8, 8, 128).transpose(0, 2, 1)),
    }
    l3f_cnt_w = {"router": I["L3_ffn_Wr"]}

    coremasks = [np.ascontiguousarray(
        (np.arange(8) < c_).astype(np.float32).reshape(8, 1)) for c_ in range(NCORES)]

    def step(nc_key, builder, wmap, extra_per_core=None, out_key="yT"):
        nonlocal xTs
        nc = _get_nc(nc_key, builder)
        per = []
        for c_ in range(NCORES):
            m = dict(wmap)
            m["xT"] = xTs[c_]
            if extra_per_core is not None:
                m.update(extra_per_core[c_])
            per.append(m)
        res = _launch(nc, per)
        xTs = [np.ascontiguousarray(res[c_][out_key]) for c_ in range(NCORES)]
        if os.environ.get("MKDEBUG"):
            import sys as _sys
            fin = all(np.isfinite(t).all() for t in xTs)
            print(f"[mk] {nc_key}: finite={fin} rms={float(np.sqrt(np.mean(xTs[0]**2))):.4f}",
                  file=_sys.stderr)

    def cnt_step(nc_key, builder, wmap):
        nc = _get_nc(nc_key, builder)
        per = []
        for c_ in range(NCORES):
            m = dict(wmap)
            m["xT"] = xTs[c_]
            per.append(m)
        res = _launch(nc, per)
        cnts = np.ascontiguousarray(
            np.stack([res[c_]["cnt"][:, 0] for c_ in range(NCORES)]).astype(np.float32))
        return cnts

    def cap_ffn(kind, wmap, cnt_wmap):
        cnts = cnt_step(f"{kind}f_cnt", lambda: build_moe(kind, "cnt", T=TFULL), cnt_wmap)
        extra = [{"cnts": cnts, "coremask": coremasks[c_]} for c_ in range(NCORES)]
        step(f"{kind}f", lambda: build_moe(kind, "main", T=TFULL), wmap, extra)

    if os.environ.get("MK_UNFUSED"):
        step("l0f", lambda: build_l0ffn(T=TFULL, TT=512), l0f_w)
        step("l0a", lambda: build_attn("l0", T=TFULL), l0a_w)
        step("l0f", lambda: build_l0ffn(T=TFULL, TT=512), l0f_w)
        step("l1f", lambda: build_moe("l1", "main", T=TFULL), l1f_w)
        step("l1a", lambda: build_attn("l1", T=TFULL), l1a_w)
        step("l1f", lambda: build_moe("l1", "main", T=TFULL), l1f_w)
        cap_ffn("l2", l2f_w, l2f_cnt_w)
        step("l2a", lambda: build_l2attn(T=TFULL), l2a_w)
        cap_ffn("l2", l2f_w, l2f_cnt_w)
        cap_ffn("l3", l3f_w, l3f_cnt_w)
        step("l3a", lambda: build_attn("l3", T=TFULL), l3a_w)
        cap_ffn("l3", l3f_w, l3f_cnt_w)
        out = np.stack([xTs[c_].T for c_ in range(NCORES)]).astype(np.float32)
        return out

    stage_w = {"l0f": l0f_w, "l0a": l0a_w, "l1f": l1f_w, "l1a": l1a_w,
               "l2f": l2f_w, "l2a": l2a_w, "l3f": l3f_w, "l3a": l3a_w,
               "l2c": l2f_cnt_w, "l3c": l3f_cnt_w}
    GROUPS = [
        ("gA", [("l0f", "l0f"), ("l0a", "l0a"), ("l0f", "l0f"),
                 ("l1f", "l1f"), ("l1a", "l1a"), ("l1f", "l1f"),
                 ("l2c", "l2f.cnt")]),
        ("gB", [("l2f", "l2f.main"), ("l2a", "l2a"), ("l2c", "l2f.cnt")]),
        ("gC", [("l2f", "l2f.main"), ("l3c", "l3f.cnt")]),
        ("gD", [("l3f", "l3f.main"), ("l3a", "l3a"), ("l3c", "l3f.cnt")]),
        ("gE", [("l3f", "l3f.main")]),
    ]
    pend_cnts = None
    for gname, stages in GROUPS:
        nc = _get_nc(gname, lambda stages=stages: build_group(stages, T=TFULL))
        cnt_tags = [tag for tag, ks in stages if ks.endswith(".cnt")]
        per = []
        for c_ in range(NCORES):
            m = {"xT": xTs[c_]}
            for tag, ks in stages:
                for k, v in stage_w[tag].items():
                    m[f"{tag}_{k}"] = v
                if ks.endswith(".main"):
                    m[f"{tag}_cnts"] = pend_cnts
                    m[f"{tag}_coremask"] = coremasks[c_]
            per.append(m)
        res = _launch(nc, per)
        if any(not ks.endswith(".cnt") for _, ks in stages):
            xTs = [np.ascontiguousarray(res[c_]["yT"]) for c_ in range(NCORES)]
        if cnt_tags:
            tag = cnt_tags[-1]
            pend_cnts = np.ascontiguousarray(
                np.stack([res[c_][f"{tag}_cnt"][:, 0] for c_ in range(NCORES)]
                         ).astype(np.float32))
    out = np.stack([xTs[c_].T for c_ in range(NCORES)]).astype(np.float32)
    return out
